# version 39
# baseline (speedup 1.0000x reference)
import numpy as np
import ml_dtypes
import concourse.bass as bass
import concourse.mybir as mybir
from concourse.bass_utils import run_bass_kernel_spmd

F32 = mybir.dt.float32
BF16 = mybir.dt.bfloat16
AF = mybir.ActivationFunctionType
ALU = mybir.AluOpType
AX = mybir.AxisListType

P = 128
NCH = 8
NI = 24
TOK = 2048
NQ = 4
QT = 512
NT128 = 16
C = 1024
DI = 3072
NCORES = 8
XCOL0 = 32
XST = XCOL0 + TOK
MAGIC = 1.5 * 2.0 ** 23
QMAX = 32767.0
U16S = 65536.0
EPS_LN = 1e-5
EPS_Q = 1e-12


def build_nc():
    nc = bass.Bass("TRN2", target_bir_lowering=False)
    dt = nc.dram_tensor

    x_d = dt("x", [TOK, C], F32, kind="ExternalInput").ap()
    xh_d = dt("xhalo", [1, C], F32, kind="ExternalInput").ap()
    sel_d = dt("sel", [1, NCORES], F32, kind="ExternalInput").ap()
    td_d = dt("td", [P, NCH], F32, kind="ExternalInput").ap()
    tf_d = dt("tf", [P, NCH], F32, kind="ExternalInput").ap()
    tmk_d = dt("tmk", [P, NCH], F32, kind="ExternalInput").ap()
    tmv_d = dt("tmv", [P, NCH], F32, kind="ExternalInput").ap()
    tmr_d = dt("tmr", [P, NCH], F32, kind="ExternalInput").ap()
    wkT_d = dt("wkT", [C, C], F32, kind="ExternalInput").ap()
    wvT_d = dt("wvT", [C, C], F32, kind="ExternalInput").ap()
    wrT_d = dt("wrT", [C, C], F32, kind="ExternalInput").ap()
    woT_d = dt("woT", [C, C], F32, kind="ExternalInput").ap()
    wup_d = dt("wupp", [3 * P, C], F32, kind="ExternalInput").ap()
    wgate_d = dt("wgatep", [3 * P, C], F32, kind="ExternalInput").ap()
    wdown_d = dt("wdownp", [P, DI], F32, kind="ExternalInput").ap()
    out_d = dt("out", [TOK, C], F32, kind="ExternalOutput").ap()

    wqu_t = dt("wqu", [DI, C], BF16)
    wqg_t = dt("wqg", [DI, C], BF16)
    wqd_t = dt("wqd", [C, DI], BF16)
    wqu_d, wqg_d, wqd_d = wqu_t.ap(), wqg_t.ap(), wqd_t.ap()
    dqu_t = dt("dqu", [3 * P, C], BF16)
    dqg_t = dt("dqg", [3 * P, C], BF16)
    dqd_t = dt("dqd", [P, DI], BF16)
    upspill_d = dt("upspill", [DI, TOK], BF16).ap()
    gspill_d = dt("gspill", [DI, TOK], BF16).ap()
    aspill_d = dt("aspill", [TOK, C], BF16).ap()
    halo_d = dt("halob", [1, C], BF16).ap()
    red_d = dt("redb", [1, P], F32).ap()
    red2_d = dt("redb2", [2, P], F32).ap()
    mxi = [dt(f"mxi{i}", [1, 16], F32) for i in range(3)]
    mxo = [dt(f"mxo{i}", [1, 16], F32) for i in range(3)]
    agi_t = dt("agi", [P, 16], F32)
    ago_t = dt("ago", [NCORES * P, 16], F32)

    sem = {}
    for n in ("va", "ac", "po", "pe", "uev", "umx"):
        sem[n] = nc.semaphore(n).__enter__()
    cnt = {n: 0 for n in sem}
    DFAMS = ("lp", "lx0", "lx1", "lx2", "lx3", "lx4", "lx5", "lx6", "lx7",
             "ldq0", "ldq1", "ldq2", "ldq3", "lwq0", "lwq1",
             "lu0", "lu1", "lc0", "lc1", "lc2", "lc3", "lc4", "lc5", "lc6", "lc7",
             "la0", "la1", "le0", "le1", "lr0", "lr1", "lo0", "lo1",
             "sdx", "sdu", "sdg", "sdd",
             "gwk0", "gwk1", "gwk2", "gwk3", "gwv0", "gwv1", "gwv2", "gwv3",
             "gwr0", "gwr1", "gwr2", "gwr3", "gwo0", "gwo1", "gwo2", "gwo3",
             "gmi", "cc", "ccu", "ccg", "ccd")
    for n in DFAMS:
        sem[n] = nc.semaphore(n).__enter__()
        cnt[n] = 0

    def mark(ins, s, n=1):
        if s in ("va", "ac", "po"):
            return cnt[s]
        cnt[s] += n
        ins.then_inc(sem[s], n)
        return cnt[s]

    def wt(eng, s, v):
        if v > 0:
            eng.wait_ge(sem[s], v)

    def wtt(eng, mk):
        if not mk:
            return
        if isinstance(mk, list):
            for m in mk:
                wtt(eng, m)
            return
        eng.wait_ge(sem[mk[0]], mk[1])

    def dma(eng, fam, builder, inc=16):
        if cnt[fam]:
            eng.wait_ge(sem[fam], cnt[fam])
        ins = builder()
        cnt[fam] += inc
        ins.then_inc(sem[fam], inc)
        return (fam, cnt[fam])

    def tot(fam):
        return (fam, cnt[fam])

    _RAW_METHODS = {"wait_ge", "nop", "print", "load_library", "sem_inc"}
    _DMA_METHODS = {"dma_start", "dma_start_transpose", "indirect_dma_start",
                    "collective_compute"}

    class RecIns:
        __slots__ = ("hooks",)

        def __init__(self):
            self.hooks = []

        def then_inc(self, semh, n=1):
            self.hooks.append(("then_inc", semh, n))
            return self

        def _wait_ge(self, semh, v):
            self.hooks.append(("_wait_ge", semh, v))
            return self

    class Rec:
        def __init__(self, ops, fam):
            self._ops = ops
            self._fam = fam

        def __getattr__(self, name):
            fam = self._fam

            def record(*a, **k):
                ri = RecIns()
                if (fam is not None and name not in _RAW_METHODS
                        and name not in _DMA_METHODS):
                    prev = cnt[fam]
                    if prev:
                        ri.hooks.append(("_wait_ge", sem[fam], prev))
                    cnt[fam] += 1
                    ri.hooks.append(("then_inc", sem[fam], 1))
                self._ops.append((name, a, k, ri))
                return ri
            return record

    _streams = {n: [] for n in ("vector", "scalar", "gpsimd", "tensor", "sync")}
    _FAM_OF2 = {"vector": "va", "scalar": "ac", "gpsimd": "po",
                "tensor": None, "sync": None}

    def flush_streams():
        for engname, ops in _streams.items():
            if not ops:
                continue

            def body(e, ops=ops):
                for name, a, k, ri in ops:
                    ins = getattr(e, name)(*a, **k)
                    for h in ri.hooks:
                        getattr(ins, h[0])(h[1], h[2])
            getattr(blk, engname)(body)

    def sb(name, shape, dtyp, side=None):
        cm = nc.sbuf_tensor(name, shape, dtyp, side=side)
        return cm.__enter__(), cm

    def psu(name, shape):
        cm = nc.psum_tensor(name, shape, F32)
        return cm.__enter__(), cm

    blk = nc.Block().__enter__()

    def on(engine, fn):
        fn(Rec(_streams[engine], _FAM_OF2[engine]))

    XBUF, cmXBUF = sb("XBUF", [P, 2, C], F32, side="right")
    DQ2, cmDQ2 = sb("DQ2", [P, 2, C // 2], F32, side="right")
    ZSC, cmZSC = sb("ZSC", [P, C // 2], F32, side="right")
    DQBLK, cmDQB = sb("DQBLK", [P, 32], F32, side="right")
    DQBLK2, cmDQB2 = sb("DQBLK2", [P, 32], F32, side="right")
    WQP, cmWQP = sb("WQP", [P, C // 2], BF16, side="right")
    HALO, cmHALO = sb("HALO", [1, C], F32, side="right")
    XNLY, cmXNLY = sb("XNLY", [P, C], BF16)
    SC, _ = sb("SCR", [P, 2, QT], F32)
    ONES32, _ = sb("ONES32", [1, P], F32)
    MAGT, _ = sb("MAGT", [P, 1], F32)
    EPST, _ = sb("EPST", [P, 1], F32)
    TD, _ = sb("TD", [P, NCH], F32)
    TF, _ = sb("TF", [P, NCH], F32)
    TMK, _ = sb("TMK", [P, NCH], F32)
    TMV, _ = sb("TMV", [P, NCH], F32)
    TMR, _ = sb("TMR", [P, NCH], F32)
    LNW, _ = sb("LNW", [P, NCH], F32)
    EW, _ = sb("EW", [P, NCH], F32)
    EU, _ = sb("EU", [P, NCH], F32)
    LNWO, _ = sb("LNWO", [P, NCH, NQ], F32)
    STAT, _ = sb("STAT", [P, 2, 8], F32)
    XMX, _ = sb("XMX", [P, 1], F32)
    RMX, _ = sb("RMX", [P, 4], F32)
    RROW, _ = sb("RROW", [1, 2 * P], F32)
    CCS, _ = sb("CCS", [1, 16], F32)
    SCAL, _ = sb("SCAL", [1, 16], F32)
    BCAST, _ = sb("BCAST", [P, 16], F32)
    CARA, _ = sb("CARA", [P, NCH], F32)
    CARB, _ = sb("CARB", [P, NCH], F32)
    STOUT, _ = sb("STOUT", [P, 16], F32)
    GST, _ = sb("GST", [P, NCORES, 16], F32)
    ASEL, _ = sb("ASEL", [P, 16], F32)
    SELB, _ = sb("SELB", [P, NCORES], F32)
    HSTA, _ = sb("HSTA", [1, 8], F32)
    HNL, _ = sb("HNL", [1, C], BF16)
    WTT, _ = sb("WTT", [P, 2, NCH, P], BF16)

    PB = []
    for i in range(4):
        h, _ = psu(f"PB{i}", [P, QT])
        PB.append(h)
    PEP = []
    for i in range(2):
        h, _ = psu(f"PEP{i}", [P, QT])
        PEP.append(h)
    PBC, _ = psu("PBC", [P, QT])
    PIOT, _ = psu("PIOT", [P, QT])
    DUMT, _ = sb("DUMT", [1, 2], F32)


    ld_x = {}
    ac_xn = {}
    sd_xp = {}
    ld_w = {}
    ac_wcast = {}
    dq_ld = {}
    dq_done = {}
    dq_free = {}
    po_z = {}
    dq_state = {"idx": 0, "u": 0, "g": 0, "d": 0}
    bank_free = [("va", 0)] * 4
    bank_state = {"i": 0}
    ep_bank_free = [("va", 0), ("va", 0)]
    ech_spill = {0: 0, 1: 0}
    atch_spill = {0: 0, 1: 0}
    uch_free = {0: 0, 1: 0}
    xch_free = {0: 0, 1: 0}
    och_free = {0: 0, 1: 0}
    atl_free = {0: 0, 1: 0}
    pe_up = {}
    va_nmr_prev = [0]

    def next_bank():
        b = bank_state["i"] % 4
        bank_state["i"] += 1
        return b

    def st0_sync(s):
        for dsttile, src in ((TD, td_d), (TF, tf_d), (TMK, tmk_d),
                             (TMV, tmv_d), (TMR, tmr_d)):
            dma(s, "lp", lambda d=dsttile, r=src: s.dma_start(out=d[:, :], in_=r[:, :]))
        dma(s, "lp", lambda: s.dma_start(out=HALO[:, :], in_=xh_d[:, :]))
        dma(s, "lp", lambda: s.dma_start(out=SELB[0:1, :], in_=sel_d[:, :]))
    on("sync", st0_sync)
    ld_params = tot("lp")

    def st0_vec(v):
        v.memset(DUMT[:, :], 1.0)
        v.memset(ONES32[:, :], 1.0)
        v.memset(MAGT[:, :], MAGIC)
        v.memset(EPST[:, :], EPS_LN)
        v.memset(ASEL[:, :], 0.0)
        v.memset(XMX[:, :], 0.0)
        v.memset(RMX[:, :], 0.0)
        mark(v.memset(CCS[:, :], 0.0), "va")
    on("vector", st0_vec)
    va_const = cnt["va"]

    def st0_act(s):
        wtt(s, ld_params)
        mark(s.activation(LNW[:, :], TD[:, :], AF.Exp), "ac")
    on("scalar", st0_act)
    ac_l0 = cnt["ac"]

    def st0_vec3(v):
        wt(v, "ac", ac_l0)
        mark(v.tensor_scalar(LNW[:, :], LNW[:, :], -1.0, None, ALU.mult), "va")
    on("vector", st0_vec3)
    va_lnw = cnt["va"]

    def st0_act2(s):
        wt(s, "va", va_lnw)
        s.activation(EW[:, :], LNW[:, :], AF.Exp)
        mark(s.activation(EU[:, :], TF[:, :], AF.Exp), "ac")
    on("scalar", st0_act2)
    ac_ewu = cnt["ac"]

    def st0_vec4(v):
        wt(v, "ac", ac_ewu)
        for q in range(NQ):
            v.tensor_scalar(LNWO[:, :, q], LNW[:, :], float(q * QT), None, ALU.mult)
        v.tensor_copy(CARA[:, :], ASEL[:, 0:NCH])
        mark(v.tensor_copy(CARB[:, :], ASEL[:, 0:NCH]), "va")
    on("vector", st0_vec4)
    va_init = cnt["va"]

    def halo_v(v):
        wtt(v, ld_params)
        mark(v.reduce_sum(HSTA[0:1, 0:1], HALO[0:1, :], AX.X), "va")
    on("vector", halo_v)
    va_hs = cnt["va"]

    def halo_v2(v):
        wt(v, "va", va_hs)
        v.scalar_tensor_tensor(SC[0:1, 1, 0:QT], HALO[0:1, 0:QT], 1.0,
                               HALO[0:1, 0:QT], ALU.bypass, ALU.mult,
                               accum_out=HSTA[0:1, 1:2])
        v.scalar_tensor_tensor(SC[0:1, 1, 0:QT], HALO[0:1, QT:C], 1.0,
                               HALO[0:1, QT:C], ALU.bypass, ALU.mult,
                               accum_out=HSTA[0:1, 6:7])
        v.tensor_tensor(HSTA[0:1, 1:2], HSTA[0:1, 1:2], HSTA[0:1, 6:7], ALU.add)
        v.tensor_scalar(HSTA[0:1, 2:3], HSTA[0:1, 0:1], 1.0 / C, None, ALU.mult)
        v.tensor_scalar(HSTA[0:1, 3:4], HSTA[0:1, 1:2], 1.0 / C, None, ALU.mult)
        v.tensor_tensor(HSTA[0:1, 4:5], HSTA[0:1, 2:3], HSTA[0:1, 2:3], ALU.mult)
        mark(v.tensor_tensor(HSTA[0:1, 5:6], HSTA[0:1, 3:4], HSTA[0:1, 4:5],
                             ALU.subtract), "va")
    on("vector", halo_v2)
    va_hv = cnt["va"]

    def halo_a2(a):
        wt(a, "va", va_hv)
        a.activation(HSTA[0:1, 6:7], HSTA[0:1, 5:6], AF.Ln, bias=EPST[0:1, 0:1])
        mark(a.activation(HSTA[0:1, 7:8], HSTA[0:1, 6:7], AF.Exp, scale=-0.5),
             "ac")
    on("scalar", halo_a2)
    ac_hr = cnt["ac"]

    def halo_v3(v):
        wt(v, "ac", ac_hr)
        mark(v.scalar_tensor_tensor(HSTA[0:1, 4:5], HSTA[0:1, 2:3], -1.0,
                                    HSTA[0:1, 7:8], ALU.mult, ALU.mult), "va")
    on("vector", halo_v3)
    va_hn = cnt["va"]

    def halo_v4(v):
        wt(v, "va", va_hn)
        v.scalar_tensor_tensor(HNL[0:1, :], HALO[0:1, :], HSTA[0:1, 7:8],
                               HSTA[0:1, 4:5].broadcast_to([1, C]),
                               ALU.mult, ALU.add)
    on("vector", halo_v4)
    ac_hnl = cnt["va"]

    cmHALO.__exit__(None, None, None)

    XNF, cmXNF = sb("XNF", [P, NCH, XST], BF16)
    NB, cmNB = sb("NB", [P, NCH, TOK], BF16)
    DB, cmDB = sb("DB", [P, NCH, TOK], BF16)
    DIFQ, cmDIFQ = sb("DIFQ", [P, NCH, QT], BF16)
    XMQ, cmXMQ = sb("XMQ", [P, NCH, QT], BF16)
    XM2Q, cmXM2Q = sb("XM2Q", [P, NCH, QT], BF16)
    WKT, cmWKT = sb("WKT", [P, NCH, C], BF16, side="right")
    WVT, cmWVT = sb("WVT", [P, NCH, C], BF16, side="right")
    EKQ, cmEKQ = sb("EKQ", [P, NCH, QT], BF16, side="right")
    EKVQ, cmEKVQ = sb("EKVQ", [P, NCH, QT], BF16, side="right")
    SA, cmSA = sb("SA", [P, 2, QT + 2], BF16, side="right")
    SB_, cmSB = sb("SB", [P, 2, QT + 2], BF16, side="right")

    xt = x_d.rearrange("(n p) c -> n p c", p=P)

    def load_xtile(k, wait_ac=0):
        def f(a, k=k, wait_ac=wait_ac):
            wt(a, "va", wait_ac)
            ld_x[k] = dma(a, f"lx{k % 2}",
                          lambda: a.dma_start(out=XBUF[:, k % 2, :],
                                              in_=xt[k, :, :]))
        on("scalar", f)

    def load_wtile(tag, src, j, dsttile, wait_pe=0, wait_va=0):
        def f(g, tag=tag, src=src, j=j, dsttile=dsttile, wait_pe=wait_pe,
              wait_va=wait_va):
            if wait_pe and j == 0:
                wt(g, "pe", wait_pe)
            if wait_va and j == 0:
                wt(g, "va", wait_va)
            ld_w[(tag, j)] = dma(
                g, "gw" + tag + str(j % 4),
                lambda: g.dma_start(out=dsttile[:, j, :],
                                    in_=src[j * P:(j + 1) * P, :]))
        on("gpsimd", f)

    gcc = {}

    def dequant_piece(tag):
        C2 = C // 2
        if tag == "u":
            r = dq_state["u"]; dq_state["u"] += 1
            src_full = wup_d[r * P:(r + 1) * P, :]
            dst_full = dqu_t.ap()[r * P:(r + 1) * P, :]
        elif tag == "g":
            r = dq_state["g"]; dq_state["g"] += 1
            src_full = wgate_d[r * P:(r + 1) * P, :]
            dst_full = dqg_t.ap()[r * P:(r + 1) * P, :]
        else:
            cp = dq_state["d"]; dq_state["d"] += 1
            src_full = wdown_d[:, cp * C:(cp + 1) * C]
            dst_full = dqd_t.ap()[:, cp * C:(cp + 1) * C]

        for h in range(2):
            i = dq_state["idx"]
            dq_state["idx"] += 1
            sl = i % 2
            src_ap = src_full[:, h * C2:(h + 1) * C2]
            dst_ap = dst_full[:, h * C2:(h + 1) * C2]

            def f(s, i=i, sl=sl, src_ap=src_ap):
                wt(s, "va", dq_free.get(i - 2, 0))
                dq_ld[i] = dma(s, f"ldq{i % 4}",
                               lambda: s.dma_start(out=DQ2[:, sl, :],
                                                   in_=src_ap))
            on("sync", f)

            def g(v, i=i, sl=sl):
                wtt(v, dq_ld[i])
                v.reduce_max(DQBLK[:, 0:16, None],
                             DQ2[:, sl, :].rearrange("p (b e) -> p b e", e=32),
                             AX.X, apply_absolute_value=True)
                v.tensor_scalar(DQBLK2[:, 0:16], DQBLK[:, 0:16], 1.0 / 7.0,
                                EPS_Q, ALU.mult, ALU.max)
                v.reciprocal(DQBLK[:, 0:16], DQBLK2[:, 0:16])
                v.tensor_tensor(ZSC[:, :].rearrange("p (b e) -> p b e", e=32),
                                DQ2[:, sl, :].rearrange("p (b e) -> p b e",
                                                        e=32),
                                DQBLK[:, 0:16, None].broadcast_to([P, 16, 32]),
                                ALU.mult)
                v.tensor_scalar(ZSC[:, :], ZSC[:, :], 1.0, MAGIC,
                                ALU.mult, ALU.add)
                wtt(v, dq_done.get(i - 1, 0))
                dq_free[i] = mark(
                    v.scalar_tensor_tensor(
                        WQP[:, :].rearrange("p (b e) -> p b e", e=32),
                        ZSC[:, :].rearrange("p (b e) -> p b e", e=32), -MAGIC,
                        DQBLK2[:, 0:16, None].broadcast_to([P, 16, 32]),
                        ALU.add, ALU.mult), "va")
            on("vector", g)

            def f2(s, i=i, sl=sl, dst_ap=dst_ap):
                wt(s, "va", dq_free[i])
                dq_done[i] = dma(s, f"lwq{sl}",
                                 lambda: s.dma_start(out=dst_ap,
                                                     in_=WQP[:, :]))
            on("sync", f2)

    load_xtile(0)
    load_xtile(1)

    def ln_stats(k):
        sl = k % 2
        scol = STAT[:, sl, :]

        def ln_v1(v, k=k, sl=sl, scol=scol):
            wtt(v, ld_x[k])
            if k >= 2:
                wt(v, "va", ac_xn[k - 2])
            mark(v.reduce_sum(scol[:, 0:1], XBUF[:, sl, :], AX.X), "va")
        on("vector", ln_v1)

        def ln_v2(v, k=k, sl=sl, scol=scol):
            wtt(v, ld_x[k])
            v.scalar_tensor_tensor(SC[:, 0, 0:QT], XBUF[:, sl, 0:QT], 1.0,
                                   XBUF[:, sl, 0:QT], ALU.bypass, ALU.mult,
                                   accum_out=scol[:, 1:2])
            v.scalar_tensor_tensor(SC[:, 0, 0:QT], XBUF[:, sl, QT:C], 1.0,
                                   XBUF[:, sl, QT:C], ALU.bypass, ALU.mult,
                                   accum_out=scol[:, 6:7])
            v.tensor_tensor(scol[:, 1:2], scol[:, 1:2], scol[:, 6:7], ALU.add)
            v.tensor_scalar(scol[:, 2:3], scol[:, 0:1], 1.0 / C, None,
                            ALU.mult)
            v.tensor_scalar(scol[:, 3:4], scol[:, 1:2], 1.0 / C, None,
                            ALU.mult)
            v.tensor_tensor(scol[:, 4:5], scol[:, 2:3], scol[:, 2:3], ALU.mult)
            mark(v.tensor_tensor(scol[:, 5:6], scol[:, 3:4], scol[:, 4:5],
                                 ALU.subtract), "va")
        on("vector", ln_v2)
        va_var = cnt["va"]

        def ln_a2(a, scol=scol, va_var=va_var):
            wt(a, "va", va_var)
            a.activation(scol[:, 6:7], scol[:, 5:6], AF.Ln, bias=EPST[:, 0:1])
            mark(a.activation(scol[:, 7:8], scol[:, 6:7], AF.Exp, scale=-0.5),
                 "ac")
        on("scalar", ln_a2)
        ac_rstd[k] = cnt["ac"]

    def ln_norm(k):
        sl = k % 2
        scol = STAT[:, sl, :]

        def ln_v3(v, scol=scol, ac_r=ac_rstd[k]):
            wt(v, "ac", ac_r)
            mark(v.scalar_tensor_tensor(scol[:, 4:5], scol[:, 2:3], -1.0,
                                        scol[:, 7:8], ALU.mult, ALU.mult),
                 "va")
        on("vector", ln_v3)

        def ln_v3b(v, k=k, sl=sl, scol=scol):
            if k >= 1:
                wtt(v, sd_xp[k - 1])
            v.scalar_tensor_tensor(XNLY[:, :], XBUF[:, sl, :], scol[:, 7:8],
                                   scol[:, 4:5].broadcast_to([P, C]),
                                   ALU.mult, ALU.add)
            ac_xn[k] = cnt["va"]
        on("vector", ln_v3b)

        def ln_v4(v, k=k):
            v.reduce_max(RMX[:, 2:3], XNLY[:, :], AX.X,
                         apply_absolute_value=True)
            mark(v.tensor_tensor(XMX[:, 0:1], XMX[:, 0:1], RMX[:, 2:3],
                                 ALU.max), "va")
        on("vector", ln_v4)

        def ln_xp(s, k=k):
            wt(s, "va", ac_xn[k])
            sd_xp[k] = dma(s, "sdx",
                           lambda: s.dma_start_transpose(
                               out=XNF[:, :, XCOL0 + 128 * k:
                                     XCOL0 + 128 * (k + 1)],
                               in_=XNLY[:, :]))
        on("scalar", ln_xp)

        if k + 2 < NT128:
            load_xtile(k + 2, wait_ac=ac_xn[k])

        if k < NCH:
            load_wtile("k", wkT_d, k, WKT, wait_va=ac_hnl)
        else:
            load_wtile("v", wvT_d, k - NCH, WVT)

        if 4 <= k <= 12:
            dequant_piece(("u", "u", "u", "g", "g", "g", "d", "d", "d")[k - 4])

    ac_rstd = {}
    for k in range(NT128 + 1):
        if k < NT128:
            ln_stats(k)
        if k >= 1:
            ln_norm(k - 1)

    va_xmx_final = cnt["va"]


    def halo_g(g):
        wt(g, "va", ac_hnl)
        dma(g, "gmi", lambda: g.dma_start(out=halo_d[:, :], in_=HNL[:, :]))
        dma(g, "gmi", lambda: g.dma_start(
            out=XNF[:, :, XCOL0 - 1:XCOL0],
            in_=halo_d[0:1, :].rearrange("a (m p) -> p m a", p=P),
            allow_slow_non_contiguous=True))
    on("gpsimd", halo_g)
    gd_halo = tot("gmi")

    def c1_g(g):
        wt(g, "va", va_xmx_final)
        dma(g, "gmi", lambda: g.dma_start(
            out=red_d[0:1, :].rearrange("a p -> p a"),
            in_=XMX[:, 0:1], allow_slow_non_contiguous=True))
        dma(g, "gmi", lambda: g.dma_start(out=RROW[0:1, 0:P],
                                          in_=red_d[:, :]))
    on("gpsimd", c1_g)
    gd_rrow = tot("gmi")

    def c1_v(v):
        wtt(v, gd_rrow)
        mark(v.reduce_max(CCS[0:1, 0:1], RROW[0:1, 0:P], AX.X,
                          apply_absolute_value=True), "va")
    on("vector", c1_v)
    va_ccs1 = cnt["va"]

    def c1_g2(g):
        wt(g, "va", va_ccs1)
        wtt(g, dma(g, "gmi", lambda: g.dma_start(out=mxi[0].ap(),
                                                 in_=CCS[:, :])))
        dma(g, "cc", lambda: g.collective_compute(
            "AllReduce", ALU.max, replica_groups=[list(range(NCORES))],
            ins=[mxi[0].ap().opt()], outs=[mxo[0].ap().opt()]), inc=1)
        wtt(g, tot("cc"))
        dma(g, "gmi", lambda: g.dma_start(out=SCAL[0:1, 0:1],
                                          in_=mxo[0].ap()[0:1, 0:1]))
    on("gpsimd", c1_g2)
    gd_s1 = tot("gmi")

    def wq_cc(g):
        wtt(g, [dq_done[k] for k in range(0, 6)])
        dma(g, "ccu", lambda: g.collective_compute(
            "AllGather", ALU.bypass, replica_groups=[list(range(NCORES))],
            ins=[dqu_t.ap().opt()], outs=[wqu_t.ap().opt()]), inc=1)
        gcc["u"] = tot("ccu")
        wtt(g, [dq_done[k] for k in range(6, 12)])
        dma(g, "ccg", lambda: g.collective_compute(
            "AllGather", ALU.bypass, replica_groups=[list(range(NCORES))],
            ins=[dqg_t.ap().opt()], outs=[wqg_t.ap().opt()]), inc=1)
        gcc["g"] = tot("ccg")
        wtt(g, [dq_done[k] for k in range(12, 18)])
        dma(g, "ccd", lambda: g.collective_compute(
            "AllGather", ALU.bypass, replica_groups=[list(range(NCORES))],
            ins=[dqd_t.ap().opt()], outs=[wqd_t.ap().opt()]), inc=1)
        gcc["d"] = tot("ccd")
    on("gpsimd", wq_cc)

    def c1_v2(v):
        wtt(v, gd_s1)
        v.tensor_scalar(SCAL[0:1, 1:2], SCAL[0:1, 0:1], 1.0 / QMAX, EPS_Q,
                        ALU.mult, ALU.max)
        mark(v.reciprocal(SCAL[0:1, 2:3], SCAL[0:1, 1:2]), "va")
    on("vector", c1_v2)
    va_s1 = cnt["va"]

    def c1_pe(t):
        wt(t, "va", va_s1)
        mark(t.matmul(PBC[:, 0:2], ONES32[:, :], SCAL[0:1, 1:3],
                      start=True, stop=True), "pe")
    on("tensor", c1_pe)
    pe_b1 = cnt["pe"]

    def c1_v3(v):
        wt(v, "pe", pe_b1)
        v.tensor_copy(BCAST[:, 1:2], PBC[:, 0:1])
        mark(v.tensor_copy(BCAST[:, 0:1], PBC[:, 1:2]), "va")
    on("vector", c1_v3)
    va_b1 = cnt["va"]

    def x_cols(q):
        return (XCOL0 + q * QT, XCOL0 + (q + 1) * QT)

    def xx_cols(q):
        return (XCOL0 - 1 + q * QT, XCOL0 - 1 + (q + 1) * QT)

    va_scan = {}
    po_nbdb = {}
    nb_prev = [None]
    nb_prev2 = [None]
    ac_ek_m = {}
    pe_kv_last = [0]

    for q in range(NQ):
        a0, a1 = x_cols(q)
        b0, b1 = xx_cols(q)

        def mix_k(v, q=q, a0=a0, a1=a1, b0=b0, b1=b1,
                  pe_prev=pe_kv_last[0]):
            kmax = (q * QT + QT) // 128 - 1
            wtt(v, sd_xp[kmax])
            wt(v, "pe", pe_prev)
            if q == 0:
                wtt(v, gd_halo)
                wt(v, "va", va_init)
            v.tensor_tensor(DIFQ[:, :, :], XNF[:, :, a0:a1], XNF[:, :, b0:b1],
                            ALU.subtract)
            for j in range(NCH):
                v.tensor_scalar(XMQ[:, j, :], DIFQ[:, j, :], TMK[:, j:j + 1],
                                None, ALU.mult)
                v.tensor_tensor(XMQ[:, j, :], XMQ[:, j, :], XNF[:, j, b0:b1],
                                ALU.add)
            mark(v.engine_nop(), "va")
        on("vector", mix_k)
        va_mixk = cnt["va"]

        def mix_v2(v, q=q, a0=a0, a1=a1, b0=b0, b1=b1,
                   pe_prev2=pe_kv_last[0]):
            wt(v, "pe", pe_prev2)
            for j in range(NCH):
                v.tensor_scalar(XM2Q[:, j, :], DIFQ[:, j, :], TMV[:, j:j + 1],
                                None, ALU.mult)
                v.tensor_tensor(XM2Q[:, j, :], XM2Q[:, j, :], XNF[:, j, b0:b1],
                                ALU.add)
            mark(v.engine_nop(), "va")
        on("vector", mix_v2)
        va_mixv = cnt["va"]

        kb = {}
        for m in range(NCH):
            bk = next_bank()
            kb[m] = bk

            def k_mm(t, m=m, bk=bk, va_mixk=va_mixk, fk=bank_free[bk]):
                wt(t, "va", va_mixk)
                wtt(t, [ld_w[("k", jj)] for jj in range(NCH - 4, NCH)])
                wt(t, *fk)
                for j in range(NCH):
                    ins = t.matmul(PB[bk][:, :], WKT[:, j, 128 * m:128 * (m + 1)],
                             XMQ[:, j, :], start=(j == 0), stop=(j == NCH - 1))
                    if j == NCH - 1:
                        mark(ins, "pe")
            on("tensor", k_mm)
            pe_k = cnt["pe"]

            def k_ev(a, m=m, bk=bk, pe_k=pe_k,
                     w_po=po_nbdb.get((q - 1, m), 0) if q else 0):
                wt(a, "pe", pe_k)
                wt(a, "po", w_po)
                mark(a.activation(EKQ[:, m, :], PB[bk][:, :], AF.Exp), "ac")
            on("scalar", k_ev)
            bank_free[bk] = ("ac", cnt["ac"])
            ac_ek_m[m] = cnt["ac"]
        ac_ek_all = cnt["ac"]
        pe_k_all = cnt["pe"]

        for m in range(NCH):
            bv = next_bank()

            def v_mm(t, m=m, bv=bv, va_mixv=va_mixv, fv=bank_free[bv]):
                wt(t, "va", va_mixv)
                wtt(t, [ld_w[("v", jj)] for jj in range(NCH - 4, NCH)])
                wt(t, *fv)
                for j in range(NCH):
                    ins = t.matmul(PB[bv][:, :], WVT[:, j, 128 * m:128 * (m + 1)],
                             XM2Q[:, j, :], start=(j == 0), stop=(j == NCH - 1))
                    if j == NCH - 1:
                        mark(ins, "pe")
            on("tensor", v_mm)
            pe_v = cnt["pe"]

            ssl = (q * NCH + m) % 2

            def scan_v(v, q=q, m=m, bv=bv, ssl=ssl, pe_v=pe_v,
                       ac_ek=ac_ek_m[m],
                       w_po=po_nbdb.get(nb_prev2[0], 0)):
                wt(v, "ac", ac_ek)
                wt(v, "pe", pe_v)
                wt(v, "po", w_po)
                v.tensor_tensor(EKVQ[:, m, :], EKQ[:, m, :], PB[bv][:, :],
                                ALU.mult)
                v.tensor_copy(SA[:, ssl, 0:1], CARA[:, m:m + 1])
                v.tensor_copy(SB_[:, ssl, 0:1], CARB[:, m:m + 1])
                v.tensor_tensor_scan(SA[:, ssl, 1:QT + 1],
                                     EW[:, m:m + 1].broadcast_to([P, QT]),
                                     EKVQ[:, m, :], CARA[:, m:m + 1],
                                     ALU.mult, ALU.add)
                v.tensor_tensor_scan(SB_[:, ssl, 1:QT + 1],
                                     EW[:, m:m + 1].broadcast_to([P, QT]),
                                     EKQ[:, m, :], CARB[:, m:m + 1],
                                     ALU.mult, ALU.add)
                v.tensor_copy(CARA[:, m:m + 1], SA[:, ssl, QT:QT + 1])
                v.tensor_copy(CARB[:, m:m + 1], SB_[:, ssl, QT:QT + 1])
                v.scalar_tensor_tensor(NB[:, m, q * QT:(q + 1) * QT],
                                       EKVQ[:, m, :], EU[:, m:m + 1],
                                       SA[:, ssl, 0:QT], ALU.mult, ALU.add)
                mark(v.scalar_tensor_tensor(DB[:, m, q * QT:(q + 1) * QT],
                                            EKQ[:, m, :], EU[:, m:m + 1],
                                            SB_[:, ssl, 0:QT],
                                            ALU.mult, ALU.add), "va")
            on("vector", scan_v)
            va_scan[(q, m)] = cnt["va"]
            bank_free[bv] = ("va", cnt["va"])
            po_nbdb[(q, m)] = 0
            nb_prev2[0] = nb_prev[0]
            nb_prev[0] = (q, m)
        pe_kv_last[0] = cnt["pe"]

    va_scan_done = cnt["va"]

    def exp_v(v):
        wt(v, "va", va_scan_done)
        v.tensor_copy(STOUT[:, 0:NCH], CARA[:, :])
        mark(v.tensor_copy(STOUT[:, NCH:16], CARB[:, :]), "va")
    on("vector", exp_v)
    va_stout = cnt["va"]

    def exp_g(g):
        wt(g, "va", va_stout)
        wtt(g, dma(g, "gmi", lambda: g.dma_start(out=agi_t.ap(),
                                                 in_=STOUT[:, :])))
        dma(g, "cc", lambda: g.collective_compute(
            "AllGather", ALU.bypass, replica_groups=[list(range(NCORES))],
            ins=[agi_t.ap().opt()], outs=[ago_t.ap().opt()]), inc=1)
        wtt(g, tot("cc"))
        dma(g, "gmi", lambda: g.dma_start(
            out=GST[:, :, :],
            in_=ago_t.ap().rearrange("(r p) c -> p r c", p=P)))
    on("gpsimd", exp_g)
    gd_gst = tot("gmi")

    for cm in (cmXM2Q, cmSB, cmSA, cmEKVQ, cmEKQ, cmWVT, cmWKT,
               cmWQP, cmDQB2, cmDQB, cmZSC, cmDQ2, cmXBUF):
        cm.__exit__(None, None, None)

    SR, cmSR = sb("SR", [P, NCH, TOK], BF16, side="right")
    RWQ, cmRWQ = sb("RWQ", [P, NCH, QT], BF16, side="right")
    PCHT, cmPCHT = sb("PCHT", [P, QT], BF16, side="right")
    RCHT, cmRCHT = sb("RCHT", [P, QT], BF16, side="right")
    ECH, cmECH = sb("ECH", [P, 2, QT], BF16, side="right")
    WRT, cmWRT = sb("WRT", [P, NCH, C], BF16, side="right")

    pe_kv_done = pe_kv_last[0]

    pe_rq_last = [pe_kv_done]

    def rw_guard(g):
        wtt(g, dq_done[17])
    on("gpsimd", rw_guard)
    for j in range(NCH):
        load_wtile("r", wrT_d, j, WRT, wait_pe=pe_kv_done)

    for q in range(NQ):
        a0, a1 = x_cols(q)
        b0, b1 = xx_cols(q)

        def rmix_v(v, a0=a0, a1=a1, b0=b0, b1=b1, pe_prev=pe_rq_last[0]):
            wt(v, "pe", pe_prev)
            v.tensor_tensor(DIFQ[:, :, :], XNF[:, :, a0:a1], XNF[:, :, b0:b1],
                            ALU.subtract)
            for j in range(NCH):
                v.tensor_scalar(XMQ[:, j, :], DIFQ[:, j, :], TMR[:, j:j + 1],
                                None, ALU.mult)
                v.tensor_tensor(XMQ[:, j, :], XMQ[:, j, :], XNF[:, j, b0:b1],
                                ALU.add)
            mark(v.engine_nop(), "va")
        on("vector", rmix_v)
        va_rmix = cnt["va"]

        for m in range(NCH):
            br = next_bank()

            def r_mm(t, q=q, m=m, br=br, va_rmix=va_rmix, fb=bank_free[br]):
                wt(t, "va", va_rmix)
                wtt(t, [ld_w[("r", jj)] for jj in range(NCH - 4, NCH)])
                wt(t, *fb)
                for j in range(NCH):
                    ins = t.matmul(PB[br][:, :], WRT[:, j, 128 * m:128 * (m + 1)],
                             XMQ[:, j, :], start=(j == 0), stop=(j == NCH - 1))
                    if j == NCH - 1:
                        mark(ins, "pe")
            on("tensor", r_mm)
            pe_r = cnt["pe"]

            def r_ev(a, q=q, m=m, br=br, pe_r=pe_r):
                wt(a, "pe", pe_r)
                mark(a.activation(SR[:, m, q * QT:(q + 1) * QT], PB[br][:, :],
                                  AF.Sigmoid), "ac")
            on("scalar", r_ev)
            bank_free[br] = ("ac", cnt["ac"])
            pe_rq_last[0] = pe_r

    ac_sr_done = cnt["ac"]
    pe_r_done = cnt["pe"]
    va_rmix_done = cnt["va"]

    cmWRT.__exit__(None, None, None)
    cmXMQ.__exit__(None, None, None)
    cmDIFQ.__exit__(None, None, None)
    WOT, cmWOT = sb("WOT", [P, NCH, C], BF16, side="right")
    PCHTB, cmPCHTB = sb("PCHTB", [P, QT], BF16, side="right")
    RCHTB, cmRCHTB = sb("RCHTB", [P, QT], BF16, side="right")
    UMAXR, cmUMAXR = sb("UMAXR", [P, QT], BF16, side="right")
    UMINR, cmUMINR = sb("UMINR", [P, QT], BF16, side="right")
    GMAXR, cmGMAXR = sb("GMAXR", [P, QT], BF16, side="right")
    GMINR, cmGMINR = sb("GMINR", [P, QT], BF16, side="right")
    OPCH, cmOPCH = sb("OPCH", [P, 2, QT], BF16, side="right")

    def umx_v(v):
        v.memset(UMAXR[:, :], 0.0)
        v.memset(UMINR[:, :], 0.0)
        v.memset(GMAXR[:, :], 0.0)
        mark(v.memset(GMINR[:, :], 0.0), "va")
    on("vector", umx_v)
    va_umx0 = cnt["va"]
    for j in range(NCH):
        load_wtile("o", woT_d, j, WOT, wait_pe=pe_r_done)
    gd_wot = [ld_w[("o", jj)] for jj in range(NCH - 4, NCH)]

    def rio_g(g):
        wt(g, "va", va_rmix_done)
        mark(g.iota(SC[0:1, 0, 0:QT], [[1, QT]], channel_multiplier=0,
                    allow_small_or_imprecise_dtypes=True), "po")
    on("gpsimd", rio_g)
    po_iota = cnt["po"]

    def rio_pe(t):
        wt(t, "po", po_iota)
        mark(t.matmul(PIOT[:, 0:QT], ONES32[:, :], SC[0:1, 0, 0:QT],
                      start=True, stop=True), "pe")
    on("tensor", rio_pe)
    pe_iota2 = cnt["pe"]

    ac_xq = {}
    va_xq = {}
    for jj in range(NQ * NCH):
        j, qq = divmod(jj, NQ)
        c0 = XCOL0 + qq * QT

        def xq_a(p, jj=jj, j=j, c0=c0):
            if jj == 0:
                wt(p, "va", va_b1)
                wt(p, "va", va_rmix_done)
                wt(p, "pe", pe_iota2)
            wt(p, "va", va_xq.get(jj - 2, 0))
            ac_xq[jj] = mark(
                p.tensor_scalar(SC[:, jj % 2, :], XNF[:, j, c0:c0 + QT],
                                BCAST[:, 0:1], MAGIC, ALU.mult, ALU.add), "po")
        on("gpsimd", xq_a)

        def xq_v(v, jj=jj, j=j, c0=c0):
            wt(v, "po", ac_xq[jj])
            va_xq[jj] = mark(
                v.tensor_scalar(XNF[:, j, c0:c0 + QT], SC[:, jj % 2, :],
                                -MAGIC, BCAST[:, 1:2], ALU.add, ALU.mult), "va")
        on("vector", xq_v)
    va_xq_done = cnt["va"]

    sd_upt = {}
    va_mx = {}
    ac_ev = {}
    gsp_done = {}

    def emit_up_step(step):
        ii, q = divmod(step, NQ)
        up = ii < NI
        i = ii if up else ii - NI
        sl = ii % 2
        if q == 0:
            def up_xp(s, ii=ii, i=i, up=up, sl=sl):
                wtt(s, gcc["u" if up else "g"])
                wt(s, "pe", pe_up.get(ii - 2, 0))
                sd_upt[ii] = dma(
                    s, "sdu",
                    lambda: s.dma_start_transpose(
                        out=WTT[:, sl, :, :],
                        in_=(wqu_d if up else wqg_d)[i * P:(i + 1) * P, :]))
            on("sync", up_xp)
        bu = next_bank()
        esl = step % 2

        def up_mm(t, ii=ii, sl=sl, q=q, bu=bu, fb=bank_free[bu]):
            wtt(t, sd_upt[ii])
            wt(t, "va", va_xq_done)
            wt(t, *fb)
            for j in range(NCH):
                ins = t.matmul(PB[bu][:, :], WTT[:, sl, j, :],
                         XNF[:, j, XCOL0 + q * QT:XCOL0 + (q + 1) * QT],
                         start=(j == 0), stop=(j == NCH - 1))
                if j == NCH - 1:
                    mark(ins, "pe")
        on("tensor", up_mm)
        pe_m = cnt["pe"]
        if q == NQ - 1:
            pe_up[ii] = pe_m

        def up_ev(a, bu=bu, esl=esl, pe_m=pe_m, w_sp=ech_spill[esl],
                  w_mx=va_mx.get(step - 2, 0)):
            wt(a, "pe", pe_m)
            wtt(a, w_sp)
            wt(a, "va", w_mx)
            ac_ev[step] = mark(
                a.activation(ECH[:, esl, :], PB[bu][:, :], AF.Identity), "ac")
        on("scalar", up_ev)
        bank_free[bu] = ("ac", ac_ev[step])

        def up_mx(v, up=up, esl=esl, step=step, first=(step == 0)):
            if first:
                wt(v, "va", va_umx0)
            wt(v, "ac", ac_ev[step])
            mxr, mnr = (UMAXR, UMINR) if up else (GMAXR, GMINR)
            v.tensor_tensor(mxr[:, :], mxr[:, :], ECH[:, esl, :], ALU.max)
            va_mx[step] = mark(
                v.tensor_tensor(mnr[:, :], mnr[:, :], ECH[:, esl, :],
                                ALU.min), "va")
        on("vector", up_mx)

        def up_sp(g, up=up, i=i, q=q, esl=esl, step=step):
            wt(g, "ac", ac_ev[step])
            dst = upspill_d if up else gspill_d
            ech_spill[esl] = dma(g, f"lu{esl}", lambda: g.dma_start(
                out=dst[i * P:(i + 1) * P, q * QT:(q + 1) * QT],
                in_=ECH[:, esl, :]))
        on("gpsimd", up_sp)
        if (not up) and q == NQ - 1:
            gsp_done[i] = [tot("lu0"), tot("lu1")]

    def sel_pe(t):
        wtt(t, ld_params)
        wt(t, "va", va_b1)
        mark(t.matmul(PBC[:, 0:NCORES], ONES32[:, :], SELB[0:1, :],
                      start=True, stop=True), "pe")
    on("tensor", sel_pe)
    pe_selb = cnt["pe"]

    def sel_v(v):
        wt(v, "pe", pe_selb)
        v.tensor_copy(SELB[:, :], PBC[:, 0:NCORES])
        wtt(v, gd_gst)
        for r in range(NCORES):
            v.scalar_tensor_tensor(ASEL[:, :], GST[:, r, :], SELB[:, r:r + 1],
                                   ASEL[:, :], ALU.mult, ALU.add)
        mark(v.engine_nop(), "va")
    on("vector", sel_v)
    va_asel = cnt["va"]

    va_rw = {}
    va_pm_l = {}
    va_fix_l = {}
    ac_p_l = {}
    ac_rc_l = {}
    NTOT = NQ * NCH

    def t_pch(idx):
        return PCHT if idx % 2 == 0 else PCHTB

    def t_rch(idx):
        return RCHT if idx % 2 == 0 else RCHTB

    def emit_v1(idx):
        q, m = divmod(idx, NCH)

        def tl_v1(v, idx=idx, q=q, m=m):
            if idx == 0:
                wt(v, "va", va_asel)
                wt(v, "pe", pe_iota2)
            wt(v, "ac", ac_rc_l.get(idx - 2, 0))
            va_pm_l[idx] = mark(
                v.tensor_scalar(SC[:, idx % 2, 0:QT], PIOT[:, 0:QT],
                                LNW[:, m:m + 1], LNWO[:, m, q:q + 1],
                                ALU.mult, ALU.add), "va")
        on("vector", tl_v1)

    def emit_actA(idx):
        def f(a, idx=idx):
            wt(a, "va", va_pm_l[idx])
            ac_p_l[idx] = mark(
                a.activation(t_pch(idx)[:, :], SC[:, idx % 2, 0:QT], AF.Exp),
                "ac")
        on("scalar", f)

    def emit_v2(idx):
        q, m = divmod(idx, NCH)
        cl = slice(q * QT, (q + 1) * QT)

        def tl_v2(v, idx=idx, q=q, m=m, cl=cl):
            wt(v, "ac", ac_p_l[idx])
            wt(v, "po", po_nbdb[(q, m)])
            v.scalar_tensor_tensor(NB[:, m, cl], t_pch(idx)[:, :],
                                   ASEL[:, m:m + 1], NB[:, m, cl],
                                   ALU.mult, ALU.add)
            va_fix_l[idx] = mark(
                v.scalar_tensor_tensor(DB[:, m, cl], t_pch(idx)[:, :],
                                       ASEL[:, NCH + m:NCH + m + 1],
                                       DB[:, m, cl], ALU.mult, ALU.add), "va")
        on("vector", tl_v2)

    def emit_actB(idx):
        q, m = divmod(idx, NCH)
        cl = slice(q * QT, (q + 1) * QT)

        def f(a, idx=idx, m=m, cl=cl):
            wt(a, "va", va_fix_l[idx])
            a.activation(SC[:, idx % 2, 0:QT], DB[:, m, cl], AF.Ln)
            ac_rc_l[idx] = mark(
                a.activation(t_rch(idx)[:, :], SC[:, idx % 2, 0:QT], AF.Exp,
                             scale=-1.0), "ac")
        on("scalar", f)

    def emit_v3(idx):
        q, m = divmod(idx, NCH)
        cl = slice(q * QT, (q + 1) * QT)

        def tl_v3(v, idx=idx, m=m, cl=cl):
            wt(v, "ac", ac_rc_l[idx])
            wt(v, "ac", ac_sr_done)
            v.tensor_tensor(RWQ[:, m, :], NB[:, m, cl], t_rch(idx)[:, :],
                            ALU.mult)
            mark(v.tensor_tensor(RWQ[:, m, :], RWQ[:, m, :], SR[:, m, cl],
                                 ALU.mult), "va")
        on("vector", tl_v3)
        va_rw[(q, m)] = cnt["va"]

    def emit_outproj(q):
        for tb in range(4):
            for hh in range(2):
                bq = (tb * 2 + hh) % 2
                asl = (tb * 2 + hh) % 2

                def op_mm(t, q=q, tb=tb, bq=bq, hh=hh,
                          fb=ep_bank_free[bq], vrw=va_rw[(q, NCH - 1)]):
                    wt(t, "va", vrw)
                    wtt(t, gd_wot)
                    wt(t, *fb)
                    for m in range(NCH):
                        ins = t.matmul(PEP[bq][:, :],
                                 RWQ[:, m, 128 * tb:128 * (tb + 1)],
                                 WOT[:, m, hh * QT:(hh + 1) * QT],
                                 start=(m == 0), stop=(m == NCH - 1))
                        if m == NCH - 1:
                            mark(ins, "pe")
                on("tensor", op_mm)
                pe_op = cnt["pe"]

                def op_ev(v, q=q, tb=tb, bq=bq, asl=asl, pe_op=pe_op,
                          w_sp=atch_spill[asl]):
                    wt(v, "pe", pe_op)
                    wtt(v, w_sp)
                    mark(v.tensor_copy(OPCH[:, asl, :], PEP[bq][:, :]), "va")
                on("vector", op_ev)
                ac_ae = cnt["va"]
                ep_bank_free[bq] = ("va", ac_ae)

                def op_sp(s, q=q, tb=tb, hh=hh, asl=asl, ac_ae=ac_ae):
                    wt(s, "va", ac_ae)
                    r0 = q * QT + tb * 128
                    atch_spill[asl] = dma(s, f"la{asl}", lambda: s.dma_start(
                        out=aspill_d[r0:r0 + 128, hh * QT:(hh + 1) * QT],
                        in_=OPCH[:, asl, :]))
                on("sync", op_sp)

    def emit_tail_step(st):
        if st < NTOT:
            emit_v1(st)
            emit_actA(st)
        idx = st - 1
        if 0 <= idx < NTOT:
            emit_v2(idx)
            emit_actB(idx)
            emit_v3(idx)
            if idx % NCH == NCH - 1:
                emit_outproj(idx // NCH)

    UPS = 2 * NI * NQ
    tstep = [0]
    for ustep in range(UPS):
        emit_up_step(ustep)
        if ustep % 6 == 2:
            emit_tail_step(tstep[0])
            tstep[0] += 1
    while tstep[0] <= NTOT:
        emit_tail_step(tstep[0])
        tstep[0] += 1

    pe_up_done = pe_up[2 * NI - 1]
    ld_upspill_done = [tot("lu0"), tot("lu1")]

    def upmax_v(v):
        wt(v, "va", va_mx[UPS - 1])
        v.reduce_max(RMX[:, 0:1], UMAXR[:, :], AX.X)
        v.reduce_max(RMX[:, 2:3], UMINR[:, :], AX.X,
                     apply_absolute_value=True)
        v.tensor_tensor(RMX[:, 0:1], RMX[:, 0:1], RMX[:, 2:3], ALU.max)
        v.reduce_max(RMX[:, 1:2], GMAXR[:, :], AX.X)
        v.reduce_max(RMX[:, 2:3], GMINR[:, :], AX.X,
                     apply_absolute_value=True)
        mark(v.tensor_tensor(RMX[:, 1:2], RMX[:, 1:2], RMX[:, 2:3], ALU.max),
             "va")
    on("vector", upmax_v)
    va_upmax = cnt["va"]
    va_gmax = va_upmax

    va_tail_done = cnt["va"]
    ac_tail_done = cnt["ac"]
    pe_op_done = cnt["pe"]
    ld_aspill_done = [tot("la0"), tot("la1")]

    for cm in (cmOPCH, cmGMINR, cmGMAXR, cmUMINR, cmUMAXR, cmRCHTB, cmPCHTB,
               cmWOT, cmECH, cmRCHT, cmPCHT, cmRWQ, cmSR, cmDB, cmNB):
        cm.__exit__(None, None, None)

    ATCH, _ = sb("ATCH", [P, 2, QT], BF16)
    FS, cmFS = sb("FS", [P, NI, TOK], BF16)
    XCH, cmXCH = sb("XCH", [P, 2, QT], F32)
    OCH, cmOCH = sb("OCH", [P, 2, QT], F32)
    SCF, cmSCF = sb("SCF", [P, 2, TOK], F32, side="right")
    GCH, cmGCH = sb("GCH", [P, 2, TOK], BF16, side="right")
    SGCH, cmSGCH = sb("SGCH", [P, TOK], BF16, side="right")
    R2CH, cmR2CH = sb("R2CH", [P, TOK], BF16, side="right")
    UCH, cmUCH = sb("UCH", [P, 2, TOK], BF16, side="right")
    GIN, cmGIN = sb("GIN", [P, 2, TOK], BF16, side="right")

    def c2_g(g):
        wt(g, "va", max(va_gmax, va_upmax))
        dma(g, "gmi", lambda: g.dma_start(
            out=red2_d[0:2, :].rearrange("a p -> p a"),
            in_=RMX[:, 0:2], allow_slow_non_contiguous=True))
        dma(g, "gmi", lambda: g.dma_start(out=RROW[0:1, 0:P],
                                          in_=red2_d[0:1, :]))
        dma(g, "gmi", lambda: g.dma_start(out=RROW[0:1, P:2 * P],
                                          in_=red2_d[1:2, :]))
    on("gpsimd", c2_g)
    gd_c2a = tot("gmi")

    def c2_v(v):
        wtt(v, gd_c2a)
        v.reduce_max(CCS[0:1, 0:1], RROW[0:1, 0:P], AX.X,
                     apply_absolute_value=True)
        mark(v.reduce_max(CCS[0:1, 1:2], RROW[0:1, P:2 * P], AX.X,
                          apply_absolute_value=True), "va")
    on("vector", c2_v)
    va_c2b = cnt["va"]

    def c2_g3(g):
        wt(g, "va", va_c2b)
        wtt(g, dma(g, "gmi", lambda: g.dma_start(out=mxi[1].ap(),
                                                 in_=CCS[:, :])))
        dma(g, "cc", lambda: g.collective_compute(
            "AllReduce", ALU.max, replica_groups=[list(range(NCORES))],
            ins=[mxi[1].ap().opt()], outs=[mxo[1].ap().opt()]), inc=1)
        wtt(g, tot("cc"))
        dma(g, "gmi", lambda: g.dma_start(out=SCAL[0:1, 4:6],
                                          in_=mxo[1].ap()[0:1, 0:2]))
    on("gpsimd", c2_g3)
    gd_c2 = tot("gmi")

    def c2_v3(v):
        wtt(v, gd_c2)
        v.memset(RMX[:, 1:2], 0.0)
        v.tensor_scalar(SCAL[0:1, 6:7], SCAL[0:1, 4:5], 1.0 / QMAX, EPS_Q,
                        ALU.mult, ALU.max)
        v.reciprocal(SCAL[0:1, 7:8], SCAL[0:1, 6:7])
        v.tensor_scalar(SCAL[0:1, 8:9], SCAL[0:1, 5:6], 1.0 / QMAX, EPS_Q,
                        ALU.mult, ALU.max)
        mark(v.reciprocal(SCAL[0:1, 9:10], SCAL[0:1, 8:9]), "va")
    on("vector", c2_v3)
    va_c2s = cnt["va"]

    def c2_pe(t):
        wt(t, "va", va_c2s)
        wt(t, "va", va_asel)
        mark(t.matmul(PBC[:, 0:4], ONES32[:, :], SCAL[0:1, 6:10],
                      start=True, stop=True), "pe")
    on("tensor", c2_pe)
    pe_c2 = cnt["pe"]

    def c2_v4(v):
        wt(v, "pe", pe_c2)
        v.tensor_copy(BCAST[:, 3:4], PBC[:, 0:1])
        v.tensor_copy(BCAST[:, 2:3], PBC[:, 1:2])
        v.tensor_copy(BCAST[:, 5:6], PBC[:, 2:3])
        v.tensor_copy(BCAST[:, 4:5], PBC[:, 3:4])
        v.tensor_scalar(BCAST[:, 10:11], BCAST[:, 5:6], -MAGIC, None, ALU.mult)
        mark(v.tensor_scalar(BCAST[:, 11:12], BCAST[:, 3:4], -MAGIC, None,
                             ALU.mult), "va")
    on("vector", c2_v4)
    va_c2bc = cnt["va"]

    sd_wqd = {}
    pe_dn_half = [0]

    gv2 = {}
    po_gqp = {}
    gin_ld = {}
    for i in range(NI):
        first = (i == 0)
        sl = i % 2
        gsl = i % 2

        def gin_s(s, i=i, gsl=gsl, wfree=po_gqp.get(i - 2, 0)):
            wt(s, "po", wfree)
            if i < 2:
                wtt(s, ld_upspill_done)
            wtt(s, gsp_done[i])
            gin_ld[i] = dma(s, f"lc{2 + i % 2}", lambda: s.dma_start(
                out=GIN[:, gsl, :], in_=gspill_d[i * P:(i + 1) * P, :]))
        on("sync", gin_s)

        def gq_p(p, i=i, sl=sl, gsl=gsl, first=first,
                 wprev=gv2.get(i - 2, 0)):
            if first:
                wt(p, "va", va_c2bc)
                wt(p, "va", va_tail_done)
                wt(p, "pe", pe_op_done)
                wtt(p, ld_upspill_done)
                wtt(p, ld_aspill_done)
            wtt(p, gin_ld[i])
            wt(p, "va", wprev)
            p.tensor_scalar(SCF[:, sl, :], GIN[:, gsl, :], BCAST[:, 4:5],
                            MAGIC, ALU.mult, ALU.add)
        on("gpsimd", gq_p)
        po_gqp[i] = cnt["po"]
        po_tg = cnt["po"]

        def gq_a(a, i=i, sl=sl, gch=i % 2, po_tg=po_tg,
                 wprev=gv2.get(i - 2, 0)):
            wt(a, "po", po_tg)
            wt(a, "va", wprev)
            a.activation(GCH[:, gch, :], SCF[:, sl, :], AF.Identity,
                         bias=BCAST[:, 10:11], scale=BCAST[:, 5:6])
            a.activation(SGCH[:, :], GCH[:, gch, :], AF.Sigmoid)
            a.activation(SCF[:, sl, :], SGCH[:, :], AF.Identity,
                         bias=MAGT[:, 0:1], scale=U16S)
        on("scalar", gq_a)
        ac_sg = cnt["ac"]

        def gq_v2(v, i=i, sl=sl, gch=i % 2, ac_sg=ac_sg):
            wt(v, "ac", ac_sg)
            v.tensor_scalar(R2CH[:, :], SCF[:, sl, :], MAGIC + U16S - 1.0,
                            -MAGIC, ALU.min, ALU.add)
            v.tensor_tensor(FS[:, i, :], GCH[:, gch, :], R2CH[:, :],
                            ALU.mult)
            v.reduce_max(RMX[:, 2:3], FS[:, i, :], AX.X,
                         apply_absolute_value=True)
            v.tensor_tensor(RMX[:, 3:4], RMX[:, 3:4], RMX[:, 2:3], ALU.max)
        on("vector", gq_v2)
        gv2[i] = cnt["va"]

    va_g2max = cnt["va"]

    def c3_g(g):
        wt(g, "va", va_g2max)
        dma(g, "gmi", lambda: g.dma_start(
            out=red_d[0:1, :].rearrange("a p -> p a"),
            in_=RMX[:, 3:4], allow_slow_non_contiguous=True))
        dma(g, "gmi", lambda: g.dma_start(out=RROW[0:1, 0:P],
                                          in_=red_d[:, :]))
    on("gpsimd", c3_g)
    gd_c3a = tot("gmi")

    def c3_v(v):
        wtt(v, gd_c3a)
        mark(v.reduce_max(CCS[0:1, 0:1], RROW[0:1, 0:P], AX.X,
                          apply_absolute_value=True), "va")
    on("vector", c3_v)
    va_c3a = cnt["va"]

    def c3_g2(g):
        wt(g, "va", va_c3a)
        wtt(g, dma(g, "gmi", lambda: g.dma_start(out=mxi[2].ap(),
                                                 in_=CCS[:, :])))
        dma(g, "cc", lambda: g.collective_compute(
            "AllReduce", ALU.max, replica_groups=[list(range(NCORES))],
            ins=[mxi[2].ap().opt()], outs=[mxo[2].ap().opt()]), inc=1)
        wtt(g, tot("cc"))
        dma(g, "gmi", lambda: g.dma_start(out=SCAL[0:1, 10:11],
                                          in_=mxo[2].ap()[0:1, 0:1]))
    on("gpsimd", c3_g2)
    gd_c3 = tot("gmi")

    def c3_v2(v):
        wtt(v, gd_c3)
        v.tensor_scalar(SCAL[0:1, 11:12], SCAL[0:1, 10:11],
                        1.0 / (U16S * QMAX), EPS_Q, ALU.mult, ALU.max)
        v.reciprocal(SCAL[0:1, 12:13], SCAL[0:1, 11:12])
        mark(v.tensor_scalar(SCAL[0:1, 13:14], SCAL[0:1, 12:13], 1.0 / U16S,
                             None, ALU.mult), "va")
    on("vector", c3_v2)
    va_c3s = cnt["va"]

    def c3_pe(t):
        wt(t, "va", va_c3s)
        wt(t, "va", va_c2bc)
        mark(t.matmul(PBC[:, 0:3], ONES32[:, :], SCAL[0:1, 11:14],
                      start=True, stop=True), "pe")
    on("tensor", c3_pe)
    pe_c3 = cnt["pe"]

    def c3_v3(v):
        wt(v, "pe", pe_c3)
        v.tensor_copy(BCAST[:, 7:8], PBC[:, 0:1])
        v.tensor_copy(BCAST[:, 6:7], PBC[:, 2:3])
        mark(v.tensor_scalar(BCAST[:, 12:13], BCAST[:, 7:8], -MAGIC, None,
                             ALU.mult), "va")
    on("vector", c3_v3)
    va_c3bc = cnt["va"]

    ov2 = {}
    og_ac2 = {}
    uch_ld = {}
    for i in range(NI):
        first = (i == 0)
        sl = i % 2

        def og_p(p, i=i, sl=sl, first=first, wprev=og_ac2.get(i - 2, 0)):
            if first:
                wt(p, "va", va_c3bc)
            wt(p, "ac", wprev)
            p.tensor_scalar(SCF[:, sl, :], FS[:, i, :], BCAST[:, 6:7], MAGIC,
                            ALU.mult, ALU.add)
        on("gpsimd", og_p)
        po_tg2 = cnt["po"]

        def og_a(a, i=i, sl=sl, po_tg2=po_tg2):
            wt(a, "po", po_tg2)
            a.activation(FS[:, i, :], SCF[:, sl, :], AF.Identity,
                         bias=BCAST[:, 12:13], scale=BCAST[:, 7:8])
        on("scalar", og_a)
        ac_g2w = cnt["ac"]

        usl = i % 2

        def og_s(s, i=i, usl=usl, wfree=ov2.get(i - 2, 0)):
            wt(s, "va", wfree)
            if i == 0:
                wtt(s, ld_upspill_done)
            uch_ld[i] = dma(s, f"lc{i % 2}", lambda: s.dma_start(
                out=UCH[:, usl, :], in_=upspill_d[i * P:(i + 1) * P, :]))
        on("sync", og_s)

        def og_a2(a, i=i, sl=sl, usl=usl):
            wtt(a, uch_ld[i])
            a.activation(SCF[:, sl, :], UCH[:, usl, :], AF.Identity,
                         bias=MAGT[:, 0:1], scale=BCAST[:, 2:3])
            a.activation(UCH[:, usl, :], SCF[:, sl, :], AF.Identity,
                         bias=BCAST[:, 11:12], scale=BCAST[:, 3:4])
        on("scalar", og_a2)
        og_ac2[i] = cnt["ac"]

        def og_v2(v, i=i, usl=usl, ac_u=og_ac2[i]):
            wt(v, "ac", ac_u)
            v.tensor_tensor(FS[:, i, :], FS[:, i, :], UCH[:, usl, :],
                            ALU.mult)
            v.reduce_max(RMX[:, 2:3], FS[:, i, :], AX.X,
                         apply_absolute_value=True)
            v.tensor_tensor(RMX[:, 1:2], RMX[:, 1:2], RMX[:, 2:3], ALU.max)
        on("vector", og_v2)
        ov2[i] = cnt["va"]

    va_omax = cnt["va"]
    ac_og_done = cnt["ac"]

    for cm in (cmGIN, cmUCH, cmR2CH, cmSGCH, cmGCH):
        cm.__exit__(None, None, None)
    WQDTH, cmWQDTH = sb("WQDTH", [P, NI, QT], BF16, side="right")

    def dn_xp0(s):
        wtt(s, gcc["d"])
        wt(s, "va", va_omax)
        wt(s, "ac", ac_og_done)
        sd_wqd[0] = dma(
            s, "sdd",
            lambda: s.dma_start_transpose(out=WQDTH[:, :, :],
                                          in_=wqd_d[0:QT, :]))
    on("sync", dn_xp0)

    def c4_g(g):
        wt(g, "va", va_omax)
        dma(g, "gmi", lambda: g.dma_start(
            out=red_d[0:1, :].rearrange("a p -> p a"),
            in_=RMX[:, 1:2], allow_slow_non_contiguous=True))
        dma(g, "gmi", lambda: g.dma_start(out=RROW[0:1, 0:P],
                                          in_=red_d[:, :]))
    on("gpsimd", c4_g)
    gd_c4a = tot("gmi")

    def c4_v(v):
        wtt(v, gd_c4a)
        mark(v.reduce_max(CCS[0:1, 0:1], RROW[0:1, 0:P], AX.X,
                          apply_absolute_value=True), "va")
    on("vector", c4_v)
    va_c4a = cnt["va"]

    def c4_g2(g):
        wt(g, "va", va_c4a)
        wtt(g, dma(g, "gmi", lambda: g.dma_start(out=mxi[0].ap(),
                                                 in_=CCS[:, :])))
        dma(g, "cc", lambda: g.collective_compute(
            "AllReduce", ALU.max, replica_groups=[list(range(NCORES))],
            ins=[mxi[0].ap().opt()], outs=[mxo[0].ap().opt()]), inc=1)
        wtt(g, tot("cc"))
        dma(g, "gmi", lambda: g.dma_start(out=SCAL[0:1, 14:15],
                                          in_=mxo[0].ap()[0:1, 0:1]))
    on("gpsimd", c4_g2)
    gd_c4 = tot("gmi")

    def c4_v2(v):
        wtt(v, gd_c4)
        v.tensor_scalar(SCAL[0:1, 15:16], SCAL[0:1, 14:15], 1.0 / QMAX, EPS_Q,
                        ALU.mult, ALU.max)
        mark(v.reciprocal(SCAL[0:1, 3:4], SCAL[0:1, 15:16]), "va")
    on("vector", c4_v2)
    va_c4s = cnt["va"]

    def c4_pe(t):
        wt(t, "va", va_c4s)
        wt(t, "va", va_c3bc)
        t.matmul(PBC[:, 0:1], ONES32[:, :], SCAL[0:1, 15:16],
                 start=True, stop=True)
        mark(t.matmul(PBC[:, 1:2], ONES32[:, :], SCAL[0:1, 3:4],
                      start=True, stop=True), "pe")
    on("tensor", c4_pe)
    pe_c4 = cnt["pe"]

    def c4_v3(v):
        wt(v, "pe", pe_c4)
        v.tensor_copy(BCAST[:, 9:10], PBC[:, 0:1])
        v.tensor_copy(BCAST[:, 8:9], PBC[:, 1:2])
        mark(v.tensor_scalar(BCAST[:, 13:14], BCAST[:, 9:10], -MAGIC, None,
                             ALU.mult), "va")
    on("vector", c4_v3)
    va_c4bc = cnt["va"]

    oqa = {}
    for i in range(NI):
        first = (i == 0)
        sl = i % 2

        def oq_p(p, i=i, sl=sl, first=first, wprev=oqa.get(i - 2, 0)):
            if first:
                wt(p, "va", va_c4bc)
            wt(p, "ac", wprev)
            p.tensor_scalar(SCF[:, sl, :], FS[:, i, :], BCAST[:, 8:9], MAGIC,
                            ALU.mult, ALU.add)
        on("gpsimd", oq_p)
        po_to = cnt["po"]

        def oq_a(a, i=i, sl=sl, po_to=po_to):
            wt(a, "po", po_to)
            a.activation(FS[:, i, :], SCF[:, sl, :], AF.Identity,
                         bias=BCAST[:, 13:14], scale=BCAST[:, 9:10])
        on("scalar", oq_a)
        oqa[i] = cnt["ac"]

    ac_o_done = cnt["ac"]


    ld_final = [0]

    for hh in range(2):
        if hh == 1:
            def dn_xp(s, hh=hh):
                wt(s, "pe", pe_dn_half[0])
                sd_wqd[hh] = dma(
                    s, "sdd",
                    lambda: s.dma_start_transpose(
                        out=WQDTH[:, :, :],
                        in_=wqd_d[hh * QT:(hh + 1) * QT, :]))
            on("sync", dn_xp)

        for k in range(NT128):
            bq = k % 2
            csl = k % 2

            def ep_ld(s, hh=hh, k=k, csl=csl, step=hh * NT128 + k):
                wt(s, "va", xch_free.get(step - 2, 0))
                dma(s, f"le{csl}", lambda: s.dma_start(
                    out=XCH[:, csl, :], in_=xt[k, :, hh * QT:(hh + 1) * QT]))
                if step == 0:
                    wtt(s, ld_aspill_done)
                dma(s, f"lr{csl}", lambda: s.dma_start(
                    out=ATCH[:, csl, :],
                    in_=aspill_d[k * P:(k + 1) * P, hh * QT:(hh + 1) * QT]))
            on("sync", ep_ld)
            ld_ep = [tot(f"le{csl}"), tot(f"lr{csl}")]

            def dn_mm(t, hh=hh, k=k, bq=bq, fb=ep_bank_free[bq]):
                wtt(t, sd_wqd[hh])
                wt(t, *fb)
                for i in range(NI):
                    if hh == 0 and k < 2:
                        wt(t, "ac", oqa[i])
                    elif i == 0:
                        wt(t, "ac", ac_o_done)
                    ins = t.matmul(PEP[bq][:, :], FS[:, i, 128 * k:128 * (k + 1)],
                             WQDTH[:, i, :], start=(i == 0), stop=(i == NI - 1))
                    if i == NI - 1:
                        mark(ins, "pe")
            on("tensor", dn_mm)
            pe_dn = cnt["pe"]
            if k == NT128 - 1:
                pe_dn_half[0] = pe_dn

            def ep_v(v, hh=hh, k=k, bq=bq, csl=csl, pe_dn=pe_dn, ld_ep=ld_ep,
                     wfree=och_free.get(0, 0)):
                wt(v, "pe", pe_dn)
                wtt(v, ld_ep)
                wtt(v, och_free.get(hh * NT128 + k - 2, 0))
                v.tensor_tensor(SC[:, 0, 0:QT], PEP[bq][:, :], ATCH[:, csl, :],
                                ALU.add)
                mark(v.tensor_tensor(OCH[:, csl, :], SC[:, 0, 0:QT],
                                     XCH[:, csl, :], ALU.add), "va")
            on("vector", ep_v)
            va_ep = cnt["va"]
            xch_free[hh * NT128 + k] = va_ep
            ep_bank_free[bq] = ("va", va_ep)

            def ep_st(s, hh=hh, k=k, csl=csl, va_ep=va_ep):
                wt(s, "va", va_ep)
                m = dma(s, f"lo{csl}", lambda: s.dma_start(
                    out=out_d[k * P:(k + 1) * P, hh * QT:(hh + 1) * QT],
                    in_=OCH[:, csl, :]))
                och_free[hh * NT128 + k] = m
                ld_final[0] = m
            on("sync", ep_st)

    def fin_g(g):
        wtt(g, tot("lo0"))
        wtt(g, tot("lo1"))
        g.engine_nop()
    on("gpsimd", fin_g)

    flush_streams()
    return nc


def kernel(**inputs):
    x = np.asarray(inputs["L0_attn_x"])
    td = np.asarray(inputs["L0_attn_time_decay"])
    tf = np.asarray(inputs["L0_attn_time_first"])
    tmk = np.asarray(inputs["L0_attn_time_mix_k"])
    tmv = np.asarray(inputs["L0_attn_time_mix_v"])
    tmr = np.asarray(inputs["L0_attn_time_mix_r"])
    Wk = np.asarray(inputs["L0_attn_Wk"])
    Wv = np.asarray(inputs["L0_attn_Wv"])
    Wr = np.asarray(inputs["L0_attn_Wr"])
    Wo = np.asarray(inputs["L0_attn_Wo"])
    wg = np.asarray(inputs["L0_ffn_w_gate"])
    wu = np.asarray(inputs["L0_ffn_w_up"])
    wd = np.asarray(inputs["L0_ffn_w_down"])

    def v128(v):
        return np.ascontiguousarray(v.reshape(NCH, P).T)

    common = dict(
        td=v128(td), tf=v128(tf), tmk=v128(tmk), tmv=v128(tmv), tmr=v128(tmr),
        wkT=np.ascontiguousarray(Wk.T), wvT=np.ascontiguousarray(Wv.T),
        wrT=np.ascontiguousarray(Wr.T), woT=np.ascontiguousarray(Wo.T))

    in_maps = []
    for core in range(NCORES):
        b, half = divmod(core, 2)
        s0 = half * TOK
        xm = np.ascontiguousarray(x[b, s0:s0 + TOK])
        if s0 == 0:
            xh = np.zeros((1, C), np.float32)
        else:
            xh = np.ascontiguousarray(x[b, s0 - 1:s0])
        selv = np.zeros((1, NCORES), np.float32)
        if half == 1:
            selv[0, core - 1] = 1.0
        m = dict(common)
        m.update(x=xm, xhalo=xh, sel=selv,
                 wupp=np.ascontiguousarray(wu[core * 3 * P:(core + 1) * 3 * P]),
                 wgatep=np.ascontiguousarray(
                     wg[core * 3 * P:(core + 1) * 3 * P]),
                 wdownp=np.ascontiguousarray(wd[core * P:(core + 1) * P]))
        in_maps.append(m)

    nc = build_nc()
    res = run_bass_kernel_spmd(nc, in_maps, core_ids=list(range(NCORES)))
    out = np.empty((4, 4096, C), np.float32)
    for core in range(NCORES):
        b, half = divmod(core, 2)
        s0 = half * TOK
        out[b, s0:s0 + TOK] = np.asarray(res.results[core]["out"])
    return out


# version 40
# speedup vs baseline: 1.0011x; 1.0011x over previous
import numpy as np
import ml_dtypes
import concourse.bass as bass
import concourse.mybir as mybir
from concourse.bass_utils import run_bass_kernel_spmd

F32 = mybir.dt.float32
BF16 = mybir.dt.bfloat16
AF = mybir.ActivationFunctionType
ALU = mybir.AluOpType
AX = mybir.AxisListType

P = 128
NCH = 8
NI = 24
TOK = 2048
NQ = 4
QT = 512
NT128 = 16
C = 1024
DI = 3072
NCORES = 8
XCOL0 = 32
XST = XCOL0 + TOK
MAGIC = 1.5 * 2.0 ** 23
QMAX = 32767.0
U16S = 65536.0
EPS_LN = 1e-5
EPS_Q = 1e-12


def build_nc():
    nc = bass.Bass("TRN2", target_bir_lowering=False)
    dt = nc.dram_tensor

    x_d = dt("x", [TOK, C], F32, kind="ExternalInput").ap()
    xh_d = dt("xhalo", [1, C], F32, kind="ExternalInput").ap()
    sel_d = dt("sel", [1, NCORES], F32, kind="ExternalInput").ap()
    td_d = dt("td", [P, NCH], F32, kind="ExternalInput").ap()
    tf_d = dt("tf", [P, NCH], F32, kind="ExternalInput").ap()
    tmk_d = dt("tmk", [P, NCH], F32, kind="ExternalInput").ap()
    tmv_d = dt("tmv", [P, NCH], F32, kind="ExternalInput").ap()
    tmr_d = dt("tmr", [P, NCH], F32, kind="ExternalInput").ap()
    wkT_d = dt("wkT", [C, C], F32, kind="ExternalInput").ap()
    wvT_d = dt("wvT", [C, C], F32, kind="ExternalInput").ap()
    wrT_d = dt("wrT", [C, C], F32, kind="ExternalInput").ap()
    woT_d = dt("woT", [C, C], F32, kind="ExternalInput").ap()
    wup_d = dt("wupp", [3 * P, C], F32, kind="ExternalInput").ap()
    wgate_d = dt("wgatep", [3 * P, C], F32, kind="ExternalInput").ap()
    wdown_d = dt("wdownp", [P, DI], F32, kind="ExternalInput").ap()
    out_d = dt("out", [TOK, C], F32, kind="ExternalOutput").ap()

    wqu_t = dt("wqu", [DI, C], BF16)
    wqg_t = dt("wqg", [DI, C], BF16)
    wqd_t = dt("wqd", [C, DI], BF16)
    wqu_d, wqg_d, wqd_d = wqu_t.ap(), wqg_t.ap(), wqd_t.ap()
    dqu_t = dt("dqu", [3 * P, C], BF16)
    dqg_t = dt("dqg", [3 * P, C], BF16)
    dqd_t = dt("dqd", [P, DI], BF16)
    upspill_d = dt("upspill", [DI, TOK], BF16).ap()
    gspill_d = dt("gspill", [DI, TOK], BF16).ap()
    aspill_d = dt("aspill", [TOK, C], BF16).ap()
    halo_d = dt("halob", [1, C], BF16).ap()
    red_d = dt("redb", [1, P], F32).ap()
    mxi = [dt(f"mxi{i}", [1, 16], F32) for i in range(3)]
    mxo = [dt(f"mxo{i}", [1, 16], F32) for i in range(3)]
    agi_t = dt("agi", [P, 16], F32)
    ago_t = dt("ago", [NCORES * P, 16], F32)

    sem = {}
    for n in ("va", "ac", "po", "pe", "uev", "umx"):
        sem[n] = nc.semaphore(n).__enter__()
    cnt = {n: 0 for n in sem}
    DFAMS = ("lp", "lx0", "lx1", "lx2", "lx3", "lx4", "lx5", "lx6", "lx7",
             "ldq0", "ldq1", "ldq2", "ldq3", "lwq0", "lwq1",
             "lu0", "lu1", "lc0", "lc1", "lc2", "lc3", "lc4", "lc5", "lc6", "lc7",
             "la0", "la1", "le0", "le1", "lr0", "lr1", "lo0", "lo1",
             "sdx", "sdu", "sdg", "sdd",
             "gwk0", "gwk1", "gwk2", "gwk3", "gwv0", "gwv1", "gwv2", "gwv3",
             "gwr0", "gwr1", "gwr2", "gwr3", "gwo0", "gwo1", "gwo2", "gwo3",
             "gmi", "cc", "ccu", "ccg", "ccd")
    for n in DFAMS:
        sem[n] = nc.semaphore(n).__enter__()
        cnt[n] = 0

    def mark(ins, s, n=1):
        if s in ("va", "ac", "po"):
            return cnt[s]
        cnt[s] += n
        ins.then_inc(sem[s], n)
        return cnt[s]

    def wt(eng, s, v):
        if v > 0:
            eng.wait_ge(sem[s], v)

    def wtt(eng, mk):
        if not mk:
            return
        if isinstance(mk, list):
            for m in mk:
                wtt(eng, m)
            return
        eng.wait_ge(sem[mk[0]], mk[1])

    def dma(eng, fam, builder, inc=16):
        if cnt[fam]:
            eng.wait_ge(sem[fam], cnt[fam])
        ins = builder()
        cnt[fam] += inc
        ins.then_inc(sem[fam], inc)
        return (fam, cnt[fam])

    def tot(fam):
        return (fam, cnt[fam])

    _RAW_METHODS = {"wait_ge", "nop", "print", "load_library", "sem_inc"}
    _DMA_METHODS = {"dma_start", "dma_start_transpose", "indirect_dma_start",
                    "collective_compute"}

    class RecIns:
        __slots__ = ("hooks",)

        def __init__(self):
            self.hooks = []

        def then_inc(self, semh, n=1):
            self.hooks.append(("then_inc", semh, n))
            return self

        def _wait_ge(self, semh, v):
            self.hooks.append(("_wait_ge", semh, v))
            return self

    class Rec:
        def __init__(self, ops, fam):
            self._ops = ops
            self._fam = fam

        def __getattr__(self, name):
            fam = self._fam

            def record(*a, **k):
                ri = RecIns()
                if (fam is not None and name not in _RAW_METHODS
                        and name not in _DMA_METHODS):
                    prev = cnt[fam]
                    if prev:
                        ri.hooks.append(("_wait_ge", sem[fam], prev))
                    cnt[fam] += 1
                    ri.hooks.append(("then_inc", sem[fam], 1))
                self._ops.append((name, a, k, ri))
                return ri
            return record

    _streams = {n: [] for n in ("vector", "scalar", "gpsimd", "tensor", "sync")}
    _FAM_OF2 = {"vector": "va", "scalar": "ac", "gpsimd": "po",
                "tensor": None, "sync": None}

    def flush_streams():
        for engname, ops in _streams.items():
            if not ops:
                continue

            def body(e, ops=ops):
                for name, a, k, ri in ops:
                    ins = getattr(e, name)(*a, **k)
                    for h in ri.hooks:
                        getattr(ins, h[0])(h[1], h[2])
            getattr(blk, engname)(body)

    def sb(name, shape, dtyp, side=None):
        cm = nc.sbuf_tensor(name, shape, dtyp, side=side)
        return cm.__enter__(), cm

    def psu(name, shape):
        cm = nc.psum_tensor(name, shape, F32)
        return cm.__enter__(), cm

    blk = nc.Block().__enter__()

    def on(engine, fn):
        fn(Rec(_streams[engine], _FAM_OF2[engine]))

    XBUF, cmXBUF = sb("XBUF", [P, 2, C], F32, side="right")
    DQ2, cmDQ2 = sb("DQ2", [P, 2, C // 2], F32, side="right")
    ZSC, cmZSC = sb("ZSC", [P, C // 2], F32, side="right")
    DQBLK, cmDQB = sb("DQBLK", [P, 32], F32, side="right")
    DQBLK2, cmDQB2 = sb("DQBLK2", [P, 32], F32, side="right")
    WQP, cmWQP = sb("WQP", [P, C // 2], BF16, side="right")
    HALO, cmHALO = sb("HALO", [1, C], F32, side="right")
    XNLY, cmXNLY = sb("XNLY", [P, C], BF16)
    SC, _ = sb("SCR", [P, 2, QT], F32)
    ONES32, _ = sb("ONES32", [1, P], F32)
    MAGT, _ = sb("MAGT", [P, 1], F32)
    NMAGT, _ = sb("NMAGT", [P, 1], F32)
    EPST, _ = sb("EPST", [P, 1], F32)
    TD, _ = sb("TD", [P, NCH], F32)
    TF, _ = sb("TF", [P, NCH], F32)
    TMK, _ = sb("TMK", [P, NCH], F32)
    TMV, _ = sb("TMV", [P, NCH], F32)
    TMR, _ = sb("TMR", [P, NCH], F32)
    LNW, _ = sb("LNW", [P, NCH], F32)
    EW, _ = sb("EW", [P, NCH], F32)
    EU, _ = sb("EU", [P, NCH], F32)
    LNWO, _ = sb("LNWO", [P, NCH, NQ], F32)
    STAT, _ = sb("STAT", [P, 2, 8], F32)
    XMX, _ = sb("XMX", [P, 1], F32)
    RMX, _ = sb("RMX", [P, 4], F32)
    RROW, _ = sb("RROW", [1, P], F32)
    CCS, _ = sb("CCS", [1, 16], F32)
    SCAL, _ = sb("SCAL", [1, 16], F32)
    BCAST, _ = sb("BCAST", [P, 16], F32)
    CARA, _ = sb("CARA", [P, NCH], F32)
    CARB, _ = sb("CARB", [P, NCH], F32)
    STOUT, _ = sb("STOUT", [P, 16], F32)
    GST, _ = sb("GST", [P, NCORES, 16], F32)
    ASEL, _ = sb("ASEL", [P, 16], F32)
    SELB, _ = sb("SELB", [P, NCORES], F32)
    HSTA, _ = sb("HSTA", [1, 8], F32)
    HNL, _ = sb("HNL", [1, C], BF16)
    WTT, _ = sb("WTT", [P, 2, NCH, P], BF16)

    PB = []
    for i in range(4):
        h, _ = psu(f"PB{i}", [P, QT])
        PB.append(h)
    PEP = []
    for i in range(2):
        h, _ = psu(f"PEP{i}", [P, QT])
        PEP.append(h)
    PBC, _ = psu("PBC", [P, QT])
    PIOT, _ = psu("PIOT", [P, QT])
    DUMT, _ = sb("DUMT", [1, 2], F32)


    ld_x = {}
    ac_xn = {}
    sd_xp = {}
    ld_w = {}
    ac_wcast = {}
    dq_ld = {}
    dq_done = {}
    dq_free = {}
    po_z = {}
    dq_state = {"idx": 0, "u": 0, "g": 0, "d": 0}
    bank_free = [("va", 0)] * 4
    bank_state = {"i": 0}
    ep_bank_free = [("va", 0), ("va", 0)]
    ech_spill = {0: 0, 1: 0}
    atch_spill = {0: 0, 1: 0}
    uch_free = {0: 0, 1: 0}
    xch_free = {0: 0, 1: 0}
    och_free = {0: 0, 1: 0}
    atl_free = {0: 0, 1: 0}
    pe_up = {}
    va_nmr_prev = [0]

    def next_bank():
        b = bank_state["i"] % 4
        bank_state["i"] += 1
        return b

    def st0_sync(s):
        for dsttile, src in ((TD, td_d), (TF, tf_d), (TMK, tmk_d),
                             (TMV, tmv_d), (TMR, tmr_d)):
            dma(s, "lp", lambda d=dsttile, r=src: s.dma_start(out=d[:, :], in_=r[:, :]))
        dma(s, "lp", lambda: s.dma_start(out=HALO[:, :], in_=xh_d[:, :]))
        dma(s, "lp", lambda: s.dma_start(out=SELB[0:1, :], in_=sel_d[:, :]))
    on("sync", st0_sync)
    ld_params = tot("lp")

    def st0_vec(v):
        v.memset(DUMT[:, :], 1.0)
        v.memset(ONES32[:, :], 1.0)
        v.memset(MAGT[:, :], MAGIC)
        v.memset(NMAGT[:, :], -MAGIC)
        v.memset(EPST[:, :], EPS_LN)
        v.memset(ASEL[:, :], 0.0)
        v.memset(XMX[:, :], 0.0)
        v.memset(RMX[:, :], 0.0)
        mark(v.memset(CCS[:, :], 0.0), "va")
    on("vector", st0_vec)
    va_const = cnt["va"]

    def st0_act(s):
        wtt(s, ld_params)
        mark(s.activation(LNW[:, :], TD[:, :], AF.Exp), "ac")
    on("scalar", st0_act)
    ac_l0 = cnt["ac"]

    def st0_vec3(v):
        wt(v, "ac", ac_l0)
        mark(v.tensor_scalar(LNW[:, :], LNW[:, :], -1.0, None, ALU.mult), "va")
    on("vector", st0_vec3)
    va_lnw = cnt["va"]

    def st0_act2(s):
        wt(s, "va", va_lnw)
        s.activation(EW[:, :], LNW[:, :], AF.Exp)
        mark(s.activation(EU[:, :], TF[:, :], AF.Exp), "ac")
    on("scalar", st0_act2)
    ac_ewu = cnt["ac"]

    def st0_vec4(v):
        wt(v, "ac", ac_ewu)
        for q in range(NQ):
            v.tensor_scalar(LNWO[:, :, q], LNW[:, :], float(q * QT), None, ALU.mult)
        v.tensor_copy(CARA[:, :], ASEL[:, 0:NCH])
        mark(v.tensor_copy(CARB[:, :], ASEL[:, 0:NCH]), "va")
    on("vector", st0_vec4)
    va_init = cnt["va"]

    def halo_v(v):
        wtt(v, ld_params)
        mark(v.reduce_sum(HSTA[0:1, 0:1], HALO[0:1, :], AX.X), "va")
    on("vector", halo_v)
    va_hs = cnt["va"]

    def halo_v2(v):
        wt(v, "va", va_hs)
        v.scalar_tensor_tensor(SC[0:1, 1, 0:QT], HALO[0:1, 0:QT], 1.0,
                               HALO[0:1, 0:QT], ALU.bypass, ALU.mult,
                               accum_out=HSTA[0:1, 1:2])
        v.scalar_tensor_tensor(SC[0:1, 1, 0:QT], HALO[0:1, QT:C], 1.0,
                               HALO[0:1, QT:C], ALU.bypass, ALU.mult,
                               accum_out=HSTA[0:1, 6:7])
        v.tensor_tensor(HSTA[0:1, 1:2], HSTA[0:1, 1:2], HSTA[0:1, 6:7], ALU.add)
        v.tensor_scalar(HSTA[0:1, 2:3], HSTA[0:1, 0:1], 1.0 / C, None, ALU.mult)
        v.tensor_scalar(HSTA[0:1, 3:4], HSTA[0:1, 1:2], 1.0 / C, None, ALU.mult)
        v.tensor_tensor(HSTA[0:1, 4:5], HSTA[0:1, 2:3], HSTA[0:1, 2:3], ALU.mult)
        mark(v.tensor_tensor(HSTA[0:1, 5:6], HSTA[0:1, 3:4], HSTA[0:1, 4:5],
                             ALU.subtract), "va")
    on("vector", halo_v2)
    va_hv = cnt["va"]

    def halo_a2(a):
        wt(a, "va", va_hv)
        a.activation(HSTA[0:1, 6:7], HSTA[0:1, 5:6], AF.Ln, bias=EPST[0:1, 0:1])
        mark(a.activation(HSTA[0:1, 7:8], HSTA[0:1, 6:7], AF.Exp, scale=-0.5),
             "ac")
    on("scalar", halo_a2)
    ac_hr = cnt["ac"]

    def halo_v3(v):
        wt(v, "ac", ac_hr)
        mark(v.scalar_tensor_tensor(HSTA[0:1, 4:5], HSTA[0:1, 2:3], -1.0,
                                    HSTA[0:1, 7:8], ALU.mult, ALU.mult), "va")
    on("vector", halo_v3)
    va_hn = cnt["va"]

    def halo_v4(v):
        wt(v, "va", va_hn)
        v.scalar_tensor_tensor(HNL[0:1, :], HALO[0:1, :], HSTA[0:1, 7:8],
                               HSTA[0:1, 4:5].broadcast_to([1, C]),
                               ALU.mult, ALU.add)
    on("vector", halo_v4)
    ac_hnl = cnt["va"]

    cmHALO.__exit__(None, None, None)

    XNF, cmXNF = sb("XNF", [P, NCH, XST], BF16)
    NB, cmNB = sb("NB", [P, NCH, TOK], BF16)
    DB, cmDB = sb("DB", [P, NCH, TOK], BF16)
    DIFQ, cmDIFQ = sb("DIFQ", [P, NCH, QT], BF16)
    XMQ, cmXMQ = sb("XMQ", [P, NCH, QT], BF16)
    XM2Q, cmXM2Q = sb("XM2Q", [P, NCH, QT], BF16)
    WKT, cmWKT = sb("WKT", [P, NCH, C], BF16, side="right")
    WVT, cmWVT = sb("WVT", [P, NCH, C], BF16, side="right")
    EKQ, cmEKQ = sb("EKQ", [P, NCH, QT], BF16, side="right")
    EKVQ, cmEKVQ = sb("EKVQ", [P, NCH, QT], BF16, side="right")
    SA, cmSA = sb("SA", [P, 2, QT + 2], BF16, side="right")
    SB_, cmSB = sb("SB", [P, 2, QT + 2], BF16, side="right")

    xt = x_d.rearrange("(n p) c -> n p c", p=P)

    def load_xtile(k, wait_ac=0):
        def f(a, k=k, wait_ac=wait_ac):
            wt(a, "va", wait_ac)
            ld_x[k] = dma(a, f"lx{k % 2}",
                          lambda: a.dma_start(out=XBUF[:, k % 2, :],
                                              in_=xt[k, :, :]))
        on("scalar", f)

    def load_wtile(tag, src, j, dsttile, wait_pe=0, wait_va=0):
        def f(g, tag=tag, src=src, j=j, dsttile=dsttile, wait_pe=wait_pe,
              wait_va=wait_va):
            if wait_pe and j == 0:
                wt(g, "pe", wait_pe)
            if wait_va and j == 0:
                wt(g, "va", wait_va)
            ld_w[(tag, j)] = dma(
                g, "gw" + tag + str(j % 4),
                lambda: g.dma_start(out=dsttile[:, j, :],
                                    in_=src[j * P:(j + 1) * P, :]))
        on("gpsimd", f)

    gcc = {}

    def dequant_piece(tag):
        C2 = C // 2
        if tag == "u":
            r = dq_state["u"]; dq_state["u"] += 1
            src_full = wup_d[r * P:(r + 1) * P, :]
            dst_full = dqu_t.ap()[r * P:(r + 1) * P, :]
        elif tag == "g":
            r = dq_state["g"]; dq_state["g"] += 1
            src_full = wgate_d[r * P:(r + 1) * P, :]
            dst_full = dqg_t.ap()[r * P:(r + 1) * P, :]
        else:
            cp = dq_state["d"]; dq_state["d"] += 1
            src_full = wdown_d[:, cp * C:(cp + 1) * C]
            dst_full = dqd_t.ap()[:, cp * C:(cp + 1) * C]

        for h in range(2):
            i = dq_state["idx"]
            dq_state["idx"] += 1
            sl = i % 2
            src_ap = src_full[:, h * C2:(h + 1) * C2]
            dst_ap = dst_full[:, h * C2:(h + 1) * C2]

            def f(s, i=i, sl=sl, src_ap=src_ap):
                wt(s, "va", dq_free.get(i - 2, 0))
                dq_ld[i] = dma(s, f"ldq{i % 4}",
                               lambda: s.dma_start(out=DQ2[:, sl, :],
                                                   in_=src_ap))
            on("sync", f)

            def g(v, i=i, sl=sl):
                wtt(v, dq_ld[i])
                v.reduce_max(DQBLK[:, 0:16, None],
                             DQ2[:, sl, :].rearrange("p (b e) -> p b e", e=32),
                             AX.X, apply_absolute_value=True)
                v.tensor_scalar(DQBLK2[:, 0:16], DQBLK[:, 0:16], 1.0 / 7.0,
                                EPS_Q, ALU.mult, ALU.max)
                v.reciprocal(DQBLK[:, 0:16], DQBLK2[:, 0:16])
                v.tensor_tensor(ZSC[:, :].rearrange("p (b e) -> p b e", e=32),
                                DQ2[:, sl, :].rearrange("p (b e) -> p b e",
                                                        e=32),
                                DQBLK[:, 0:16, None].broadcast_to([P, 16, 32]),
                                ALU.mult)
                v.tensor_scalar(ZSC[:, :], ZSC[:, :], 1.0, MAGIC,
                                ALU.mult, ALU.add)
                wtt(v, dq_done.get(i - 1, 0))
                dq_free[i] = mark(
                    v.scalar_tensor_tensor(
                        WQP[:, :].rearrange("p (b e) -> p b e", e=32),
                        ZSC[:, :].rearrange("p (b e) -> p b e", e=32), -MAGIC,
                        DQBLK2[:, 0:16, None].broadcast_to([P, 16, 32]),
                        ALU.add, ALU.mult), "va")
            on("vector", g)

            def f2(s, i=i, sl=sl, dst_ap=dst_ap):
                wt(s, "va", dq_free[i])
                dq_done[i] = dma(s, f"lwq{sl}",
                                 lambda: s.dma_start(out=dst_ap,
                                                     in_=WQP[:, :]))
            on("sync", f2)

    load_xtile(0)
    load_xtile(1)

    def ln_stats(k):
        sl = k % 2
        scol = STAT[:, sl, :]

        def ln_v1(v, k=k, sl=sl, scol=scol):
            wtt(v, ld_x[k])
            if k >= 2:
                wt(v, "va", ac_xn[k - 2])
            mark(v.reduce_sum(scol[:, 0:1], XBUF[:, sl, :], AX.X), "va")
        on("vector", ln_v1)

        def ln_v2(v, k=k, sl=sl, scol=scol):
            wtt(v, ld_x[k])
            v.scalar_tensor_tensor(SC[:, 0, 0:QT], XBUF[:, sl, 0:QT], 1.0,
                                   XBUF[:, sl, 0:QT], ALU.bypass, ALU.mult,
                                   accum_out=scol[:, 1:2])
            v.scalar_tensor_tensor(SC[:, 0, 0:QT], XBUF[:, sl, QT:C], 1.0,
                                   XBUF[:, sl, QT:C], ALU.bypass, ALU.mult,
                                   accum_out=scol[:, 6:7])
            v.tensor_tensor(scol[:, 1:2], scol[:, 1:2], scol[:, 6:7], ALU.add)
            v.tensor_scalar(scol[:, 2:3], scol[:, 0:1], 1.0 / C, None,
                            ALU.mult)
            v.tensor_scalar(scol[:, 3:4], scol[:, 1:2], 1.0 / C, None,
                            ALU.mult)
            v.tensor_tensor(scol[:, 4:5], scol[:, 2:3], scol[:, 2:3], ALU.mult)
            mark(v.tensor_tensor(scol[:, 5:6], scol[:, 3:4], scol[:, 4:5],
                                 ALU.subtract), "va")
        on("vector", ln_v2)
        va_var = cnt["va"]

        def ln_a2(a, scol=scol, va_var=va_var):
            wt(a, "va", va_var)
            a.activation(scol[:, 6:7], scol[:, 5:6], AF.Ln, bias=EPST[:, 0:1])
            mark(a.activation(scol[:, 7:8], scol[:, 6:7], AF.Exp, scale=-0.5),
                 "ac")
        on("scalar", ln_a2)
        ac_rstd[k] = cnt["ac"]

    def ln_norm(k):
        sl = k % 2
        scol = STAT[:, sl, :]

        def ln_v3(v, scol=scol, ac_r=ac_rstd[k]):
            wt(v, "ac", ac_r)
            mark(v.scalar_tensor_tensor(scol[:, 4:5], scol[:, 2:3], -1.0,
                                        scol[:, 7:8], ALU.mult, ALU.mult),
                 "va")
        on("vector", ln_v3)

        def ln_v3b(v, k=k, sl=sl, scol=scol):
            if k >= 1:
                wtt(v, sd_xp[k - 1])
            v.scalar_tensor_tensor(XNLY[:, :], XBUF[:, sl, :], scol[:, 7:8],
                                   scol[:, 4:5].broadcast_to([P, C]),
                                   ALU.mult, ALU.add)
            ac_xn[k] = cnt["va"]
        on("vector", ln_v3b)

        def ln_v4(v, k=k):
            v.reduce_max(RMX[:, 2:3], XNLY[:, :], AX.X,
                         apply_absolute_value=True)
            mark(v.tensor_tensor(XMX[:, 0:1], XMX[:, 0:1], RMX[:, 2:3],
                                 ALU.max), "va")
        on("vector", ln_v4)

        def ln_xp(s, k=k):
            wt(s, "va", ac_xn[k])
            sd_xp[k] = dma(s, "sdx",
                           lambda: s.dma_start_transpose(
                               out=XNF[:, :, XCOL0 + 128 * k:
                                     XCOL0 + 128 * (k + 1)],
                               in_=XNLY[:, :]))
        on("scalar", ln_xp)

        if k + 2 < NT128:
            load_xtile(k + 2, wait_ac=ac_xn[k])

        if k < NCH:
            load_wtile("k", wkT_d, k, WKT, wait_va=ac_hnl)
        else:
            load_wtile("v", wvT_d, k - NCH, WVT)

        if 4 <= k <= 12:
            dequant_piece(("u", "u", "u", "g", "g", "g", "d", "d", "d")[k - 4])

    ac_rstd = {}
    for k in range(NT128 + 1):
        if k < NT128:
            ln_stats(k)
        if k >= 1:
            ln_norm(k - 1)

    va_xmx_final = cnt["va"]


    def halo_g(g):
        wt(g, "va", ac_hnl)
        dma(g, "gmi", lambda: g.dma_start(out=halo_d[:, :], in_=HNL[:, :]))
        dma(g, "gmi", lambda: g.dma_start(
            out=XNF[:, :, XCOL0 - 1:XCOL0],
            in_=halo_d[0:1, :].rearrange("a (m p) -> p m a", p=P),
            allow_slow_non_contiguous=True))
    on("gpsimd", halo_g)
    gd_halo = tot("gmi")

    def c1_g(g):
        wt(g, "va", va_xmx_final)
        dma(g, "gmi", lambda: g.dma_start(
            out=red_d[0:1, :].rearrange("a p -> p a"),
            in_=XMX[:, 0:1], allow_slow_non_contiguous=True))
        dma(g, "gmi", lambda: g.dma_start(out=RROW[:, :], in_=red_d[:, :]))
    on("gpsimd", c1_g)
    gd_rrow = tot("gmi")

    def c1_v(v):
        wtt(v, gd_rrow)
        mark(v.reduce_max(CCS[0:1, 0:1], RROW[0:1, :], AX.X,
                          apply_absolute_value=True), "va")
    on("vector", c1_v)
    va_ccs1 = cnt["va"]

    def c1_g2(g):
        wt(g, "va", va_ccs1)
        wtt(g, dma(g, "gmi", lambda: g.dma_start(out=mxi[0].ap(),
                                                 in_=CCS[:, :])))
        dma(g, "cc", lambda: g.collective_compute(
            "AllReduce", ALU.max, replica_groups=[list(range(NCORES))],
            ins=[mxi[0].ap().opt()], outs=[mxo[0].ap().opt()]), inc=1)
        wtt(g, tot("cc"))
        dma(g, "gmi", lambda: g.dma_start(out=SCAL[0:1, 0:1],
                                          in_=mxo[0].ap()[0:1, 0:1]))
    on("gpsimd", c1_g2)
    gd_s1 = tot("gmi")

    def wq_cc(g):
        wtt(g, [dq_done[k] for k in range(0, 6)])
        dma(g, "ccu", lambda: g.collective_compute(
            "AllGather", ALU.bypass, replica_groups=[list(range(NCORES))],
            ins=[dqu_t.ap().opt()], outs=[wqu_t.ap().opt()]), inc=1)
        gcc["u"] = tot("ccu")
        wtt(g, [dq_done[k] for k in range(6, 12)])
        dma(g, "ccg", lambda: g.collective_compute(
            "AllGather", ALU.bypass, replica_groups=[list(range(NCORES))],
            ins=[dqg_t.ap().opt()], outs=[wqg_t.ap().opt()]), inc=1)
        gcc["g"] = tot("ccg")
        wtt(g, [dq_done[k] for k in range(12, 18)])
        dma(g, "ccd", lambda: g.collective_compute(
            "AllGather", ALU.bypass, replica_groups=[list(range(NCORES))],
            ins=[dqd_t.ap().opt()], outs=[wqd_t.ap().opt()]), inc=1)
        gcc["d"] = tot("ccd")
    on("gpsimd", wq_cc)

    def c1_v2(v):
        wtt(v, gd_s1)
        v.tensor_scalar(SCAL[0:1, 1:2], SCAL[0:1, 0:1], 1.0 / QMAX, EPS_Q,
                        ALU.mult, ALU.max)
        mark(v.reciprocal(SCAL[0:1, 2:3], SCAL[0:1, 1:2]), "va")
    on("vector", c1_v2)
    va_s1 = cnt["va"]

    def c1_pe(t):
        wt(t, "va", va_s1)
        mark(t.matmul(PBC[:, 0:2], ONES32[:, :], SCAL[0:1, 1:3],
                      start=True, stop=True), "pe")
    on("tensor", c1_pe)
    pe_b1 = cnt["pe"]

    def c1_v3(v):
        wt(v, "pe", pe_b1)
        v.tensor_copy(BCAST[:, 1:2], PBC[:, 0:1])
        mark(v.tensor_copy(BCAST[:, 0:1], PBC[:, 1:2]), "va")
    on("vector", c1_v3)
    va_b1 = cnt["va"]

    def x_cols(q):
        return (XCOL0 + q * QT, XCOL0 + (q + 1) * QT)

    def xx_cols(q):
        return (XCOL0 - 1 + q * QT, XCOL0 - 1 + (q + 1) * QT)

    va_scan = {}
    po_nbdb = {}
    nb_prev = [None]
    nb_prev2 = [None]
    ac_ek_m = {}
    pe_kv_last = [0]

    for q in range(NQ):
        a0, a1 = x_cols(q)
        b0, b1 = xx_cols(q)

        def mix_k(v, q=q, a0=a0, a1=a1, b0=b0, b1=b1,
                  pe_prev=pe_kv_last[0]):
            kmax = (q * QT + QT) // 128 - 1
            wtt(v, sd_xp[kmax])
            wt(v, "pe", pe_prev)
            if q == 0:
                wtt(v, gd_halo)
                wt(v, "va", va_init)
            v.tensor_tensor(DIFQ[:, :, :], XNF[:, :, a0:a1], XNF[:, :, b0:b1],
                            ALU.subtract)
            for j in range(NCH):
                v.tensor_scalar(XMQ[:, j, :], DIFQ[:, j, :], TMK[:, j:j + 1],
                                None, ALU.mult)
                v.tensor_tensor(XMQ[:, j, :], XMQ[:, j, :], XNF[:, j, b0:b1],
                                ALU.add)
            mark(v.engine_nop(), "va")
        on("vector", mix_k)
        va_mixk = cnt["va"]

        def mix_v2(v, q=q, a0=a0, a1=a1, b0=b0, b1=b1,
                   pe_prev2=pe_kv_last[0]):
            wt(v, "pe", pe_prev2)
            for j in range(NCH):
                v.tensor_scalar(XM2Q[:, j, :], DIFQ[:, j, :], TMV[:, j:j + 1],
                                None, ALU.mult)
                v.tensor_tensor(XM2Q[:, j, :], XM2Q[:, j, :], XNF[:, j, b0:b1],
                                ALU.add)
            mark(v.engine_nop(), "va")
        on("vector", mix_v2)
        va_mixv = cnt["va"]

        kb = {}
        for m in range(NCH):
            bk = next_bank()
            kb[m] = bk

            def k_mm(t, m=m, bk=bk, va_mixk=va_mixk, fk=bank_free[bk]):
                wt(t, "va", va_mixk)
                wtt(t, [ld_w[("k", jj)] for jj in range(NCH - 4, NCH)])
                wt(t, *fk)
                for j in range(NCH):
                    ins = t.matmul(PB[bk][:, :], WKT[:, j, 128 * m:128 * (m + 1)],
                             XMQ[:, j, :], start=(j == 0), stop=(j == NCH - 1))
                    if j == NCH - 1:
                        mark(ins, "pe")
            on("tensor", k_mm)
            pe_k = cnt["pe"]

            def k_ev(a, m=m, bk=bk, pe_k=pe_k,
                     w_po=po_nbdb.get((q - 1, m), 0) if q else 0):
                wt(a, "pe", pe_k)
                wt(a, "po", w_po)
                mark(a.activation(EKQ[:, m, :], PB[bk][:, :], AF.Exp), "ac")
            on("scalar", k_ev)
            bank_free[bk] = ("ac", cnt["ac"])
            ac_ek_m[m] = cnt["ac"]
        ac_ek_all = cnt["ac"]
        pe_k_all = cnt["pe"]

        for m in range(NCH):
            bv = next_bank()

            def v_mm(t, m=m, bv=bv, va_mixv=va_mixv, fv=bank_free[bv]):
                wt(t, "va", va_mixv)
                wtt(t, [ld_w[("v", jj)] for jj in range(NCH - 4, NCH)])
                wt(t, *fv)
                for j in range(NCH):
                    ins = t.matmul(PB[bv][:, :], WVT[:, j, 128 * m:128 * (m + 1)],
                             XM2Q[:, j, :], start=(j == 0), stop=(j == NCH - 1))
                    if j == NCH - 1:
                        mark(ins, "pe")
            on("tensor", v_mm)
            pe_v = cnt["pe"]

            ssl = (q * NCH + m) % 2

            def scan_v(v, q=q, m=m, bv=bv, ssl=ssl, pe_v=pe_v,
                       ac_ek=ac_ek_m[m],
                       w_po=po_nbdb.get(nb_prev2[0], 0)):
                wt(v, "ac", ac_ek)
                wt(v, "pe", pe_v)
                wt(v, "po", w_po)
                v.tensor_tensor(EKVQ[:, m, :], EKQ[:, m, :], PB[bv][:, :],
                                ALU.mult)
                v.tensor_copy(SA[:, ssl, 0:1], CARA[:, m:m + 1])
                v.tensor_copy(SB_[:, ssl, 0:1], CARB[:, m:m + 1])
                v.tensor_tensor_scan(SA[:, ssl, 1:QT + 1],
                                     EW[:, m:m + 1].broadcast_to([P, QT]),
                                     EKVQ[:, m, :], CARA[:, m:m + 1],
                                     ALU.mult, ALU.add)
                v.tensor_tensor_scan(SB_[:, ssl, 1:QT + 1],
                                     EW[:, m:m + 1].broadcast_to([P, QT]),
                                     EKQ[:, m, :], CARB[:, m:m + 1],
                                     ALU.mult, ALU.add)
                v.tensor_copy(CARA[:, m:m + 1], SA[:, ssl, QT:QT + 1])
                v.tensor_copy(CARB[:, m:m + 1], SB_[:, ssl, QT:QT + 1])
                v.scalar_tensor_tensor(NB[:, m, q * QT:(q + 1) * QT],
                                       EKVQ[:, m, :], EU[:, m:m + 1],
                                       SA[:, ssl, 0:QT], ALU.mult, ALU.add)
                mark(v.scalar_tensor_tensor(DB[:, m, q * QT:(q + 1) * QT],
                                            EKQ[:, m, :], EU[:, m:m + 1],
                                            SB_[:, ssl, 0:QT],
                                            ALU.mult, ALU.add), "va")
            on("vector", scan_v)
            va_scan[(q, m)] = cnt["va"]
            bank_free[bv] = ("va", cnt["va"])
            po_nbdb[(q, m)] = 0
            nb_prev2[0] = nb_prev[0]
            nb_prev[0] = (q, m)
        pe_kv_last[0] = cnt["pe"]

    va_scan_done = cnt["va"]

    def exp_v(v):
        wt(v, "va", va_scan_done)
        v.tensor_copy(STOUT[:, 0:NCH], CARA[:, :])
        mark(v.tensor_copy(STOUT[:, NCH:16], CARB[:, :]), "va")
    on("vector", exp_v)
    va_stout = cnt["va"]

    def exp_g(g):
        wt(g, "va", va_stout)
        wtt(g, dma(g, "gmi", lambda: g.dma_start(out=agi_t.ap(),
                                                 in_=STOUT[:, :])))
        dma(g, "cc", lambda: g.collective_compute(
            "AllGather", ALU.bypass, replica_groups=[list(range(NCORES))],
            ins=[agi_t.ap().opt()], outs=[ago_t.ap().opt()]), inc=1)
        wtt(g, tot("cc"))
        dma(g, "gmi", lambda: g.dma_start(
            out=GST[:, :, :],
            in_=ago_t.ap().rearrange("(r p) c -> p r c", p=P)))
    on("gpsimd", exp_g)
    gd_gst = tot("gmi")

    for cm in (cmXM2Q, cmSB, cmSA, cmEKVQ, cmEKQ, cmWVT, cmWKT,
               cmWQP, cmDQB2, cmDQB, cmZSC, cmDQ2, cmXBUF):
        cm.__exit__(None, None, None)

    SR, cmSR = sb("SR", [P, NCH, TOK], BF16, side="right")
    RWQ, cmRWQ = sb("RWQ", [P, NCH, QT], BF16, side="right")
    PCHT, cmPCHT = sb("PCHT", [P, QT], BF16, side="right")
    RCHT, cmRCHT = sb("RCHT", [P, QT], BF16, side="right")
    ECH, cmECH = sb("ECH", [P, 2, QT], BF16, side="right")
    WRT, cmWRT = sb("WRT", [P, NCH, C], BF16, side="right")

    pe_kv_done = pe_kv_last[0]

    pe_rq_last = [pe_kv_done]

    def rw_guard(g):
        wtt(g, dq_done[17])
    on("gpsimd", rw_guard)
    for j in range(NCH):
        load_wtile("r", wrT_d, j, WRT, wait_pe=pe_kv_done)

    for q in range(NQ):
        a0, a1 = x_cols(q)
        b0, b1 = xx_cols(q)

        def rmix_v(v, a0=a0, a1=a1, b0=b0, b1=b1, pe_prev=pe_rq_last[0]):
            wt(v, "pe", pe_prev)
            v.tensor_tensor(DIFQ[:, :, :], XNF[:, :, a0:a1], XNF[:, :, b0:b1],
                            ALU.subtract)
            for j in range(NCH):
                v.tensor_scalar(XMQ[:, j, :], DIFQ[:, j, :], TMR[:, j:j + 1],
                                None, ALU.mult)
                v.tensor_tensor(XMQ[:, j, :], XMQ[:, j, :], XNF[:, j, b0:b1],
                                ALU.add)
            mark(v.engine_nop(), "va")
        on("vector", rmix_v)
        va_rmix = cnt["va"]

        for m in range(NCH):
            br = next_bank()

            def r_mm(t, q=q, m=m, br=br, va_rmix=va_rmix, fb=bank_free[br]):
                wt(t, "va", va_rmix)
                wtt(t, [ld_w[("r", jj)] for jj in range(NCH - 4, NCH)])
                wt(t, *fb)
                for j in range(NCH):
                    ins = t.matmul(PB[br][:, :], WRT[:, j, 128 * m:128 * (m + 1)],
                             XMQ[:, j, :], start=(j == 0), stop=(j == NCH - 1))
                    if j == NCH - 1:
                        mark(ins, "pe")
            on("tensor", r_mm)
            pe_r = cnt["pe"]

            def r_ev(a, q=q, m=m, br=br, pe_r=pe_r):
                wt(a, "pe", pe_r)
                mark(a.activation(SR[:, m, q * QT:(q + 1) * QT], PB[br][:, :],
                                  AF.Sigmoid), "ac")
            on("scalar", r_ev)
            bank_free[br] = ("ac", cnt["ac"])
            pe_rq_last[0] = pe_r

    ac_sr_done = cnt["ac"]
    pe_r_done = cnt["pe"]
    va_rmix_done = cnt["va"]

    cmWRT.__exit__(None, None, None)
    cmXMQ.__exit__(None, None, None)
    cmDIFQ.__exit__(None, None, None)
    WOT, cmWOT = sb("WOT", [P, NCH, C], BF16, side="right")
    PCHTB, cmPCHTB = sb("PCHTB", [P, QT], BF16, side="right")
    RCHTB, cmRCHTB = sb("RCHTB", [P, QT], BF16, side="right")
    UMAXR, cmUMAXR = sb("UMAXR", [P, QT], BF16, side="right")
    UMINR, cmUMINR = sb("UMINR", [P, QT], BF16, side="right")
    GMAXR, cmGMAXR = sb("GMAXR", [P, QT], BF16, side="right")
    GMINR, cmGMINR = sb("GMINR", [P, QT], BF16, side="right")
    OPCH, cmOPCH = sb("OPCH", [P, 2, QT], BF16, side="right")

    def umx_v(v):
        v.memset(UMAXR[:, :], 0.0)
        v.memset(UMINR[:, :], 0.0)
        v.memset(GMAXR[:, :], 0.0)
        mark(v.memset(GMINR[:, :], 0.0), "va")
    on("vector", umx_v)
    va_umx0 = cnt["va"]
    for j in range(NCH):
        load_wtile("o", woT_d, j, WOT, wait_pe=pe_r_done)
    gd_wot = [ld_w[("o", jj)] for jj in range(NCH - 4, NCH)]

    def rio_g(g):
        wt(g, "va", va_rmix_done)
        mark(g.iota(SC[0:1, 0, 0:QT], [[1, QT]], channel_multiplier=0,
                    allow_small_or_imprecise_dtypes=True), "po")
    on("gpsimd", rio_g)
    po_iota = cnt["po"]

    def rio_pe(t):
        wt(t, "po", po_iota)
        mark(t.matmul(PIOT[:, 0:QT], ONES32[:, :], SC[0:1, 0, 0:QT],
                      start=True, stop=True), "pe")
    on("tensor", rio_pe)
    pe_iota2 = cnt["pe"]

    ac_xq = {}
    va_xq = {}
    for jj in range(NQ * NCH):
        j, qq = divmod(jj, NQ)
        c0 = XCOL0 + qq * QT

        def xq_a(p, jj=jj, j=j, c0=c0):
            if jj == 0:
                wt(p, "va", va_b1)
                wt(p, "va", va_rmix_done)
                wt(p, "pe", pe_iota2)
            wt(p, "va", va_xq.get(jj - 2, 0))
            ac_xq[jj] = mark(
                p.tensor_scalar(SC[:, jj % 2, :], XNF[:, j, c0:c0 + QT],
                                BCAST[:, 0:1], MAGIC, ALU.mult, ALU.add), "po")
        on("gpsimd", xq_a)

        def xq_v(v, jj=jj, j=j, c0=c0):
            wt(v, "po", ac_xq[jj])
            va_xq[jj] = mark(
                v.tensor_scalar(XNF[:, j, c0:c0 + QT], SC[:, jj % 2, :],
                                -MAGIC, BCAST[:, 1:2], ALU.add, ALU.mult), "va")
        on("vector", xq_v)
    va_xq_done = cnt["va"]

    sd_upt = {}
    va_mx = {}
    ac_ev = {}
    gsp_done = {}

    def emit_up_step(step):
        ii, q = divmod(step, NQ)
        up = ii < NI
        i = ii if up else ii - NI
        sl = ii % 2
        if q == 0:
            def up_xp(s, ii=ii, i=i, up=up, sl=sl):
                wtt(s, gcc["u" if up else "g"])
                wt(s, "pe", pe_up.get(ii - 2, 0))
                sd_upt[ii] = dma(
                    s, "sdu",
                    lambda: s.dma_start_transpose(
                        out=WTT[:, sl, :, :],
                        in_=(wqu_d if up else wqg_d)[i * P:(i + 1) * P, :]))
            on("sync", up_xp)
        bu = next_bank()
        esl = step % 2

        def up_mm(t, ii=ii, sl=sl, q=q, bu=bu, fb=bank_free[bu]):
            wtt(t, sd_upt[ii])
            wt(t, "va", va_xq_done)
            wt(t, *fb)
            for j in range(NCH):
                ins = t.matmul(PB[bu][:, :], WTT[:, sl, j, :],
                         XNF[:, j, XCOL0 + q * QT:XCOL0 + (q + 1) * QT],
                         start=(j == 0), stop=(j == NCH - 1))
                if j == NCH - 1:
                    mark(ins, "pe")
        on("tensor", up_mm)
        pe_m = cnt["pe"]
        if q == NQ - 1:
            pe_up[ii] = pe_m

        def up_ev(a, bu=bu, esl=esl, pe_m=pe_m, w_sp=ech_spill[esl],
                  w_mx=va_mx.get(step - 2, 0)):
            wt(a, "pe", pe_m)
            wtt(a, w_sp)
            wt(a, "va", w_mx)
            ac_ev[step] = mark(
                a.activation(ECH[:, esl, :], PB[bu][:, :], AF.Identity), "ac")
        on("scalar", up_ev)
        bank_free[bu] = ("ac", ac_ev[step])

        def up_mx(v, up=up, esl=esl, step=step, first=(step == 0)):
            if first:
                wt(v, "va", va_umx0)
            wt(v, "ac", ac_ev[step])
            mxr, mnr = (UMAXR, UMINR) if up else (GMAXR, GMINR)
            v.tensor_tensor(mxr[:, :], mxr[:, :], ECH[:, esl, :], ALU.max)
            va_mx[step] = mark(
                v.tensor_tensor(mnr[:, :], mnr[:, :], ECH[:, esl, :],
                                ALU.min), "va")
        on("vector", up_mx)

        def up_sp(g, up=up, i=i, q=q, esl=esl, step=step):
            wt(g, "ac", ac_ev[step])
            dst = upspill_d if up else gspill_d
            ech_spill[esl] = dma(g, f"lu{esl}", lambda: g.dma_start(
                out=dst[i * P:(i + 1) * P, q * QT:(q + 1) * QT],
                in_=ECH[:, esl, :]))
        on("gpsimd", up_sp)
        if (not up) and q == NQ - 1:
            gsp_done[i] = [tot("lu0"), tot("lu1")]

    def sel_pe(t):
        wtt(t, ld_params)
        wt(t, "va", va_b1)
        mark(t.matmul(PBC[:, 0:NCORES], ONES32[:, :], SELB[0:1, :],
                      start=True, stop=True), "pe")
    on("tensor", sel_pe)
    pe_selb = cnt["pe"]

    def sel_v(v):
        wt(v, "pe", pe_selb)
        v.tensor_copy(SELB[:, :], PBC[:, 0:NCORES])
        wtt(v, gd_gst)
        for r in range(NCORES):
            v.scalar_tensor_tensor(ASEL[:, :], GST[:, r, :], SELB[:, r:r + 1],
                                   ASEL[:, :], ALU.mult, ALU.add)
        mark(v.engine_nop(), "va")
    on("vector", sel_v)
    va_asel = cnt["va"]

    va_rw = {}
    va_pm_l = {}
    va_fix_l = {}
    ac_p_l = {}
    ac_rc_l = {}
    NTOT = NQ * NCH

    def t_pch(idx):
        return PCHT if idx % 2 == 0 else PCHTB

    def t_rch(idx):
        return RCHT if idx % 2 == 0 else RCHTB

    def emit_v1(idx):
        q, m = divmod(idx, NCH)

        def tl_v1(v, idx=idx, q=q, m=m):
            if idx == 0:
                wt(v, "va", va_asel)
                wt(v, "pe", pe_iota2)
            wt(v, "ac", ac_rc_l.get(idx - 2, 0))
            va_pm_l[idx] = mark(
                v.tensor_scalar(SC[:, idx % 2, 0:QT], PIOT[:, 0:QT],
                                LNW[:, m:m + 1], LNWO[:, m, q:q + 1],
                                ALU.mult, ALU.add), "va")
        on("vector", tl_v1)

    def emit_actA(idx):
        def f(a, idx=idx):
            wt(a, "va", va_pm_l[idx])
            ac_p_l[idx] = mark(
                a.activation(t_pch(idx)[:, :], SC[:, idx % 2, 0:QT], AF.Exp),
                "ac")
        on("scalar", f)

    def emit_v2(idx):
        q, m = divmod(idx, NCH)
        cl = slice(q * QT, (q + 1) * QT)

        def tl_v2(v, idx=idx, q=q, m=m, cl=cl):
            wt(v, "ac", ac_p_l[idx])
            wt(v, "po", po_nbdb[(q, m)])
            v.scalar_tensor_tensor(NB[:, m, cl], t_pch(idx)[:, :],
                                   ASEL[:, m:m + 1], NB[:, m, cl],
                                   ALU.mult, ALU.add)
            va_fix_l[idx] = mark(
                v.scalar_tensor_tensor(DB[:, m, cl], t_pch(idx)[:, :],
                                       ASEL[:, NCH + m:NCH + m + 1],
                                       DB[:, m, cl], ALU.mult, ALU.add), "va")
        on("vector", tl_v2)

    def emit_actB(idx):
        q, m = divmod(idx, NCH)
        cl = slice(q * QT, (q + 1) * QT)

        def f(a, idx=idx, m=m, cl=cl):
            wt(a, "va", va_fix_l[idx])
            a.activation(SC[:, idx % 2, 0:QT], DB[:, m, cl], AF.Ln)
            ac_rc_l[idx] = mark(
                a.activation(t_rch(idx)[:, :], SC[:, idx % 2, 0:QT], AF.Exp,
                             scale=-1.0), "ac")
        on("scalar", f)

    def emit_v3(idx):
        q, m = divmod(idx, NCH)
        cl = slice(q * QT, (q + 1) * QT)

        def tl_v3(v, idx=idx, m=m, cl=cl):
            wt(v, "ac", ac_rc_l[idx])
            wt(v, "ac", ac_sr_done)
            v.tensor_tensor(RWQ[:, m, :], NB[:, m, cl], t_rch(idx)[:, :],
                            ALU.mult)
            mark(v.tensor_tensor(RWQ[:, m, :], RWQ[:, m, :], SR[:, m, cl],
                                 ALU.mult), "va")
        on("vector", tl_v3)
        va_rw[(q, m)] = cnt["va"]

    def emit_outproj(q):
        for tb in range(4):
            for hh in range(2):
                bq = (tb * 2 + hh) % 2
                asl = (tb * 2 + hh) % 2

                def op_mm(t, q=q, tb=tb, bq=bq, hh=hh,
                          fb=ep_bank_free[bq], vrw=va_rw[(q, NCH - 1)]):
                    wt(t, "va", vrw)
                    wtt(t, gd_wot)
                    wt(t, *fb)
                    for m in range(NCH):
                        ins = t.matmul(PEP[bq][:, :],
                                 RWQ[:, m, 128 * tb:128 * (tb + 1)],
                                 WOT[:, m, hh * QT:(hh + 1) * QT],
                                 start=(m == 0), stop=(m == NCH - 1))
                        if m == NCH - 1:
                            mark(ins, "pe")
                on("tensor", op_mm)
                pe_op = cnt["pe"]

                def op_ev(v, q=q, tb=tb, bq=bq, asl=asl, pe_op=pe_op,
                          w_sp=atch_spill[asl]):
                    wt(v, "pe", pe_op)
                    wtt(v, w_sp)
                    mark(v.tensor_copy(OPCH[:, asl, :], PEP[bq][:, :]), "va")
                on("vector", op_ev)
                ac_ae = cnt["va"]
                ep_bank_free[bq] = ("va", ac_ae)

                def op_sp(s, q=q, tb=tb, hh=hh, asl=asl, ac_ae=ac_ae):
                    wt(s, "va", ac_ae)
                    r0 = q * QT + tb * 128
                    atch_spill[asl] = dma(s, f"la{asl}", lambda: s.dma_start(
                        out=aspill_d[r0:r0 + 128, hh * QT:(hh + 1) * QT],
                        in_=OPCH[:, asl, :]))
                on("sync", op_sp)

    def emit_tail_step(st):
        if st < NTOT:
            emit_v1(st)
            emit_actA(st)
        idx = st - 1
        if 0 <= idx < NTOT:
            emit_v2(idx)
            emit_actB(idx)
            emit_v3(idx)
            if idx % NCH == NCH - 1:
                emit_outproj(idx // NCH)

    UPS = 2 * NI * NQ
    tstep = [0]
    for ustep in range(UPS):
        emit_up_step(ustep)
        if ustep % 6 == 2:
            emit_tail_step(tstep[0])
            tstep[0] += 1
    while tstep[0] <= NTOT:
        emit_tail_step(tstep[0])
        tstep[0] += 1

    pe_up_done = pe_up[2 * NI - 1]
    ld_upspill_done = [tot("lu0"), tot("lu1")]

    def upmax_v(v):
        wt(v, "va", va_mx[UPS - 1])
        v.reduce_max(RMX[:, 0:1], UMAXR[:, :], AX.X)
        v.reduce_max(RMX[:, 2:3], UMINR[:, :], AX.X,
                     apply_absolute_value=True)
        v.tensor_tensor(RMX[:, 0:1], RMX[:, 0:1], RMX[:, 2:3], ALU.max)
        v.reduce_max(RMX[:, 1:2], GMAXR[:, :], AX.X)
        v.reduce_max(RMX[:, 2:3], GMINR[:, :], AX.X,
                     apply_absolute_value=True)
        mark(v.tensor_tensor(RMX[:, 1:2], RMX[:, 1:2], RMX[:, 2:3], ALU.max),
             "va")
    on("vector", upmax_v)
    va_upmax = cnt["va"]
    va_gmax = va_upmax

    va_tail_done = cnt["va"]
    ac_tail_done = cnt["ac"]
    pe_op_done = cnt["pe"]
    ld_aspill_done = [tot("la0"), tot("la1")]

    for cm in (cmOPCH, cmGMINR, cmGMAXR, cmUMINR, cmUMAXR, cmRCHTB, cmPCHTB,
               cmWOT, cmECH, cmRCHT, cmPCHT, cmRWQ, cmSR, cmDB, cmNB):
        cm.__exit__(None, None, None)

    ATCH, _ = sb("ATCH", [P, 2, QT], BF16)
    FS, cmFS = sb("FS", [P, NI, TOK], BF16)
    XCH, cmXCH = sb("XCH", [P, 2, QT], F32)
    OCH, cmOCH = sb("OCH", [P, 2, QT], F32)
    SCF, cmSCF = sb("SCF", [P, 2, TOK], F32, side="right")
    GCH, cmGCH = sb("GCH", [P, 2, TOK], BF16, side="right")
    SGCH, cmSGCH = sb("SGCH", [P, TOK], BF16, side="right")
    R2CH, cmR2CH = sb("R2CH", [P, TOK], BF16, side="right")
    UCH, cmUCH = sb("UCH", [P, 2, TOK], BF16, side="right")
    GIN, cmGIN = sb("GIN", [P, 2, TOK], BF16, side="right")

    def c2_g(g):
        wt(g, "va", max(va_gmax, va_upmax))
        dma(g, "gmi", lambda: g.dma_start(
            out=red_d[0:1, :].rearrange("a p -> p a"),
            in_=RMX[:, 0:1], allow_slow_non_contiguous=True))
        dma(g, "gmi", lambda: g.dma_start(out=RROW[:, :], in_=red_d[:, :]))
    on("gpsimd", c2_g)
    gd_c2a = tot("gmi")

    def c2_v(v):
        wtt(v, gd_c2a)
        mark(v.reduce_max(CCS[0:1, 0:1], RROW[0:1, :], AX.X,
                          apply_absolute_value=True), "va")
    on("vector", c2_v)
    va_c2a = cnt["va"]

    def c2_g2(g):
        wt(g, "va", va_c2a)
        dma(g, "gmi", lambda: g.dma_start(
            out=red_d[0:1, :].rearrange("a p -> p a"),
            in_=RMX[:, 1:2], allow_slow_non_contiguous=True))
        dma(g, "gmi", lambda: g.dma_start(out=RROW[:, :], in_=red_d[:, :]))
    on("gpsimd", c2_g2)
    gd_c2b = tot("gmi")

    def c2_v2(v):
        wtt(v, gd_c2b)
        mark(v.reduce_max(CCS[0:1, 1:2], RROW[0:1, :], AX.X,
                          apply_absolute_value=True), "va")
    on("vector", c2_v2)
    va_c2b = cnt["va"]

    def c2_g3(g):
        wt(g, "va", va_c2b)
        wtt(g, dma(g, "gmi", lambda: g.dma_start(out=mxi[1].ap(),
                                                 in_=CCS[:, :])))
        dma(g, "cc", lambda: g.collective_compute(
            "AllReduce", ALU.max, replica_groups=[list(range(NCORES))],
            ins=[mxi[1].ap().opt()], outs=[mxo[1].ap().opt()]), inc=1)
        wtt(g, tot("cc"))
        dma(g, "gmi", lambda: g.dma_start(out=SCAL[0:1, 4:6],
                                          in_=mxo[1].ap()[0:1, 0:2]))
    on("gpsimd", c2_g3)
    gd_c2 = tot("gmi")

    def c2_v3(v):
        wtt(v, gd_c2)
        v.memset(RMX[:, 1:2], 0.0)
        v.tensor_scalar(SCAL[0:1, 6:7], SCAL[0:1, 4:5], 1.0 / QMAX, EPS_Q,
                        ALU.mult, ALU.max)
        v.reciprocal(SCAL[0:1, 7:8], SCAL[0:1, 6:7])
        v.tensor_scalar(SCAL[0:1, 8:9], SCAL[0:1, 5:6], 1.0 / QMAX, EPS_Q,
                        ALU.mult, ALU.max)
        mark(v.reciprocal(SCAL[0:1, 9:10], SCAL[0:1, 8:9]), "va")
    on("vector", c2_v3)
    va_c2s = cnt["va"]

    def c2_pe(t):
        wt(t, "va", va_c2s)
        wt(t, "va", va_asel)
        mark(t.matmul(PBC[:, 0:4], ONES32[:, :], SCAL[0:1, 6:10],
                      start=True, stop=True), "pe")
    on("tensor", c2_pe)
    pe_c2 = cnt["pe"]

    def c2_v4(v):
        wt(v, "pe", pe_c2)
        v.tensor_copy(BCAST[:, 3:4], PBC[:, 0:1])
        v.tensor_copy(BCAST[:, 2:3], PBC[:, 1:2])
        v.tensor_copy(BCAST[:, 5:6], PBC[:, 2:3])
        v.tensor_copy(BCAST[:, 4:5], PBC[:, 3:4])
        v.tensor_scalar(BCAST[:, 10:11], BCAST[:, 5:6], -MAGIC, None, ALU.mult)
        mark(v.tensor_scalar(BCAST[:, 11:12], BCAST[:, 3:4], -MAGIC, None,
                             ALU.mult), "va")
    on("vector", c2_v4)
    va_c2bc = cnt["va"]

    sd_wqd = {}
    pe_dn_half = [0]

    gv2 = {}
    po_gqp = {}
    gin_ld = {}
    for i in range(NI):
        first = (i == 0)
        sl = i % 2
        gsl = i % 2

        def gin_s(s, i=i, gsl=gsl, wfree=po_gqp.get(i - 2, 0)):
            wt(s, "po", wfree)
            if i < 2:
                wtt(s, ld_upspill_done)
            wtt(s, gsp_done[i])
            gin_ld[i] = dma(s, f"lc{2 + i % 2}", lambda: s.dma_start(
                out=GIN[:, gsl, :], in_=gspill_d[i * P:(i + 1) * P, :]))
        on("sync", gin_s)

        def gq_p(p, i=i, sl=sl, gsl=gsl, first=first,
                 wprev=gv2.get(i - 2, 0)):
            if first:
                wt(p, "va", va_c2bc)
                wt(p, "va", va_tail_done)
                wt(p, "pe", pe_op_done)
                wtt(p, ld_upspill_done)
                wtt(p, ld_aspill_done)
            wtt(p, gin_ld[i])
            wt(p, "va", wprev)
            p.tensor_scalar(SCF[:, sl, :], GIN[:, gsl, :], BCAST[:, 4:5],
                            MAGIC, ALU.mult, ALU.add)
        on("gpsimd", gq_p)
        po_gqp[i] = cnt["po"]
        po_tg = cnt["po"]

        def gq_a(a, i=i, sl=sl, gch=i % 2, po_tg=po_tg,
                 wprev=gv2.get(i - 2, 0)):
            wt(a, "po", po_tg)
            wt(a, "va", wprev)
            a.activation(GCH[:, gch, :], SCF[:, sl, :], AF.Identity,
                         bias=BCAST[:, 10:11], scale=BCAST[:, 5:6])
            a.activation(SGCH[:, :], GCH[:, gch, :], AF.Sigmoid)
            a.activation(SCF[:, sl, :], SGCH[:, :], AF.Identity,
                         bias=MAGT[:, 0:1], scale=U16S)
        on("scalar", gq_a)
        ac_sg = cnt["ac"]

        def gq_v2(v, i=i, sl=sl, gch=i % 2, ac_sg=ac_sg):
            wt(v, "ac", ac_sg)
            v.tensor_scalar(R2CH[:, :], SCF[:, sl, :], MAGIC + U16S - 1.0,
                            -MAGIC, ALU.min, ALU.add)
            v.tensor_tensor(FS[:, i, :], GCH[:, gch, :], R2CH[:, :],
                            ALU.mult)
            v.reduce_max(RMX[:, 2:3], FS[:, i, :], AX.X,
                         apply_absolute_value=True)
            v.tensor_tensor(RMX[:, 3:4], RMX[:, 3:4], RMX[:, 2:3], ALU.max)
        on("vector", gq_v2)
        gv2[i] = cnt["va"]

    va_g2max = cnt["va"]

    def c3_g(g):
        wt(g, "va", va_g2max)
        dma(g, "gmi", lambda: g.dma_start(
            out=red_d[0:1, :].rearrange("a p -> p a"),
            in_=RMX[:, 3:4], allow_slow_non_contiguous=True))
        dma(g, "gmi", lambda: g.dma_start(out=RROW[:, :], in_=red_d[:, :]))
    on("gpsimd", c3_g)
    gd_c3a = tot("gmi")

    def c3_v(v):
        wtt(v, gd_c3a)
        mark(v.reduce_max(CCS[0:1, 0:1], RROW[0:1, :], AX.X,
                          apply_absolute_value=True), "va")
    on("vector", c3_v)
    va_c3a = cnt["va"]

    def c3_g2(g):
        wt(g, "va", va_c3a)
        wtt(g, dma(g, "gmi", lambda: g.dma_start(out=mxi[2].ap(),
                                                 in_=CCS[:, :])))
        dma(g, "cc", lambda: g.collective_compute(
            "AllReduce", ALU.max, replica_groups=[list(range(NCORES))],
            ins=[mxi[2].ap().opt()], outs=[mxo[2].ap().opt()]), inc=1)
        wtt(g, tot("cc"))
        dma(g, "gmi", lambda: g.dma_start(out=SCAL[0:1, 10:11],
                                          in_=mxo[2].ap()[0:1, 0:1]))
    on("gpsimd", c3_g2)
    gd_c3 = tot("gmi")

    def c3_v2(v):
        wtt(v, gd_c3)
        v.tensor_scalar(SCAL[0:1, 11:12], SCAL[0:1, 10:11],
                        1.0 / (U16S * QMAX), EPS_Q, ALU.mult, ALU.max)
        v.reciprocal(SCAL[0:1, 12:13], SCAL[0:1, 11:12])
        mark(v.tensor_scalar(SCAL[0:1, 13:14], SCAL[0:1, 12:13], 1.0 / U16S,
                             None, ALU.mult), "va")
    on("vector", c3_v2)
    va_c3s = cnt["va"]

    def c3_pe(t):
        wt(t, "va", va_c3s)
        wt(t, "va", va_c2bc)
        mark(t.matmul(PBC[:, 0:3], ONES32[:, :], SCAL[0:1, 11:14],
                      start=True, stop=True), "pe")
    on("tensor", c3_pe)
    pe_c3 = cnt["pe"]

    def c3_v3(v):
        wt(v, "pe", pe_c3)
        v.tensor_copy(BCAST[:, 7:8], PBC[:, 0:1])
        v.tensor_copy(BCAST[:, 6:7], PBC[:, 2:3])
        mark(v.tensor_scalar(BCAST[:, 12:13], BCAST[:, 7:8], -MAGIC, None,
                             ALU.mult), "va")
    on("vector", c3_v3)
    va_c3bc = cnt["va"]

    ov2 = {}
    og_ac2 = {}
    uch_ld = {}
    for i in range(NI):
        first = (i == 0)
        sl = i % 2

        def og_p(p, i=i, sl=sl, first=first, wprev=og_ac2.get(i - 2, 0)):
            if first:
                wt(p, "va", va_c3bc)
            wt(p, "ac", wprev)
            p.tensor_scalar(SCF[:, sl, :], FS[:, i, :], BCAST[:, 6:7], MAGIC,
                            ALU.mult, ALU.add)
        on("gpsimd", og_p)
        po_tg2 = cnt["po"]

        def og_a(a, i=i, sl=sl, po_tg2=po_tg2):
            wt(a, "po", po_tg2)
            a.activation(FS[:, i, :], SCF[:, sl, :], AF.Identity,
                         bias=BCAST[:, 12:13], scale=BCAST[:, 7:8])
        on("scalar", og_a)
        ac_g2w = cnt["ac"]

        usl = i % 2

        def og_s(s, i=i, usl=usl, wfree=ov2.get(i - 2, 0)):
            wt(s, "va", wfree)
            if i == 0:
                wtt(s, ld_upspill_done)
            uch_ld[i] = dma(s, f"lc{i % 2}", lambda: s.dma_start(
                out=UCH[:, usl, :], in_=upspill_d[i * P:(i + 1) * P, :]))
        on("sync", og_s)

        def og_a2(a, i=i, sl=sl, usl=usl):
            wtt(a, uch_ld[i])
            a.activation(SCF[:, sl, :], UCH[:, usl, :], AF.Identity,
                         bias=MAGT[:, 0:1], scale=BCAST[:, 2:3])
            a.activation(UCH[:, usl, :], SCF[:, sl, :], AF.Identity,
                         bias=NMAGT[:, 0:1])
        on("scalar", og_a2)
        og_ac2[i] = cnt["ac"]

        def og_v2(v, i=i, usl=usl, ac_u=og_ac2[i]):
            wt(v, "ac", ac_u)
            v.tensor_tensor(FS[:, i, :], FS[:, i, :], UCH[:, usl, :],
                            ALU.mult)
            v.reduce_max(RMX[:, 2:3], FS[:, i, :], AX.X,
                         apply_absolute_value=True)
            v.tensor_tensor(RMX[:, 1:2], RMX[:, 1:2], RMX[:, 2:3], ALU.max)
        on("vector", og_v2)
        ov2[i] = cnt["va"]

    va_omax = cnt["va"]
    ac_og_done = cnt["ac"]

    for cm in (cmGIN, cmUCH, cmR2CH, cmSGCH, cmGCH):
        cm.__exit__(None, None, None)
    WQDTH, cmWQDTH = sb("WQDTH", [P, NI, QT], BF16, side="right")

    def dn_xp0(s):
        wtt(s, gcc["d"])
        wt(s, "va", va_omax)
        wt(s, "ac", ac_og_done)
        sd_wqd[0] = dma(
            s, "sdd",
            lambda: s.dma_start_transpose(out=WQDTH[:, :, :],
                                          in_=wqd_d[0:QT, :]))
    on("sync", dn_xp0)

    def c4_g(g):
        wt(g, "va", va_omax)
        dma(g, "gmi", lambda: g.dma_start(
            out=red_d[0:1, :].rearrange("a p -> p a"),
            in_=RMX[:, 1:2], allow_slow_non_contiguous=True))
        dma(g, "gmi", lambda: g.dma_start(out=RROW[:, :], in_=red_d[:, :]))
    on("gpsimd", c4_g)
    gd_c4a = tot("gmi")

    def c4_v(v):
        wtt(v, gd_c4a)
        mark(v.reduce_max(CCS[0:1, 0:1], RROW[0:1, :], AX.X,
                          apply_absolute_value=True), "va")
    on("vector", c4_v)
    va_c4a = cnt["va"]

    def c4_g2(g):
        wt(g, "va", va_c4a)
        wtt(g, dma(g, "gmi", lambda: g.dma_start(out=mxi[0].ap(),
                                                 in_=CCS[:, :])))
        dma(g, "cc", lambda: g.collective_compute(
            "AllReduce", ALU.max, replica_groups=[list(range(NCORES))],
            ins=[mxi[0].ap().opt()], outs=[mxo[0].ap().opt()]), inc=1)
        wtt(g, tot("cc"))
        dma(g, "gmi", lambda: g.dma_start(out=SCAL[0:1, 14:15],
                                          in_=mxo[0].ap()[0:1, 0:1]))
    on("gpsimd", c4_g2)
    gd_c4 = tot("gmi")

    def c4_v2(v):
        wtt(v, gd_c4)
        v.tensor_tensor(SCAL[0:1, 14:15], SCAL[0:1, 14:15], SCAL[0:1, 6:7],
                        ALU.mult)
        v.tensor_scalar(SCAL[0:1, 15:16], SCAL[0:1, 14:15], 1.0 / QMAX, EPS_Q,
                        ALU.mult, ALU.max)
        v.reciprocal(SCAL[0:1, 3:4], SCAL[0:1, 15:16])
        mark(v.tensor_tensor(SCAL[0:1, 3:4], SCAL[0:1, 3:4], SCAL[0:1, 6:7],
                             ALU.mult), "va")
    on("vector", c4_v2)
    va_c4s = cnt["va"]

    def c4_pe(t):
        wt(t, "va", va_c4s)
        wt(t, "va", va_c3bc)
        t.matmul(PBC[:, 0:1], ONES32[:, :], SCAL[0:1, 15:16],
                 start=True, stop=True)
        mark(t.matmul(PBC[:, 1:2], ONES32[:, :], SCAL[0:1, 3:4],
                      start=True, stop=True), "pe")
    on("tensor", c4_pe)
    pe_c4 = cnt["pe"]

    def c4_v3(v):
        wt(v, "pe", pe_c4)
        v.tensor_copy(BCAST[:, 9:10], PBC[:, 0:1])
        v.tensor_copy(BCAST[:, 8:9], PBC[:, 1:2])
        mark(v.tensor_scalar(BCAST[:, 13:14], BCAST[:, 9:10], -MAGIC, None,
                             ALU.mult), "va")
    on("vector", c4_v3)
    va_c4bc = cnt["va"]

    oqa = {}
    for i in range(NI):
        first = (i == 0)
        sl = i % 2

        def oq_p(p, i=i, sl=sl, first=first, wprev=oqa.get(i - 2, 0)):
            if first:
                wt(p, "va", va_c4bc)
            wt(p, "ac", wprev)
            p.tensor_scalar(SCF[:, sl, :], FS[:, i, :], BCAST[:, 8:9], MAGIC,
                            ALU.mult, ALU.add)
        on("gpsimd", oq_p)
        po_to = cnt["po"]

        def oq_a(a, i=i, sl=sl, po_to=po_to):
            wt(a, "po", po_to)
            a.activation(FS[:, i, :], SCF[:, sl, :], AF.Identity,
                         bias=BCAST[:, 13:14], scale=BCAST[:, 9:10])
        on("scalar", oq_a)
        oqa[i] = cnt["ac"]

    ac_o_done = cnt["ac"]


    ld_final = [0]

    for hh in range(2):
        if hh == 1:
            def dn_xp(s, hh=hh):
                wt(s, "pe", pe_dn_half[0])
                sd_wqd[hh] = dma(
                    s, "sdd",
                    lambda: s.dma_start_transpose(
                        out=WQDTH[:, :, :],
                        in_=wqd_d[hh * QT:(hh + 1) * QT, :]))
            on("sync", dn_xp)

        for k in range(NT128):
            bq = k % 2
            csl = k % 2

            def ep_ld(s, hh=hh, k=k, csl=csl, step=hh * NT128 + k):
                wt(s, "va", xch_free.get(step - 2, 0))
                dma(s, f"le{csl}", lambda: s.dma_start(
                    out=XCH[:, csl, :], in_=xt[k, :, hh * QT:(hh + 1) * QT]))
                if step == 0:
                    wtt(s, ld_aspill_done)
                dma(s, f"lr{csl}", lambda: s.dma_start(
                    out=ATCH[:, csl, :],
                    in_=aspill_d[k * P:(k + 1) * P, hh * QT:(hh + 1) * QT]))
            on("sync", ep_ld)
            ld_ep = [tot(f"le{csl}"), tot(f"lr{csl}")]

            def dn_mm(t, hh=hh, k=k, bq=bq, fb=ep_bank_free[bq]):
                wtt(t, sd_wqd[hh])
                wt(t, *fb)
                for i in range(NI):
                    if hh == 0 and k < 2:
                        wt(t, "ac", oqa[i])
                    elif i == 0:
                        wt(t, "ac", ac_o_done)
                    ins = t.matmul(PEP[bq][:, :], FS[:, i, 128 * k:128 * (k + 1)],
                             WQDTH[:, i, :], start=(i == 0), stop=(i == NI - 1))
                    if i == NI - 1:
                        mark(ins, "pe")
            on("tensor", dn_mm)
            pe_dn = cnt["pe"]
            if k == NT128 - 1:
                pe_dn_half[0] = pe_dn

            def ep_v(v, hh=hh, k=k, bq=bq, csl=csl, pe_dn=pe_dn, ld_ep=ld_ep,
                     wfree=och_free.get(0, 0)):
                wt(v, "pe", pe_dn)
                wtt(v, ld_ep)
                wtt(v, och_free.get(hh * NT128 + k - 2, 0))
                v.tensor_tensor(SC[:, 0, 0:QT], PEP[bq][:, :], ATCH[:, csl, :],
                                ALU.add)
                mark(v.tensor_tensor(OCH[:, csl, :], SC[:, 0, 0:QT],
                                     XCH[:, csl, :], ALU.add), "va")
            on("vector", ep_v)
            va_ep = cnt["va"]
            xch_free[hh * NT128 + k] = va_ep
            ep_bank_free[bq] = ("va", va_ep)

            def ep_st(s, hh=hh, k=k, csl=csl, va_ep=va_ep):
                wt(s, "va", va_ep)
                m = dma(s, f"lo{csl}", lambda: s.dma_start(
                    out=out_d[k * P:(k + 1) * P, hh * QT:(hh + 1) * QT],
                    in_=OCH[:, csl, :]))
                och_free[hh * NT128 + k] = m
                ld_final[0] = m
            on("sync", ep_st)

    def fin_g(g):
        wtt(g, tot("lo0"))
        wtt(g, tot("lo1"))
        g.engine_nop()
    on("gpsimd", fin_g)

    flush_streams()
    return nc


def kernel(**inputs):
    x = np.asarray(inputs["L0_attn_x"])
    td = np.asarray(inputs["L0_attn_time_decay"])
    tf = np.asarray(inputs["L0_attn_time_first"])
    tmk = np.asarray(inputs["L0_attn_time_mix_k"])
    tmv = np.asarray(inputs["L0_attn_time_mix_v"])
    tmr = np.asarray(inputs["L0_attn_time_mix_r"])
    Wk = np.asarray(inputs["L0_attn_Wk"])
    Wv = np.asarray(inputs["L0_attn_Wv"])
    Wr = np.asarray(inputs["L0_attn_Wr"])
    Wo = np.asarray(inputs["L0_attn_Wo"])
    wg = np.asarray(inputs["L0_ffn_w_gate"])
    wu = np.asarray(inputs["L0_ffn_w_up"])
    wd = np.asarray(inputs["L0_ffn_w_down"])

    def v128(v):
        return np.ascontiguousarray(v.reshape(NCH, P).T)

    common = dict(
        td=v128(td), tf=v128(tf), tmk=v128(tmk), tmv=v128(tmv), tmr=v128(tmr),
        wkT=np.ascontiguousarray(Wk.T), wvT=np.ascontiguousarray(Wv.T),
        wrT=np.ascontiguousarray(Wr.T), woT=np.ascontiguousarray(Wo.T))

    in_maps = []
    for core in range(NCORES):
        b, half = divmod(core, 2)
        s0 = half * TOK
        xm = np.ascontiguousarray(x[b, s0:s0 + TOK])
        if s0 == 0:
            xh = np.zeros((1, C), np.float32)
        else:
            xh = np.ascontiguousarray(x[b, s0 - 1:s0])
        selv = np.zeros((1, NCORES), np.float32)
        if half == 1:
            selv[0, core - 1] = 1.0
        m = dict(common)
        m.update(x=xm, xhalo=xh, sel=selv,
                 wupp=np.ascontiguousarray(wu[core * 3 * P:(core + 1) * 3 * P]),
                 wgatep=np.ascontiguousarray(
                     wg[core * 3 * P:(core + 1) * 3 * P]),
                 wdownp=np.ascontiguousarray(wd[core * P:(core + 1) * P]))
        in_maps.append(m)

    nc = build_nc()
    res = run_bass_kernel_spmd(nc, in_maps, core_ids=list(range(NCORES)))
    out = np.empty((4, 4096, C), np.float32)
    for core in range(NCORES):
        b, half = divmod(core, 2)
        s0 = half * TOK
        out[b, s0:s0 + TOK] = np.asarray(res.results[core]["out"])
    return out


# version 41 (final)
# speedup vs baseline: 1.0157x; 1.0146x over previous
import numpy as np
import ml_dtypes
import concourse.bass as bass
import concourse.mybir as mybir
from concourse.bass_utils import run_bass_kernel_spmd

F32 = mybir.dt.float32
BF16 = mybir.dt.bfloat16
AF = mybir.ActivationFunctionType
ALU = mybir.AluOpType
AX = mybir.AxisListType

P = 128
NCH = 8
NI = 24
TOK = 2048
NQ = 4
QT = 512
NT128 = 16
C = 1024
DI = 3072
NCORES = 8
XCOL0 = 32
XST = XCOL0 + TOK
MAGIC = 1.5 * 2.0 ** 23
QMAX = 32767.0
U16S = 65536.0
EPS_LN = 1e-5
EPS_Q = 1e-12


def build_nc():
    nc = bass.Bass("TRN2", target_bir_lowering=False)
    dt = nc.dram_tensor

    x_d = dt("x", [TOK, C], F32, kind="ExternalInput").ap()
    xh_d = dt("xhalo", [1, C], F32, kind="ExternalInput").ap()
    sel_d = dt("sel", [1, NCORES], F32, kind="ExternalInput").ap()
    td_d = dt("td", [P, NCH], F32, kind="ExternalInput").ap()
    tf_d = dt("tf", [P, NCH], F32, kind="ExternalInput").ap()
    tmk_d = dt("tmk", [P, NCH], F32, kind="ExternalInput").ap()
    tmv_d = dt("tmv", [P, NCH], F32, kind="ExternalInput").ap()
    tmr_d = dt("tmr", [P, NCH], F32, kind="ExternalInput").ap()
    wkT_d = dt("wkT", [C, C], F32, kind="ExternalInput").ap()
    wvT_d = dt("wvT", [C, C], F32, kind="ExternalInput").ap()
    wrT_d = dt("wrT", [C, C], F32, kind="ExternalInput").ap()
    woT_d = dt("woT", [C, C], F32, kind="ExternalInput").ap()
    wup_d = dt("wupp", [3 * P, C], F32, kind="ExternalInput").ap()
    wgate_d = dt("wgatep", [3 * P, C], F32, kind="ExternalInput").ap()
    wdown_d = dt("wdownp", [P, DI], F32, kind="ExternalInput").ap()
    out_d = dt("out", [TOK, C], F32, kind="ExternalOutput").ap()

    wqu_t = dt("wqu", [DI, C], BF16)
    wqg_t = dt("wqg", [DI, C], BF16)
    wqd_t = dt("wqd", [C, DI], BF16)
    wqu_d, wqg_d, wqd_d = wqu_t.ap(), wqg_t.ap(), wqd_t.ap()
    dqu_t = dt("dqu", [3 * P, C], BF16)
    dqg_t = dt("dqg", [3 * P, C], BF16)
    dqd_t = dt("dqd", [P, DI], BF16)
    upspill_d = dt("upspill", [DI, TOK], BF16).ap()
    gspill_d = dt("gspill", [DI, TOK], BF16).ap()
    aspill_d = dt("aspill", [TOK, C], BF16).ap()
    halo_d = dt("halob", [1, C], BF16).ap()
    red_d = dt("redb", [1, P], F32).ap()
    mxi = [dt(f"mxi{i}", [1, 16], F32) for i in range(3)]
    mxo = [dt(f"mxo{i}", [1, 16], F32) for i in range(3)]
    agi_t = dt("agi", [P, 16], F32)
    ago_t = dt("ago", [NCORES * P, 16], F32)

    sem = {}
    for n in ("va", "ac", "po", "pe", "uev", "umx"):
        sem[n] = nc.semaphore(n).__enter__()
    cnt = {n: 0 for n in sem}
    DFAMS = ("lp", "lx0", "lx1", "lx2", "lx3", "lx4", "lx5", "lx6", "lx7",
             "ldq0", "ldq1", "ldq2", "ldq3", "lwq0", "lwq1",
             "lu0", "lu1", "lc0", "lc1", "lc2", "lc3", "lc4", "lc5", "lc6", "lc7",
             "la0", "la1", "le0", "le1", "lr0", "lr1", "lo0", "lo1",
             "sdx", "sdu", "sdg", "sdd",
             "gwk0", "gwk1", "gwk2", "gwk3", "gwv0", "gwv1", "gwv2", "gwv3",
             "gwr0", "gwr1", "gwr2", "gwr3", "gwo0", "gwo1", "gwo2", "gwo3",
             "gmi", "cc", "ccu", "ccg", "ccd")
    for n in DFAMS:
        sem[n] = nc.semaphore(n).__enter__()
        cnt[n] = 0

    def mark(ins, s, n=1):
        if s in ("va", "ac", "po"):
            return cnt[s]
        cnt[s] += n
        ins.then_inc(sem[s], n)
        return cnt[s]

    def wt(eng, s, v):
        if v > 0:
            eng.wait_ge(sem[s], v)

    def wtt(eng, mk):
        if not mk:
            return
        if isinstance(mk, list):
            for m in mk:
                wtt(eng, m)
            return
        eng.wait_ge(sem[mk[0]], mk[1])

    def dma(eng, fam, builder, inc=16):
        if cnt[fam]:
            eng.wait_ge(sem[fam], cnt[fam])
        ins = builder()
        cnt[fam] += inc
        ins.then_inc(sem[fam], inc)
        return (fam, cnt[fam])

    def tot(fam):
        return (fam, cnt[fam])

    _RAW_METHODS = {"wait_ge", "nop", "print", "load_library", "sem_inc"}
    _DMA_METHODS = {"dma_start", "dma_start_transpose", "indirect_dma_start",
                    "collective_compute"}

    class RecIns:
        __slots__ = ("hooks",)

        def __init__(self):
            self.hooks = []

        def then_inc(self, semh, n=1):
            self.hooks.append(("then_inc", semh, n))
            return self

        def _wait_ge(self, semh, v):
            self.hooks.append(("_wait_ge", semh, v))
            return self

    class Rec:
        def __init__(self, ops, fam):
            self._ops = ops
            self._fam = fam

        def __getattr__(self, name):
            fam = self._fam

            def record(*a, **k):
                ri = RecIns()
                if (fam is not None and name not in _RAW_METHODS
                        and name not in _DMA_METHODS):
                    prev = cnt[fam]
                    if prev:
                        ri.hooks.append(("_wait_ge", sem[fam], prev))
                    cnt[fam] += 1
                    ri.hooks.append(("then_inc", sem[fam], 1))
                self._ops.append((name, a, k, ri))
                return ri
            return record

    _streams = {n: [] for n in ("vector", "scalar", "gpsimd", "tensor", "sync")}
    _FAM_OF2 = {"vector": "va", "scalar": "ac", "gpsimd": "po",
                "tensor": None, "sync": None}

    def flush_streams():
        for engname, ops in _streams.items():
            if not ops:
                continue

            def body(e, ops=ops):
                for name, a, k, ri in ops:
                    ins = getattr(e, name)(*a, **k)
                    for h in ri.hooks:
                        getattr(ins, h[0])(h[1], h[2])
            getattr(blk, engname)(body)

    def sb(name, shape, dtyp, side=None):
        cm = nc.sbuf_tensor(name, shape, dtyp, side=side)
        return cm.__enter__(), cm

    def psu(name, shape):
        cm = nc.psum_tensor(name, shape, F32)
        return cm.__enter__(), cm

    blk = nc.Block().__enter__()

    def on(engine, fn):
        fn(Rec(_streams[engine], _FAM_OF2[engine]))

    XBUF, cmXBUF = sb("XBUF", [P, 2, C], F32, side="right")
    DQ2, cmDQ2 = sb("DQ2", [P, 2, C // 2], F32, side="right")
    ZSC, cmZSC = sb("ZSC", [P, C // 2], F32, side="right")
    DQBLK, cmDQB = sb("DQBLK", [P, 32], F32, side="right")
    DQBLK2, cmDQB2 = sb("DQBLK2", [P, 32], F32, side="right")
    WQP, cmWQP = sb("WQP", [P, C // 2], BF16, side="right")
    HALO, cmHALO = sb("HALO", [1, C], F32, side="right")
    XNLY, cmXNLY = sb("XNLY", [P, C], BF16)
    SC, _ = sb("SCR", [P, 2, QT], F32)
    ONES32, _ = sb("ONES32", [1, P], F32)
    MAGT, _ = sb("MAGT", [P, 1], F32)
    EPST, _ = sb("EPST", [P, 1], F32)
    TD, _ = sb("TD", [P, NCH], F32)
    TF, _ = sb("TF", [P, NCH], F32)
    TMK, _ = sb("TMK", [P, NCH], F32)
    TMV, _ = sb("TMV", [P, NCH], F32)
    TMR, _ = sb("TMR", [P, NCH], F32)
    LNW, _ = sb("LNW", [P, NCH], F32)
    EW, _ = sb("EW", [P, NCH], F32)
    EU, _ = sb("EU", [P, NCH], F32)
    LNWO, _ = sb("LNWO", [P, NCH, NQ], F32)
    STAT, _ = sb("STAT", [P, 2, 8], F32)
    XMX, _ = sb("XMX", [P, 1], F32)
    RMX, _ = sb("RMX", [P, 4], F32)
    RROW, _ = sb("RROW", [1, P], F32)
    CCS, _ = sb("CCS", [1, 16], F32)
    SCAL, _ = sb("SCAL", [1, 16], F32)
    BCAST, _ = sb("BCAST", [P, 16], F32)
    CARA, _ = sb("CARA", [P, NCH], F32)
    CARB, _ = sb("CARB", [P, NCH], F32)
    STOUT, _ = sb("STOUT", [P, 16], F32)
    GST, _ = sb("GST", [P, NCORES, 16], F32)
    ASEL, _ = sb("ASEL", [P, 16], F32)
    SELB, _ = sb("SELB", [P, NCORES], F32)
    HSTA, _ = sb("HSTA", [1, 8], F32)
    HNL, _ = sb("HNL", [1, C], BF16)
    WTT, _ = sb("WTT", [P, 2, NCH, P], BF16)

    PB = []
    for i in range(4):
        h, _ = psu(f"PB{i}", [P, QT])
        PB.append(h)
    PEP = []
    for i in range(2):
        h, _ = psu(f"PEP{i}", [P, QT])
        PEP.append(h)
    PBC, _ = psu("PBC", [P, QT])
    PIOT, _ = psu("PIOT", [P, QT])
    DUMT, _ = sb("DUMT", [1, 2], F32)


    ld_x = {}
    ac_xn = {}
    sd_xp = {}
    ld_w = {}
    ac_wcast = {}
    dq_ld = {}
    dq_done = {}
    dq_free = {}
    po_z = {}
    dq_state = {"idx": 0, "u": 0, "g": 0, "d": 0}
    bank_free = [("va", 0)] * 4
    bank_state = {"i": 0}
    ep_bank_free = [("va", 0), ("va", 0)]
    ech_spill = {0: 0, 1: 0}
    atch_spill = {0: 0, 1: 0}
    uch_free = {0: 0, 1: 0}
    xch_free = {0: 0, 1: 0}
    och_free = {0: 0, 1: 0}
    atl_free = {0: 0, 1: 0}
    pe_up = {}
    va_nmr_prev = [0]

    def next_bank():
        b = bank_state["i"] % 4
        bank_state["i"] += 1
        return b

    def st0_sync(s):
        for dsttile, src in ((TD, td_d), (TF, tf_d), (TMK, tmk_d),
                             (TMV, tmv_d), (TMR, tmr_d)):
            dma(s, "lp", lambda d=dsttile, r=src: s.dma_start(out=d[:, :], in_=r[:, :]))
        dma(s, "lp", lambda: s.dma_start(out=HALO[:, :], in_=xh_d[:, :]))
        dma(s, "lp", lambda: s.dma_start(out=SELB[0:1, :], in_=sel_d[:, :]))
    on("sync", st0_sync)
    ld_params = tot("lp")

    def st0_vec(v):
        v.memset(DUMT[:, :], 1.0)
        v.memset(ONES32[:, :], 1.0)
        v.memset(MAGT[:, :], MAGIC)
        v.memset(EPST[:, :], EPS_LN)
        v.memset(ASEL[:, :], 0.0)
        v.memset(XMX[:, :], 0.0)
        v.memset(RMX[:, :], 0.0)
        mark(v.memset(CCS[:, :], 0.0), "va")
    on("vector", st0_vec)
    va_const = cnt["va"]

    def st0_act(s):
        wtt(s, ld_params)
        mark(s.activation(LNW[:, :], TD[:, :], AF.Exp), "ac")
    on("scalar", st0_act)
    ac_l0 = cnt["ac"]

    def st0_vec3(v):
        wt(v, "ac", ac_l0)
        mark(v.tensor_scalar(LNW[:, :], LNW[:, :], -1.0, None, ALU.mult), "va")
    on("vector", st0_vec3)
    va_lnw = cnt["va"]

    def st0_act2(s):
        wt(s, "va", va_lnw)
        s.activation(EW[:, :], LNW[:, :], AF.Exp)
        mark(s.activation(EU[:, :], TF[:, :], AF.Exp), "ac")
    on("scalar", st0_act2)
    ac_ewu = cnt["ac"]

    def st0_vec4(v):
        wt(v, "ac", ac_ewu)
        for q in range(NQ):
            v.tensor_scalar(LNWO[:, :, q], LNW[:, :], float(q * QT), None, ALU.mult)
        v.tensor_copy(CARA[:, :], ASEL[:, 0:NCH])
        mark(v.tensor_copy(CARB[:, :], ASEL[:, 0:NCH]), "va")
    on("vector", st0_vec4)
    va_init = cnt["va"]

    def halo_v(v):
        wtt(v, ld_params)
        mark(v.reduce_sum(HSTA[0:1, 0:1], HALO[0:1, :], AX.X), "va")
    on("vector", halo_v)
    va_hs = cnt["va"]

    def halo_v2(v):
        wt(v, "va", va_hs)
        v.scalar_tensor_tensor(SC[0:1, 1, 0:QT], HALO[0:1, 0:QT], 1.0,
                               HALO[0:1, 0:QT], ALU.bypass, ALU.mult,
                               accum_out=HSTA[0:1, 1:2])
        v.scalar_tensor_tensor(SC[0:1, 1, 0:QT], HALO[0:1, QT:C], 1.0,
                               HALO[0:1, QT:C], ALU.bypass, ALU.mult,
                               accum_out=HSTA[0:1, 6:7])
        v.tensor_tensor(HSTA[0:1, 1:2], HSTA[0:1, 1:2], HSTA[0:1, 6:7], ALU.add)
        v.tensor_scalar(HSTA[0:1, 2:3], HSTA[0:1, 0:1], 1.0 / C, None, ALU.mult)
        v.tensor_scalar(HSTA[0:1, 3:4], HSTA[0:1, 1:2], 1.0 / C, None, ALU.mult)
        v.tensor_tensor(HSTA[0:1, 4:5], HSTA[0:1, 2:3], HSTA[0:1, 2:3], ALU.mult)
        mark(v.tensor_tensor(HSTA[0:1, 5:6], HSTA[0:1, 3:4], HSTA[0:1, 4:5],
                             ALU.subtract), "va")
    on("vector", halo_v2)
    va_hv = cnt["va"]

    def halo_a2(a):
        wt(a, "va", va_hv)
        a.activation(HSTA[0:1, 6:7], HSTA[0:1, 5:6], AF.Ln, bias=EPST[0:1, 0:1])
        mark(a.activation(HSTA[0:1, 7:8], HSTA[0:1, 6:7], AF.Exp, scale=-0.5),
             "ac")
    on("scalar", halo_a2)
    ac_hr = cnt["ac"]

    def halo_v3(v):
        wt(v, "ac", ac_hr)
        mark(v.scalar_tensor_tensor(HSTA[0:1, 4:5], HSTA[0:1, 2:3], -1.0,
                                    HSTA[0:1, 7:8], ALU.mult, ALU.mult), "va")
    on("vector", halo_v3)
    va_hn = cnt["va"]

    def halo_v4(v):
        wt(v, "va", va_hn)
        v.scalar_tensor_tensor(HNL[0:1, :], HALO[0:1, :], HSTA[0:1, 7:8],
                               HSTA[0:1, 4:5].broadcast_to([1, C]),
                               ALU.mult, ALU.add)
    on("vector", halo_v4)
    ac_hnl = cnt["va"]

    cmHALO.__exit__(None, None, None)

    XNF, cmXNF = sb("XNF", [P, NCH, XST], BF16)
    NB, cmNB = sb("NB", [P, NCH, TOK], BF16)
    DB, cmDB = sb("DB", [P, NCH, TOK], BF16)
    DIFQ, cmDIFQ = sb("DIFQ", [P, NCH, QT], BF16)
    XMQ, cmXMQ = sb("XMQ", [P, NCH, QT], BF16)
    XM2Q, cmXM2Q = sb("XM2Q", [P, NCH, QT], BF16)
    WKT, cmWKT = sb("WKT", [P, NCH, C], BF16, side="right")
    WVT, cmWVT = sb("WVT", [P, NCH, C], BF16, side="right")
    EKQ, cmEKQ = sb("EKQ", [P, NCH, QT], BF16, side="right")
    EKVQ, cmEKVQ = sb("EKVQ", [P, NCH, QT], BF16, side="right")
    SA, cmSA = sb("SA", [P, 2, QT + 2], BF16, side="right")
    SB_, cmSB = sb("SB", [P, 2, QT + 2], BF16, side="right")

    xt = x_d.rearrange("(n p) c -> n p c", p=P)

    def load_xtile(k, wait_ac=0):
        def f(a, k=k, wait_ac=wait_ac):
            wt(a, "va", wait_ac)
            ld_x[k] = dma(a, f"lx{k % 2}",
                          lambda: a.dma_start(out=XBUF[:, k % 2, :],
                                              in_=xt[k, :, :]))
        on("scalar", f)

    def load_wtile(tag, src, j, dsttile, wait_pe=0, wait_va=0):
        def f(g, tag=tag, src=src, j=j, dsttile=dsttile, wait_pe=wait_pe,
              wait_va=wait_va):
            if wait_pe and j == 0:
                wt(g, "pe", wait_pe)
            if wait_va and j == 0:
                wt(g, "va", wait_va)
            ld_w[(tag, j)] = dma(
                g, "gw" + tag + str(j % 4),
                lambda: g.dma_start(out=dsttile[:, j, :],
                                    in_=src[j * P:(j + 1) * P, :]))
        on("gpsimd", f)

    gcc = {}

    def dequant_piece(tag):
        C2 = C // 2
        if tag == "u":
            r = dq_state["u"]; dq_state["u"] += 1
            src_full = wup_d[r * P:(r + 1) * P, :]
            dst_full = dqu_t.ap()[r * P:(r + 1) * P, :]
        elif tag == "g":
            r = dq_state["g"]; dq_state["g"] += 1
            src_full = wgate_d[r * P:(r + 1) * P, :]
            dst_full = dqg_t.ap()[r * P:(r + 1) * P, :]
        else:
            cp = dq_state["d"]; dq_state["d"] += 1
            src_full = wdown_d[:, cp * C:(cp + 1) * C]
            dst_full = dqd_t.ap()[:, cp * C:(cp + 1) * C]

        for h in range(2):
            i = dq_state["idx"]
            dq_state["idx"] += 1
            sl = i % 2
            src_ap = src_full[:, h * C2:(h + 1) * C2]
            dst_ap = dst_full[:, h * C2:(h + 1) * C2]

            def f(s, i=i, sl=sl, src_ap=src_ap):
                wt(s, "va", dq_free.get(i - 2, 0))
                dq_ld[i] = dma(s, f"ldq{i % 4}",
                               lambda: s.dma_start(out=DQ2[:, sl, :],
                                                   in_=src_ap))
            on("sync", f)

            def g(v, i=i, sl=sl):
                wtt(v, dq_ld[i])
                v.reduce_max(DQBLK[:, 0:16, None],
                             DQ2[:, sl, :].rearrange("p (b e) -> p b e", e=32),
                             AX.X, apply_absolute_value=True)
                v.tensor_scalar(DQBLK2[:, 0:16], DQBLK[:, 0:16], 1.0 / 7.0,
                                EPS_Q, ALU.mult, ALU.max)
                v.reciprocal(DQBLK[:, 0:16], DQBLK2[:, 0:16])
                v.tensor_tensor(ZSC[:, :].rearrange("p (b e) -> p b e", e=32),
                                DQ2[:, sl, :].rearrange("p (b e) -> p b e",
                                                        e=32),
                                DQBLK[:, 0:16, None].broadcast_to([P, 16, 32]),
                                ALU.mult)
                v.tensor_scalar(ZSC[:, :], ZSC[:, :], 1.0, MAGIC,
                                ALU.mult, ALU.add)
                wtt(v, dq_done.get(i - 1, 0))
                dq_free[i] = mark(
                    v.scalar_tensor_tensor(
                        WQP[:, :].rearrange("p (b e) -> p b e", e=32),
                        ZSC[:, :].rearrange("p (b e) -> p b e", e=32), -MAGIC,
                        DQBLK2[:, 0:16, None].broadcast_to([P, 16, 32]),
                        ALU.add, ALU.mult), "va")
            on("vector", g)

            def f2(s, i=i, sl=sl, dst_ap=dst_ap):
                wt(s, "va", dq_free[i])
                dq_done[i] = dma(s, f"lwq{sl}",
                                 lambda: s.dma_start(out=dst_ap,
                                                     in_=WQP[:, :]))
            on("sync", f2)

    load_xtile(0)
    load_xtile(1)

    def ln_stats(k):
        sl = k % 2
        scol = STAT[:, sl, :]

        def ln_v1(v, k=k, sl=sl, scol=scol):
            wtt(v, ld_x[k])
            if k >= 2:
                wt(v, "va", ac_xn[k - 2])
            mark(v.reduce_sum(scol[:, 0:1], XBUF[:, sl, :], AX.X), "va")
        on("vector", ln_v1)

        def ln_v2(v, k=k, sl=sl, scol=scol):
            wtt(v, ld_x[k])
            v.scalar_tensor_tensor(SC[:, 0, 0:QT], XBUF[:, sl, 0:QT], 1.0,
                                   XBUF[:, sl, 0:QT], ALU.bypass, ALU.mult,
                                   accum_out=scol[:, 1:2])
            v.scalar_tensor_tensor(SC[:, 0, 0:QT], XBUF[:, sl, QT:C], 1.0,
                                   XBUF[:, sl, QT:C], ALU.bypass, ALU.mult,
                                   accum_out=scol[:, 6:7])
            v.tensor_tensor(scol[:, 1:2], scol[:, 1:2], scol[:, 6:7], ALU.add)
            v.tensor_scalar(scol[:, 2:3], scol[:, 0:1], 1.0 / C, None,
                            ALU.mult)
            v.tensor_scalar(scol[:, 3:4], scol[:, 1:2], 1.0 / C, None,
                            ALU.mult)
            v.tensor_tensor(scol[:, 4:5], scol[:, 2:3], scol[:, 2:3], ALU.mult)
            mark(v.tensor_tensor(scol[:, 5:6], scol[:, 3:4], scol[:, 4:5],
                                 ALU.subtract), "va")
        on("vector", ln_v2)
        va_var = cnt["va"]

        def ln_a2(a, scol=scol, va_var=va_var):
            wt(a, "va", va_var)
            a.activation(scol[:, 6:7], scol[:, 5:6], AF.Ln, bias=EPST[:, 0:1])
            mark(a.activation(scol[:, 7:8], scol[:, 6:7], AF.Exp, scale=-0.5),
                 "ac")
        on("scalar", ln_a2)
        ac_rstd[k] = cnt["ac"]

    def ln_norm(k):
        sl = k % 2
        scol = STAT[:, sl, :]

        def ln_v3(v, scol=scol, ac_r=ac_rstd[k]):
            wt(v, "ac", ac_r)
            mark(v.scalar_tensor_tensor(scol[:, 4:5], scol[:, 2:3], -1.0,
                                        scol[:, 7:8], ALU.mult, ALU.mult),
                 "va")
        on("vector", ln_v3)

        def ln_v3b(v, k=k, sl=sl, scol=scol):
            if k >= 1:
                wtt(v, sd_xp[k - 1])
            v.scalar_tensor_tensor(XNLY[:, :], XBUF[:, sl, :], scol[:, 7:8],
                                   scol[:, 4:5].broadcast_to([P, C]),
                                   ALU.mult, ALU.add)
            ac_xn[k] = cnt["va"]
        on("vector", ln_v3b)

        def ln_v4(v, k=k):
            v.reduce_max(RMX[:, 2:3], XNLY[:, :], AX.X,
                         apply_absolute_value=True)
            mark(v.tensor_tensor(XMX[:, 0:1], XMX[:, 0:1], RMX[:, 2:3],
                                 ALU.max), "va")
        on("vector", ln_v4)

        def ln_xp(s, k=k):
            wt(s, "va", ac_xn[k])
            sd_xp[k] = dma(s, "sdx",
                           lambda: s.dma_start_transpose(
                               out=XNF[:, :, XCOL0 + 128 * k:
                                     XCOL0 + 128 * (k + 1)],
                               in_=XNLY[:, :]))
        on("scalar", ln_xp)

        if k + 2 < NT128:
            load_xtile(k + 2, wait_ac=ac_xn[k])

        if k < NCH:
            load_wtile("k", wkT_d, k, WKT, wait_va=ac_hnl)
        else:
            load_wtile("v", wvT_d, k - NCH, WVT)

        if 4 <= k <= 12:
            dequant_piece(("u", "u", "u", "g", "g", "g", "d", "d", "d")[k - 4])

    ac_rstd = {}
    for k in range(NT128 + 1):
        if k < NT128:
            ln_stats(k)
        if k >= 1:
            ln_norm(k - 1)

    va_xmx_final = cnt["va"]


    def halo_g(g):
        wt(g, "va", ac_hnl)
        dma(g, "gmi", lambda: g.dma_start(out=halo_d[:, :], in_=HNL[:, :]))
        dma(g, "gmi", lambda: g.dma_start(
            out=XNF[:, :, XCOL0 - 1:XCOL0],
            in_=halo_d[0:1, :].rearrange("a (m p) -> p m a", p=P),
            allow_slow_non_contiguous=True))
    on("gpsimd", halo_g)
    gd_halo = tot("gmi")

    def c1_g(g):
        wt(g, "va", va_xmx_final)
        dma(g, "gmi", lambda: g.dma_start(
            out=red_d[0:1, :].rearrange("a p -> p a"),
            in_=XMX[:, 0:1], allow_slow_non_contiguous=True))
        dma(g, "gmi", lambda: g.dma_start(out=RROW[:, :], in_=red_d[:, :]))
    on("gpsimd", c1_g)
    gd_rrow = tot("gmi")

    def c1_v(v):
        wtt(v, gd_rrow)
        mark(v.reduce_max(CCS[0:1, 0:1], RROW[0:1, :], AX.X,
                          apply_absolute_value=True), "va")
    on("vector", c1_v)
    va_ccs1 = cnt["va"]

    def c1_g2(g):
        wt(g, "va", va_ccs1)
        wtt(g, dma(g, "gmi", lambda: g.dma_start(out=mxi[0].ap(),
                                                 in_=CCS[:, :])))
        dma(g, "cc", lambda: g.collective_compute(
            "AllReduce", ALU.max, replica_groups=[list(range(NCORES))],
            ins=[mxi[0].ap().opt()], outs=[mxo[0].ap().opt()]), inc=1)
        wtt(g, tot("cc"))
        dma(g, "gmi", lambda: g.dma_start(out=SCAL[0:1, 0:1],
                                          in_=mxo[0].ap()[0:1, 0:1]))
    on("gpsimd", c1_g2)
    gd_s1 = tot("gmi")

    def wq_cc(g):
        wtt(g, [dq_done[k] for k in range(0, 6)])
        dma(g, "ccu", lambda: g.collective_compute(
            "AllGather", ALU.bypass, replica_groups=[list(range(NCORES))],
            ins=[dqu_t.ap().opt()], outs=[wqu_t.ap().opt()]), inc=1)
        gcc["u"] = tot("ccu")
        wtt(g, [dq_done[k] for k in range(6, 12)])
        dma(g, "ccg", lambda: g.collective_compute(
            "AllGather", ALU.bypass, replica_groups=[list(range(NCORES))],
            ins=[dqg_t.ap().opt()], outs=[wqg_t.ap().opt()]), inc=1)
        gcc["g"] = tot("ccg")
        wtt(g, [dq_done[k] for k in range(12, 18)])
        dma(g, "ccd", lambda: g.collective_compute(
            "AllGather", ALU.bypass, replica_groups=[list(range(NCORES))],
            ins=[dqd_t.ap().opt()], outs=[wqd_t.ap().opt()]), inc=1)
        gcc["d"] = tot("ccd")
    on("gpsimd", wq_cc)

    def c1_v2(v):
        wtt(v, gd_s1)
        v.tensor_scalar(SCAL[0:1, 1:2], SCAL[0:1, 0:1], 1.0 / QMAX, EPS_Q,
                        ALU.mult, ALU.max)
        mark(v.reciprocal(SCAL[0:1, 2:3], SCAL[0:1, 1:2]), "va")
    on("vector", c1_v2)
    va_s1 = cnt["va"]

    def c1_pe(t):
        wt(t, "va", va_s1)
        mark(t.matmul(PBC[:, 0:2], ONES32[:, :], SCAL[0:1, 1:3],
                      start=True, stop=True), "pe")
    on("tensor", c1_pe)
    pe_b1 = cnt["pe"]

    def c1_v3(v):
        wt(v, "pe", pe_b1)
        v.tensor_copy(BCAST[:, 1:2], PBC[:, 0:1])
        mark(v.tensor_copy(BCAST[:, 0:1], PBC[:, 1:2]), "va")
    on("vector", c1_v3)
    va_b1 = cnt["va"]

    def x_cols(q):
        return (XCOL0 + q * QT, XCOL0 + (q + 1) * QT)

    def xx_cols(q):
        return (XCOL0 - 1 + q * QT, XCOL0 - 1 + (q + 1) * QT)

    va_scan = {}
    po_nbdb = {}
    nb_prev = [None]
    nb_prev2 = [None]
    ac_ek_m = {}
    pe_kv_last = [0]

    for q in range(NQ):
        a0, a1 = x_cols(q)
        b0, b1 = xx_cols(q)

        def mix_k(v, q=q, a0=a0, a1=a1, b0=b0, b1=b1,
                  pe_prev=pe_kv_last[0]):
            kmax = (q * QT + QT) // 128 - 1
            wtt(v, sd_xp[kmax])
            wt(v, "pe", pe_prev)
            if q == 0:
                wtt(v, gd_halo)
                wt(v, "va", va_init)
            v.tensor_tensor(DIFQ[:, :, :], XNF[:, :, a0:a1], XNF[:, :, b0:b1],
                            ALU.subtract)
            for j in range(NCH):
                v.tensor_scalar(XMQ[:, j, :], DIFQ[:, j, :], TMK[:, j:j + 1],
                                None, ALU.mult)
                v.tensor_tensor(XMQ[:, j, :], XMQ[:, j, :], XNF[:, j, b0:b1],
                                ALU.add)
            mark(v.engine_nop(), "va")
        on("vector", mix_k)
        va_mixk = cnt["va"]

        def mix_v2(v, q=q, a0=a0, a1=a1, b0=b0, b1=b1,
                   pe_prev2=pe_kv_last[0]):
            wt(v, "pe", pe_prev2)
            for j in range(NCH):
                v.tensor_scalar(XM2Q[:, j, :], DIFQ[:, j, :], TMV[:, j:j + 1],
                                None, ALU.mult)
                v.tensor_tensor(XM2Q[:, j, :], XM2Q[:, j, :], XNF[:, j, b0:b1],
                                ALU.add)
            mark(v.engine_nop(), "va")
        on("vector", mix_v2)
        va_mixv = cnt["va"]

        kb = {}
        for m in range(NCH):
            bk = next_bank()
            kb[m] = bk

            def k_mm(t, m=m, bk=bk, va_mixk=va_mixk, fk=bank_free[bk]):
                wt(t, "va", va_mixk)
                wtt(t, [ld_w[("k", jj)] for jj in range(NCH - 4, NCH)])
                wt(t, *fk)
                for j in range(NCH):
                    ins = t.matmul(PB[bk][:, :], WKT[:, j, 128 * m:128 * (m + 1)],
                             XMQ[:, j, :], start=(j == 0), stop=(j == NCH - 1))
                    if j == NCH - 1:
                        mark(ins, "pe")
            on("tensor", k_mm)
            pe_k = cnt["pe"]

            def k_ev(a, m=m, bk=bk, pe_k=pe_k,
                     w_po=po_nbdb.get((q - 1, m), 0) if q else 0):
                wt(a, "pe", pe_k)
                wt(a, "po", w_po)
                mark(a.activation(EKQ[:, m, :], PB[bk][:, :], AF.Exp), "ac")
            on("scalar", k_ev)
            bank_free[bk] = ("ac", cnt["ac"])
            ac_ek_m[m] = cnt["ac"]
        ac_ek_all = cnt["ac"]
        pe_k_all = cnt["pe"]

        for m in range(NCH):
            bv = next_bank()

            def v_mm(t, m=m, bv=bv, va_mixv=va_mixv, fv=bank_free[bv]):
                wt(t, "va", va_mixv)
                wtt(t, [ld_w[("v", jj)] for jj in range(NCH - 4, NCH)])
                wt(t, *fv)
                for j in range(NCH):
                    ins = t.matmul(PB[bv][:, :], WVT[:, j, 128 * m:128 * (m + 1)],
                             XM2Q[:, j, :], start=(j == 0), stop=(j == NCH - 1))
                    if j == NCH - 1:
                        mark(ins, "pe")
            on("tensor", v_mm)
            pe_v = cnt["pe"]

            ssl = (q * NCH + m) % 2

            def scan_v(v, q=q, m=m, bv=bv, ssl=ssl, pe_v=pe_v,
                       ac_ek=ac_ek_m[m],
                       w_po=po_nbdb.get(nb_prev2[0], 0)):
                wt(v, "ac", ac_ek)
                wt(v, "pe", pe_v)
                wt(v, "po", w_po)
                v.tensor_tensor(EKVQ[:, m, :], EKQ[:, m, :], PB[bv][:, :],
                                ALU.mult)
                v.tensor_copy(SA[:, ssl, 0:1], CARA[:, m:m + 1])
                v.tensor_copy(SB_[:, ssl, 0:1], CARB[:, m:m + 1])
                v.tensor_tensor_scan(SA[:, ssl, 1:QT + 1],
                                     EW[:, m:m + 1].broadcast_to([P, QT]),
                                     EKVQ[:, m, :], CARA[:, m:m + 1],
                                     ALU.mult, ALU.add)
                v.tensor_tensor_scan(SB_[:, ssl, 1:QT + 1],
                                     EW[:, m:m + 1].broadcast_to([P, QT]),
                                     EKQ[:, m, :], CARB[:, m:m + 1],
                                     ALU.mult, ALU.add)
                v.tensor_copy(CARA[:, m:m + 1], SA[:, ssl, QT:QT + 1])
                v.tensor_copy(CARB[:, m:m + 1], SB_[:, ssl, QT:QT + 1])
                v.scalar_tensor_tensor(NB[:, m, q * QT:(q + 1) * QT],
                                       EKVQ[:, m, :], EU[:, m:m + 1],
                                       SA[:, ssl, 0:QT], ALU.mult, ALU.add)
                mark(v.scalar_tensor_tensor(DB[:, m, q * QT:(q + 1) * QT],
                                            EKQ[:, m, :], EU[:, m:m + 1],
                                            SB_[:, ssl, 0:QT],
                                            ALU.mult, ALU.add), "va")
            on("vector", scan_v)
            va_scan[(q, m)] = cnt["va"]
            bank_free[bv] = ("va", cnt["va"])
            po_nbdb[(q, m)] = 0
            nb_prev2[0] = nb_prev[0]
            nb_prev[0] = (q, m)
        pe_kv_last[0] = cnt["pe"]

    va_scan_done = cnt["va"]

    def exp_v(v):
        wt(v, "va", va_scan_done)
        v.tensor_copy(STOUT[:, 0:NCH], CARA[:, :])
        mark(v.tensor_copy(STOUT[:, NCH:16], CARB[:, :]), "va")
    on("vector", exp_v)
    va_stout = cnt["va"]

    def exp_g(g):
        wt(g, "va", va_stout)
        wtt(g, dma(g, "gmi", lambda: g.dma_start(out=agi_t.ap(),
                                                 in_=STOUT[:, :])))
        dma(g, "cc", lambda: g.collective_compute(
            "AllGather", ALU.bypass, replica_groups=[list(range(NCORES))],
            ins=[agi_t.ap().opt()], outs=[ago_t.ap().opt()]), inc=1)
        wtt(g, tot("cc"))
        dma(g, "gmi", lambda: g.dma_start(
            out=GST[:, :, :],
            in_=ago_t.ap().rearrange("(r p) c -> p r c", p=P)))
    on("gpsimd", exp_g)
    gd_gst = tot("gmi")

    for cm in (cmXM2Q, cmSB, cmSA, cmEKVQ, cmEKQ, cmWVT, cmWKT,
               cmWQP, cmDQB2, cmDQB, cmZSC, cmDQ2, cmXBUF):
        cm.__exit__(None, None, None)

    SR, cmSR = sb("SR", [P, NCH, TOK], BF16, side="right")
    RWQ, cmRWQ = sb("RWQ", [P, NCH, QT], BF16, side="right")
    PCHT, cmPCHT = sb("PCHT", [P, QT], BF16, side="right")
    RCHT, cmRCHT = sb("RCHT", [P, QT], BF16, side="right")
    ECH, cmECH = sb("ECH", [P, 2, QT], BF16, side="right")
    WRT, cmWRT = sb("WRT", [P, NCH, C], BF16, side="right")

    pe_kv_done = pe_kv_last[0]

    pe_rq_last = [pe_kv_done]

    def rw_guard(g):
        wtt(g, dq_done[17])
    on("gpsimd", rw_guard)
    for j in range(NCH):
        load_wtile("r", wrT_d, j, WRT, wait_pe=pe_kv_done)

    for q in range(NQ):
        a0, a1 = x_cols(q)
        b0, b1 = xx_cols(q)

        def rmix_v(v, a0=a0, a1=a1, b0=b0, b1=b1, pe_prev=pe_rq_last[0]):
            wt(v, "pe", pe_prev)
            v.tensor_tensor(DIFQ[:, :, :], XNF[:, :, a0:a1], XNF[:, :, b0:b1],
                            ALU.subtract)
            for j in range(NCH):
                v.tensor_scalar(XMQ[:, j, :], DIFQ[:, j, :], TMR[:, j:j + 1],
                                None, ALU.mult)
                v.tensor_tensor(XMQ[:, j, :], XMQ[:, j, :], XNF[:, j, b0:b1],
                                ALU.add)
            mark(v.engine_nop(), "va")
        on("vector", rmix_v)
        va_rmix = cnt["va"]

        for m in range(NCH):
            br = next_bank()

            def r_mm(t, q=q, m=m, br=br, va_rmix=va_rmix, fb=bank_free[br]):
                wt(t, "va", va_rmix)
                wtt(t, [ld_w[("r", jj)] for jj in range(NCH - 4, NCH)])
                wt(t, *fb)
                for j in range(NCH):
                    ins = t.matmul(PB[br][:, :], WRT[:, j, 128 * m:128 * (m + 1)],
                             XMQ[:, j, :], start=(j == 0), stop=(j == NCH - 1))
                    if j == NCH - 1:
                        mark(ins, "pe")
            on("tensor", r_mm)
            pe_r = cnt["pe"]

            def r_ev(a, q=q, m=m, br=br, pe_r=pe_r):
                wt(a, "pe", pe_r)
                mark(a.activation(SR[:, m, q * QT:(q + 1) * QT], PB[br][:, :],
                                  AF.Sigmoid), "ac")
            on("scalar", r_ev)
            bank_free[br] = ("ac", cnt["ac"])
            pe_rq_last[0] = pe_r

    ac_sr_done = cnt["ac"]
    pe_r_done = cnt["pe"]
    va_rmix_done = cnt["va"]

    cmWRT.__exit__(None, None, None)
    cmXMQ.__exit__(None, None, None)
    cmDIFQ.__exit__(None, None, None)
    WOT, cmWOT = sb("WOT", [P, NCH, C], BF16, side="right")
    PCHTB, cmPCHTB = sb("PCHTB", [P, QT], BF16, side="right")
    RCHTB, cmRCHTB = sb("RCHTB", [P, QT], BF16, side="right")
    UMAXR, cmUMAXR = sb("UMAXR", [P, QT], BF16, side="right")
    UMINR, cmUMINR = sb("UMINR", [P, QT], BF16, side="right")
    GMAXR, cmGMAXR = sb("GMAXR", [P, QT], BF16, side="right")
    GMINR, cmGMINR = sb("GMINR", [P, QT], BF16, side="right")
    OPCH, cmOPCH = sb("OPCH", [P, 2, QT], BF16, side="right")

    def umx_v(v):
        v.memset(UMAXR[:, :], 0.0)
        v.memset(UMINR[:, :], 0.0)
        v.memset(GMAXR[:, :], 0.0)
        mark(v.memset(GMINR[:, :], 0.0), "va")
    on("vector", umx_v)
    va_umx0 = cnt["va"]
    for j in range(NCH):
        load_wtile("o", woT_d, j, WOT, wait_pe=pe_r_done)
    gd_wot = [ld_w[("o", jj)] for jj in range(NCH - 4, NCH)]

    def rio_g(g):
        wt(g, "va", va_rmix_done)
        mark(g.iota(SC[0:1, 0, 0:QT], [[1, QT]], channel_multiplier=0,
                    allow_small_or_imprecise_dtypes=True), "po")
    on("gpsimd", rio_g)
    po_iota = cnt["po"]

    def rio_pe(t):
        wt(t, "po", po_iota)
        mark(t.matmul(PIOT[:, 0:QT], ONES32[:, :], SC[0:1, 0, 0:QT],
                      start=True, stop=True), "pe")
    on("tensor", rio_pe)
    pe_iota2 = cnt["pe"]

    ac_xq = {}
    va_xq = {}
    for jj in range(NQ * NCH):
        j, qq = divmod(jj, NQ)
        c0 = XCOL0 + qq * QT

        def xq_a(p, jj=jj, j=j, c0=c0):
            if jj == 0:
                wt(p, "va", va_b1)
                wt(p, "va", va_rmix_done)
                wt(p, "pe", pe_iota2)
            wt(p, "va", va_xq.get(jj - 2, 0))
            ac_xq[jj] = mark(
                p.tensor_scalar(SC[:, jj % 2, :], XNF[:, j, c0:c0 + QT],
                                BCAST[:, 0:1], MAGIC, ALU.mult, ALU.add), "po")
        on("gpsimd", xq_a)

        def xq_v(v, jj=jj, j=j, c0=c0):
            wt(v, "po", ac_xq[jj])
            va_xq[jj] = mark(
                v.tensor_scalar(XNF[:, j, c0:c0 + QT], SC[:, jj % 2, :],
                                -MAGIC, BCAST[:, 1:2], ALU.add, ALU.mult), "va")
        on("vector", xq_v)
    va_xq_done = cnt["va"]

    sd_upt = {}
    va_mx = {}
    ac_ev = {}
    gsp_done = {}

    def emit_up_step(step):
        ii, q = divmod(step, NQ)
        up = ii < NI
        i = ii if up else ii - NI
        sl = ii % 2
        if q == 0:
            def up_xp(s, ii=ii, i=i, up=up, sl=sl):
                wtt(s, gcc["u" if up else "g"])
                wt(s, "pe", pe_up.get(ii - 2, 0))
                sd_upt[ii] = dma(
                    s, "sdu",
                    lambda: s.dma_start_transpose(
                        out=WTT[:, sl, :, :],
                        in_=(wqu_d if up else wqg_d)[i * P:(i + 1) * P, :]))
            on("sync", up_xp)
        bu = next_bank()
        esl = step % 2

        def up_mm(t, ii=ii, sl=sl, q=q, bu=bu, fb=bank_free[bu]):
            wtt(t, sd_upt[ii])
            wt(t, "va", va_xq_done)
            wt(t, *fb)
            for j in range(NCH):
                ins = t.matmul(PB[bu][:, :], WTT[:, sl, j, :],
                         XNF[:, j, XCOL0 + q * QT:XCOL0 + (q + 1) * QT],
                         start=(j == 0), stop=(j == NCH - 1))
                if j == NCH - 1:
                    mark(ins, "pe")
        on("tensor", up_mm)
        pe_m = cnt["pe"]
        if q == NQ - 1:
            pe_up[ii] = pe_m

        def up_ev(a, bu=bu, esl=esl, pe_m=pe_m, w_sp=ech_spill[esl],
                  w_mx=va_mx.get(step - 2, 0)):
            wt(a, "pe", pe_m)
            wtt(a, w_sp)
            wt(a, "va", w_mx)
            ac_ev[step] = mark(
                a.activation(ECH[:, esl, :], PB[bu][:, :], AF.Identity), "ac")
        on("scalar", up_ev)
        bank_free[bu] = ("ac", ac_ev[step])

        def up_mx(v, up=up, esl=esl, step=step, first=(step == 0)):
            if first:
                wt(v, "va", va_umx0)
            wt(v, "ac", ac_ev[step])
            mxr, mnr = (UMAXR, UMINR) if up else (GMAXR, GMINR)
            v.tensor_tensor(mxr[:, :], mxr[:, :], ECH[:, esl, :], ALU.max)
            va_mx[step] = mark(
                v.tensor_tensor(mnr[:, :], mnr[:, :], ECH[:, esl, :],
                                ALU.min), "va")
        on("vector", up_mx)

        def up_sp(g, up=up, i=i, q=q, esl=esl, step=step):
            wt(g, "ac", ac_ev[step])
            dst = upspill_d if up else gspill_d
            ech_spill[esl] = dma(g, f"lu{esl}", lambda: g.dma_start(
                out=dst[i * P:(i + 1) * P, q * QT:(q + 1) * QT],
                in_=ECH[:, esl, :]))
        on("gpsimd", up_sp)
        if (not up) and q == NQ - 1:
            gsp_done[i] = [tot("lu0"), tot("lu1")]

    def sel_pe(t):
        wtt(t, ld_params)
        wt(t, "va", va_b1)
        mark(t.matmul(PBC[:, 0:NCORES], ONES32[:, :], SELB[0:1, :],
                      start=True, stop=True), "pe")
    on("tensor", sel_pe)
    pe_selb = cnt["pe"]

    def sel_v(v):
        wt(v, "pe", pe_selb)
        v.tensor_copy(SELB[:, :], PBC[:, 0:NCORES])
        wtt(v, gd_gst)
        for r in range(NCORES):
            v.scalar_tensor_tensor(ASEL[:, :], GST[:, r, :], SELB[:, r:r + 1],
                                   ASEL[:, :], ALU.mult, ALU.add)
        mark(v.engine_nop(), "va")
    on("vector", sel_v)
    va_asel = cnt["va"]

    va_rw = {}
    va_pm_l = {}
    va_fix_l = {}
    ac_p_l = {}
    ac_rc_l = {}
    NTOT = NQ * NCH

    def t_pch(idx):
        return PCHT if idx % 2 == 0 else PCHTB

    def t_rch(idx):
        return RCHT if idx % 2 == 0 else RCHTB

    def emit_v1(idx):
        q, m = divmod(idx, NCH)

        def tl_v1(v, idx=idx, q=q, m=m):
            if idx == 0:
                wt(v, "va", va_asel)
                wt(v, "pe", pe_iota2)
            wt(v, "ac", ac_rc_l.get(idx - 2, 0))
            va_pm_l[idx] = mark(
                v.tensor_scalar(SC[:, idx % 2, 0:QT], PIOT[:, 0:QT],
                                LNW[:, m:m + 1], LNWO[:, m, q:q + 1],
                                ALU.mult, ALU.add), "va")
        on("vector", tl_v1)

    def emit_actA(idx):
        def f(a, idx=idx):
            wt(a, "va", va_pm_l[idx])
            ac_p_l[idx] = mark(
                a.activation(t_pch(idx)[:, :], SC[:, idx % 2, 0:QT], AF.Exp),
                "ac")
        on("scalar", f)

    def emit_v2(idx):
        q, m = divmod(idx, NCH)
        cl = slice(q * QT, (q + 1) * QT)

        def tl_v2(v, idx=idx, q=q, m=m, cl=cl):
            wt(v, "ac", ac_p_l[idx])
            wt(v, "po", po_nbdb[(q, m)])
            v.scalar_tensor_tensor(NB[:, m, cl], t_pch(idx)[:, :],
                                   ASEL[:, m:m + 1], NB[:, m, cl],
                                   ALU.mult, ALU.add)
            va_fix_l[idx] = mark(
                v.scalar_tensor_tensor(DB[:, m, cl], t_pch(idx)[:, :],
                                       ASEL[:, NCH + m:NCH + m + 1],
                                       DB[:, m, cl], ALU.mult, ALU.add), "va")
        on("vector", tl_v2)

    def emit_actB(idx):
        q, m = divmod(idx, NCH)
        cl = slice(q * QT, (q + 1) * QT)

        def f(a, idx=idx, m=m, cl=cl):
            wt(a, "va", va_fix_l[idx])
            a.activation(SC[:, idx % 2, 0:QT], DB[:, m, cl], AF.Ln)
            ac_rc_l[idx] = mark(
                a.activation(t_rch(idx)[:, :], SC[:, idx % 2, 0:QT], AF.Exp,
                             scale=-1.0), "ac")
        on("scalar", f)

    def emit_v3(idx):
        q, m = divmod(idx, NCH)
        cl = slice(q * QT, (q + 1) * QT)

        def tl_v3(v, idx=idx, m=m, cl=cl):
            wt(v, "ac", ac_rc_l[idx])
            wt(v, "ac", ac_sr_done)
            v.tensor_tensor(RWQ[:, m, :], NB[:, m, cl], t_rch(idx)[:, :],
                            ALU.mult)
            mark(v.tensor_tensor(RWQ[:, m, :], RWQ[:, m, :], SR[:, m, cl],
                                 ALU.mult), "va")
        on("vector", tl_v3)
        va_rw[(q, m)] = cnt["va"]

    def emit_outproj(q):
        for tb in range(4):
            for hh in range(2):
                bq = (tb * 2 + hh) % 2
                asl = (tb * 2 + hh) % 2

                def op_mm(t, q=q, tb=tb, bq=bq, hh=hh,
                          fb=ep_bank_free[bq], vrw=va_rw[(q, NCH - 1)]):
                    wt(t, "va", vrw)
                    wtt(t, gd_wot)
                    wt(t, *fb)
                    for m in range(NCH):
                        ins = t.matmul(PEP[bq][:, :],
                                 RWQ[:, m, 128 * tb:128 * (tb + 1)],
                                 WOT[:, m, hh * QT:(hh + 1) * QT],
                                 start=(m == 0), stop=(m == NCH - 1))
                        if m == NCH - 1:
                            mark(ins, "pe")
                on("tensor", op_mm)
                pe_op = cnt["pe"]

                def op_ev(v, q=q, tb=tb, bq=bq, asl=asl, pe_op=pe_op,
                          w_sp=atch_spill[asl]):
                    wt(v, "pe", pe_op)
                    wtt(v, w_sp)
                    mark(v.tensor_copy(OPCH[:, asl, :], PEP[bq][:, :]), "va")
                on("vector", op_ev)
                ac_ae = cnt["va"]
                ep_bank_free[bq] = ("va", ac_ae)

                def op_sp(s, q=q, tb=tb, hh=hh, asl=asl, ac_ae=ac_ae):
                    wt(s, "va", ac_ae)
                    r0 = q * QT + tb * 128
                    atch_spill[asl] = dma(s, f"la{asl}", lambda: s.dma_start(
                        out=aspill_d[r0:r0 + 128, hh * QT:(hh + 1) * QT],
                        in_=OPCH[:, asl, :]))
                on("sync", op_sp)

    def emit_tail_step(st):
        if st < NTOT:
            emit_v1(st)
            emit_actA(st)
        idx = st - 1
        if 0 <= idx < NTOT:
            emit_v2(idx)
            emit_actB(idx)
            emit_v3(idx)
            if idx % NCH == NCH - 1:
                emit_outproj(idx // NCH)

    UPS = 2 * NI * NQ
    tstep = [0]
    for ustep in range(UPS):
        emit_up_step(ustep)
        if ustep % 6 == 2:
            emit_tail_step(tstep[0])
            tstep[0] += 1
    while tstep[0] <= NTOT:
        emit_tail_step(tstep[0])
        tstep[0] += 1

    pe_up_done = pe_up[2 * NI - 1]
    ld_upspill_done = [tot("lu0"), tot("lu1")]

    def upmax_v(v):
        wt(v, "va", va_mx[UPS - 1])
        v.reduce_max(RMX[:, 0:1], UMAXR[:, :], AX.X)
        v.reduce_max(RMX[:, 2:3], UMINR[:, :], AX.X,
                     apply_absolute_value=True)
        v.tensor_tensor(RMX[:, 0:1], RMX[:, 0:1], RMX[:, 2:3], ALU.max)
        v.reduce_max(RMX[:, 1:2], GMAXR[:, :], AX.X)
        v.reduce_max(RMX[:, 2:3], GMINR[:, :], AX.X,
                     apply_absolute_value=True)
        mark(v.tensor_tensor(RMX[:, 1:2], RMX[:, 1:2], RMX[:, 2:3], ALU.max),
             "va")
    on("vector", upmax_v)
    va_upmax = cnt["va"]
    va_gmax = va_upmax

    va_tail_done = cnt["va"]
    ac_tail_done = cnt["ac"]
    pe_op_done = cnt["pe"]
    ld_aspill_done = [tot("la0"), tot("la1")]

    for cm in (cmOPCH, cmGMINR, cmGMAXR, cmUMINR, cmUMAXR, cmRCHTB, cmPCHTB,
               cmWOT, cmECH, cmRCHT, cmPCHT, cmRWQ, cmSR, cmDB, cmNB):
        cm.__exit__(None, None, None)

    ATCH, _ = sb("ATCH", [P, 2, QT], BF16)
    FS, cmFS = sb("FS", [P, NI, TOK], BF16)
    XCH, cmXCH = sb("XCH", [P, 2, QT], F32)
    OCH, cmOCH = sb("OCH", [P, 2, QT], F32)
    SCF, cmSCF = sb("SCF", [P, 2, TOK], F32, side="right")
    GCH, cmGCH = sb("GCH", [P, 2, TOK], BF16, side="right")
    SGCH, cmSGCH = sb("SGCH", [P, TOK], BF16, side="right")
    R2CH, cmR2CH = sb("R2CH", [P, TOK], BF16, side="right")
    UCH, cmUCH = sb("UCH", [P, 2, TOK], BF16, side="right")
    GIN, cmGIN = sb("GIN", [P, 2, TOK], BF16, side="right")

    def c2_g(g):
        wt(g, "va", max(va_gmax, va_upmax))
        dma(g, "gmi", lambda: g.dma_start(
            out=red_d[0:1, :].rearrange("a p -> p a"),
            in_=RMX[:, 0:1], allow_slow_non_contiguous=True))
        dma(g, "gmi", lambda: g.dma_start(out=RROW[:, :], in_=red_d[:, :]))
    on("gpsimd", c2_g)
    gd_c2a = tot("gmi")

    def c2_v(v):
        wtt(v, gd_c2a)
        mark(v.reduce_max(CCS[0:1, 0:1], RROW[0:1, :], AX.X,
                          apply_absolute_value=True), "va")
    on("vector", c2_v)
    va_c2a = cnt["va"]

    def c2_g2(g):
        wt(g, "va", va_c2a)
        dma(g, "gmi", lambda: g.dma_start(
            out=red_d[0:1, :].rearrange("a p -> p a"),
            in_=RMX[:, 1:2], allow_slow_non_contiguous=True))
        dma(g, "gmi", lambda: g.dma_start(out=RROW[:, :], in_=red_d[:, :]))
    on("gpsimd", c2_g2)
    gd_c2b = tot("gmi")

    def c2_v2(v):
        wtt(v, gd_c2b)
        mark(v.reduce_max(CCS[0:1, 1:2], RROW[0:1, :], AX.X,
                          apply_absolute_value=True), "va")
    on("vector", c2_v2)
    va_c2b = cnt["va"]

    def c2_g3(g):
        wt(g, "va", va_c2b)
        wtt(g, dma(g, "gmi", lambda: g.dma_start(out=mxi[1].ap(),
                                                 in_=CCS[:, :])))
        dma(g, "cc", lambda: g.collective_compute(
            "AllReduce", ALU.max, replica_groups=[list(range(NCORES))],
            ins=[mxi[1].ap().opt()], outs=[mxo[1].ap().opt()]), inc=1)
        wtt(g, tot("cc"))
        dma(g, "gmi", lambda: g.dma_start(out=SCAL[0:1, 4:6],
                                          in_=mxo[1].ap()[0:1, 0:2]))
    on("gpsimd", c2_g3)
    gd_c2 = tot("gmi")

    def c2_v3(v):
        wtt(v, gd_c2)
        v.memset(RMX[:, 1:2], 0.0)
        v.tensor_scalar(SCAL[0:1, 6:7], SCAL[0:1, 4:5], 1.0 / QMAX, EPS_Q,
                        ALU.mult, ALU.max)
        v.reciprocal(SCAL[0:1, 7:8], SCAL[0:1, 6:7])
        v.tensor_scalar(SCAL[0:1, 8:9], SCAL[0:1, 5:6], 1.0 / QMAX, EPS_Q,
                        ALU.mult, ALU.max)
        mark(v.reciprocal(SCAL[0:1, 9:10], SCAL[0:1, 8:9]), "va")
    on("vector", c2_v3)
    va_c2s = cnt["va"]

    def c2_pe(t):
        wt(t, "va", va_c2s)
        wt(t, "va", va_asel)
        mark(t.matmul(PBC[:, 0:4], ONES32[:, :], SCAL[0:1, 6:10],
                      start=True, stop=True), "pe")
    on("tensor", c2_pe)
    pe_c2 = cnt["pe"]

    def c2_v4(v):
        wt(v, "pe", pe_c2)
        v.tensor_copy(BCAST[:, 3:4], PBC[:, 0:1])
        v.tensor_copy(BCAST[:, 2:3], PBC[:, 1:2])
        v.tensor_copy(BCAST[:, 5:6], PBC[:, 2:3])
        v.tensor_copy(BCAST[:, 4:5], PBC[:, 3:4])
        v.tensor_scalar(BCAST[:, 10:11], BCAST[:, 5:6], -MAGIC, None, ALU.mult)
        mark(v.tensor_scalar(BCAST[:, 11:12], BCAST[:, 3:4], -MAGIC, None,
                             ALU.mult), "va")
    on("vector", c2_v4)
    va_c2bc = cnt["va"]

    sd_wqd = {}
    pe_dn_half = [0]

    gv2 = {}
    po_gqp = {}
    gin_ld = {}
    for i in range(NI):
        first = (i == 0)
        sl = i % 2
        gsl = i % 2

        def gin_s(s, i=i, gsl=gsl, wfree=po_gqp.get(i - 2, 0)):
            wt(s, "po", wfree)
            if i < 2:
                wtt(s, ld_upspill_done)
            wtt(s, gsp_done[i])
            gin_ld[i] = dma(s, f"lc{2 + i % 2}", lambda: s.dma_start(
                out=GIN[:, gsl, :], in_=gspill_d[i * P:(i + 1) * P, :]))
        on("sync", gin_s)

        def gq_p(p, i=i, sl=sl, gsl=gsl, first=first,
                 wprev=gv2.get(i - 2, 0)):
            if first:
                wt(p, "va", va_c2bc)
                wt(p, "va", va_tail_done)
                wt(p, "pe", pe_op_done)
                wtt(p, ld_upspill_done)
                wtt(p, ld_aspill_done)
            wtt(p, gin_ld[i])
            wt(p, "va", wprev)
            p.tensor_scalar(SCF[:, sl, :], GIN[:, gsl, :], BCAST[:, 4:5],
                            MAGIC, ALU.mult, ALU.add)
        on("gpsimd", gq_p)
        po_gqp[i] = cnt["po"]
        po_tg = cnt["po"]

        def gq_a(a, i=i, sl=sl, gch=i % 2, po_tg=po_tg,
                 wprev=gv2.get(i - 2, 0)):
            wt(a, "po", po_tg)
            wt(a, "va", wprev)
            a.activation(GCH[:, gch, :], SCF[:, sl, :], AF.Identity,
                         bias=BCAST[:, 10:11], scale=BCAST[:, 5:6])
            a.activation(SGCH[:, :], GCH[:, gch, :], AF.Sigmoid)
            a.activation(SCF[:, sl, :], SGCH[:, :], AF.Identity,
                         bias=MAGT[:, 0:1], scale=U16S)
        on("scalar", gq_a)
        ac_sg = cnt["ac"]

        def gq_v2(v, i=i, sl=sl, gch=i % 2, ac_sg=ac_sg):
            wt(v, "ac", ac_sg)
            v.tensor_scalar(R2CH[:, :], SCF[:, sl, :], MAGIC + U16S - 1.0,
                            -MAGIC, ALU.min, ALU.add)
            v.tensor_tensor(FS[:, i, :], GCH[:, gch, :], R2CH[:, :],
                            ALU.mult)
            v.reduce_max(RMX[:, 2:3], FS[:, i, :], AX.X,
                         apply_absolute_value=True)
            v.tensor_tensor(RMX[:, 3:4], RMX[:, 3:4], RMX[:, 2:3], ALU.max)
        on("vector", gq_v2)
        gv2[i] = cnt["va"]

    va_g2max = cnt["va"]

    def c3_g(g):
        wt(g, "va", va_g2max)
        dma(g, "gmi", lambda: g.dma_start(
            out=red_d[0:1, :].rearrange("a p -> p a"),
            in_=RMX[:, 3:4], allow_slow_non_contiguous=True))
        dma(g, "gmi", lambda: g.dma_start(out=RROW[:, :], in_=red_d[:, :]))
    on("gpsimd", c3_g)
    gd_c3a = tot("gmi")

    def c3_v(v):
        wtt(v, gd_c3a)
        mark(v.reduce_max(CCS[0:1, 0:1], RROW[0:1, :], AX.X,
                          apply_absolute_value=True), "va")
    on("vector", c3_v)
    va_c3a = cnt["va"]

    def c3_g2(g):
        wt(g, "va", va_c3a)
        wtt(g, dma(g, "gmi", lambda: g.dma_start(out=mxi[2].ap(),
                                                 in_=CCS[:, :])))
        dma(g, "cc", lambda: g.collective_compute(
            "AllReduce", ALU.max, replica_groups=[list(range(NCORES))],
            ins=[mxi[2].ap().opt()], outs=[mxo[2].ap().opt()]), inc=1)
        wtt(g, tot("cc"))
        dma(g, "gmi", lambda: g.dma_start(out=SCAL[0:1, 10:11],
                                          in_=mxo[2].ap()[0:1, 0:1]))
    on("gpsimd", c3_g2)
    gd_c3 = tot("gmi")

    def c3_v2(v):
        wtt(v, gd_c3)
        v.tensor_scalar(SCAL[0:1, 11:12], SCAL[0:1, 10:11],
                        1.0 / (U16S * QMAX), EPS_Q, ALU.mult, ALU.max)
        v.reciprocal(SCAL[0:1, 12:13], SCAL[0:1, 11:12])
        mark(v.tensor_scalar(SCAL[0:1, 13:14], SCAL[0:1, 12:13], 1.0 / U16S,
                             None, ALU.mult), "va")
    on("vector", c3_v2)
    va_c3s = cnt["va"]

    def c3_pe(t):
        wt(t, "va", va_c3s)
        wt(t, "va", va_c2bc)
        mark(t.matmul(PBC[:, 0:3], ONES32[:, :], SCAL[0:1, 11:14],
                      start=True, stop=True), "pe")
    on("tensor", c3_pe)
    pe_c3 = cnt["pe"]

    def c3_v3(v):
        wt(v, "pe", pe_c3)
        v.tensor_copy(BCAST[:, 7:8], PBC[:, 0:1])
        v.tensor_copy(BCAST[:, 6:7], PBC[:, 2:3])
        mark(v.tensor_scalar(BCAST[:, 12:13], BCAST[:, 7:8], -MAGIC, None,
                             ALU.mult), "va")
    on("vector", c3_v3)
    va_c3bc = cnt["va"]

    ov2 = {}
    og_ac2 = {}
    uch_ld = {}
    for i in range(NI):
        first = (i == 0)
        sl = i % 2

        def og_p(p, i=i, sl=sl, first=first, wprev=og_ac2.get(i - 2, 0)):
            if first:
                wt(p, "va", va_c3bc)
            wt(p, "ac", wprev)
            p.tensor_scalar(SCF[:, sl, :], FS[:, i, :], BCAST[:, 6:7], MAGIC,
                            ALU.mult, ALU.add)
        on("gpsimd", og_p)
        po_tg2 = cnt["po"]

        def og_a(a, i=i, sl=sl, po_tg2=po_tg2):
            wt(a, "po", po_tg2)
            a.activation(FS[:, i, :], SCF[:, sl, :], AF.Identity,
                         bias=BCAST[:, 12:13], scale=BCAST[:, 7:8])
        on("scalar", og_a)
        ac_g2w = cnt["ac"]

        usl = i % 2

        def og_s(s, i=i, usl=usl, wfree=ov2.get(i - 2, 0)):
            wt(s, "va", wfree)
            if i == 0:
                wtt(s, ld_upspill_done)
            uch_ld[i] = dma(s, f"lc{i % 2}", lambda: s.dma_start(
                out=UCH[:, usl, :], in_=upspill_d[i * P:(i + 1) * P, :]))
        on("sync", og_s)

        def og_a2(a, i=i, sl=sl, usl=usl):
            wtt(a, uch_ld[i])
            a.activation(SCF[:, sl, :], UCH[:, usl, :], AF.Identity,
                         bias=MAGT[:, 0:1], scale=BCAST[:, 2:3])
            a.activation(UCH[:, usl, :], SCF[:, sl, :], AF.Identity,
                         bias=BCAST[:, 11:12], scale=BCAST[:, 3:4])
        on("scalar", og_a2)
        og_ac2[i] = cnt["ac"]

        def og_v2(v, i=i, usl=usl, ac_u=og_ac2[i]):
            wt(v, "ac", ac_u)
            v.tensor_tensor(FS[:, i, :], FS[:, i, :], UCH[:, usl, :],
                            ALU.mult)
            v.reduce_max(RMX[:, 2:3], FS[:, i, :], AX.X,
                         apply_absolute_value=True)
            v.tensor_tensor(RMX[:, 1:2], RMX[:, 1:2], RMX[:, 2:3], ALU.max)
        on("vector", og_v2)
        ov2[i] = cnt["va"]

    va_omax = cnt["va"]
    ac_og_done = cnt["ac"]

    for cm in (cmGIN, cmUCH, cmR2CH, cmSGCH, cmGCH):
        cm.__exit__(None, None, None)
    WQDTH, cmWQDTH = sb("WQDTH", [P, NI, QT], BF16, side="right")

    def dn_xp0(s):
        wtt(s, gcc["d"])
        wt(s, "va", va_omax)
        wt(s, "ac", ac_og_done)
        sd_wqd[0] = dma(
            s, "sdd",
            lambda: s.dma_start_transpose(out=WQDTH[:, :, :],
                                          in_=wqd_d[0:QT, :]))
    on("sync", dn_xp0)

    def c4_g(g):
        wt(g, "va", va_omax)
        dma(g, "gmi", lambda: g.dma_start(
            out=red_d[0:1, :].rearrange("a p -> p a"),
            in_=RMX[:, 1:2], allow_slow_non_contiguous=True))
        dma(g, "gmi", lambda: g.dma_start(out=RROW[:, :], in_=red_d[:, :]))
    on("gpsimd", c4_g)
    gd_c4a = tot("gmi")

    def c4_v(v):
        wtt(v, gd_c4a)
        mark(v.reduce_max(CCS[0:1, 0:1], RROW[0:1, :], AX.X,
                          apply_absolute_value=True), "va")
    on("vector", c4_v)
    va_c4a = cnt["va"]

    def c4_g2(g):
        wt(g, "va", va_c4a)
        wtt(g, dma(g, "gmi", lambda: g.dma_start(out=mxi[0].ap(),
                                                 in_=CCS[:, :])))
        dma(g, "cc", lambda: g.collective_compute(
            "AllReduce", ALU.max, replica_groups=[list(range(NCORES))],
            ins=[mxi[0].ap().opt()], outs=[mxo[0].ap().opt()]), inc=1)
        wtt(g, tot("cc"))
        dma(g, "gmi", lambda: g.dma_start(out=SCAL[0:1, 14:15],
                                          in_=mxo[0].ap()[0:1, 0:1]))
    on("gpsimd", c4_g2)
    gd_c4 = tot("gmi")

    def c4_v2(v):
        wtt(v, gd_c4)
        v.tensor_scalar(SCAL[0:1, 15:16], SCAL[0:1, 14:15], 1.0 / QMAX, EPS_Q,
                        ALU.mult, ALU.max)
        mark(v.reciprocal(SCAL[0:1, 3:4], SCAL[0:1, 15:16]), "va")
    on("vector", c4_v2)
    va_c4s = cnt["va"]

    def c4_pe(t):
        wt(t, "va", va_c4s)
        wt(t, "va", va_c3bc)
        t.matmul(PBC[:, 0:1], ONES32[:, :], SCAL[0:1, 15:16],
                 start=True, stop=True)
        mark(t.matmul(PBC[:, 1:2], ONES32[:, :], SCAL[0:1, 3:4],
                      start=True, stop=True), "pe")
    on("tensor", c4_pe)
    pe_c4 = cnt["pe"]

    def c4_v3(v):
        wt(v, "pe", pe_c4)
        v.tensor_copy(BCAST[:, 9:10], PBC[:, 0:1])
        v.tensor_copy(BCAST[:, 8:9], PBC[:, 1:2])
        mark(v.tensor_scalar(BCAST[:, 13:14], BCAST[:, 9:10], -MAGIC, None,
                             ALU.mult), "va")
    on("vector", c4_v3)
    va_c4bc = cnt["va"]

    oqa = {}
    for i in range(NI):
        first = (i == 0)
        sl = i % 2

        def oq_p(p, i=i, sl=sl, first=first, wprev=oqa.get(i - 2, 0)):
            if first:
                wt(p, "va", va_c4bc)
            wt(p, "ac", wprev)
            p.tensor_scalar(SCF[:, sl, :], FS[:, i, :], BCAST[:, 8:9], MAGIC,
                            ALU.mult, ALU.add)
        on("gpsimd", oq_p)
        po_to = cnt["po"]

        def oq_a(a, i=i, sl=sl, po_to=po_to):
            wt(a, "po", po_to)
            a.activation(FS[:, i, :], SCF[:, sl, :], AF.Identity,
                         bias=BCAST[:, 13:14], scale=BCAST[:, 9:10])
        on("scalar", oq_a)
        oqa[i] = cnt["ac"]

    ac_o_done = cnt["ac"]


    ld_final = [0]

    for hh in range(2):
        if hh == 1:
            def dn_xp(s, hh=hh):
                wt(s, "pe", pe_dn_half[0])
                sd_wqd[hh] = dma(
                    s, "sdd",
                    lambda: s.dma_start_transpose(
                        out=WQDTH[:, :, :],
                        in_=wqd_d[hh * QT:(hh + 1) * QT, :]))
            on("sync", dn_xp)

        for k in range(NT128):
            bq = k % 2
            csl = k % 2

            def ep_ld(s, hh=hh, k=k, csl=csl, step=hh * NT128 + k):
                wt(s, "va", xch_free.get(step - 2, 0))
                dma(s, f"le{csl}", lambda: s.dma_start(
                    out=XCH[:, csl, :], in_=xt[k, :, hh * QT:(hh + 1) * QT]))
                if step == 0:
                    wtt(s, ld_aspill_done)
                dma(s, f"lr{csl}", lambda: s.dma_start(
                    out=ATCH[:, csl, :],
                    in_=aspill_d[k * P:(k + 1) * P, hh * QT:(hh + 1) * QT]))
            on("sync", ep_ld)
            ld_ep = [tot(f"le{csl}"), tot(f"lr{csl}")]

            def dn_mm(t, hh=hh, k=k, bq=bq, fb=ep_bank_free[bq]):
                wtt(t, sd_wqd[hh])
                wt(t, *fb)
                for i in range(NI):
                    if hh == 0:
                        wt(t, "ac", oqa[i])
                    elif i == 0:
                        wt(t, "ac", ac_o_done)
                    ins = t.matmul(PEP[bq][:, :], FS[:, i, 128 * k:128 * (k + 1)],
                             WQDTH[:, i, :], start=(i == 0), stop=(i == NI - 1))
                    if i == NI - 1:
                        mark(ins, "pe")
            on("tensor", dn_mm)
            pe_dn = cnt["pe"]
            if k == NT128 - 1:
                pe_dn_half[0] = pe_dn

            def ep_v(v, hh=hh, k=k, bq=bq, csl=csl, pe_dn=pe_dn, ld_ep=ld_ep,
                     wfree=och_free.get(0, 0)):
                wt(v, "pe", pe_dn)
                wtt(v, ld_ep)
                wtt(v, och_free.get(hh * NT128 + k - 2, 0))
                v.tensor_tensor(SC[:, 0, 0:QT], PEP[bq][:, :], ATCH[:, csl, :],
                                ALU.add)
                mark(v.tensor_tensor(OCH[:, csl, :], SC[:, 0, 0:QT],
                                     XCH[:, csl, :], ALU.add), "va")
            on("vector", ep_v)
            va_ep = cnt["va"]
            xch_free[hh * NT128 + k] = va_ep
            ep_bank_free[bq] = ("va", va_ep)

            def ep_st(s, hh=hh, k=k, csl=csl, va_ep=va_ep):
                wt(s, "va", va_ep)
                m = dma(s, f"lo{csl}", lambda: s.dma_start(
                    out=out_d[k * P:(k + 1) * P, hh * QT:(hh + 1) * QT],
                    in_=OCH[:, csl, :]))
                och_free[hh * NT128 + k] = m
                ld_final[0] = m
            on("sync", ep_st)

    def fin_g(g):
        wtt(g, tot("lo0"))
        wtt(g, tot("lo1"))
        g.engine_nop()
    on("gpsimd", fin_g)

    flush_streams()
    return nc


def kernel(**inputs):
    x = np.asarray(inputs["L0_attn_x"])
    td = np.asarray(inputs["L0_attn_time_decay"])
    tf = np.asarray(inputs["L0_attn_time_first"])
    tmk = np.asarray(inputs["L0_attn_time_mix_k"])
    tmv = np.asarray(inputs["L0_attn_time_mix_v"])
    tmr = np.asarray(inputs["L0_attn_time_mix_r"])
    Wk = np.asarray(inputs["L0_attn_Wk"])
    Wv = np.asarray(inputs["L0_attn_Wv"])
    Wr = np.asarray(inputs["L0_attn_Wr"])
    Wo = np.asarray(inputs["L0_attn_Wo"])
    wg = np.asarray(inputs["L0_ffn_w_gate"])
    wu = np.asarray(inputs["L0_ffn_w_up"])
    wd = np.asarray(inputs["L0_ffn_w_down"])

    def v128(v):
        return np.ascontiguousarray(v.reshape(NCH, P).T)

    common = dict(
        td=v128(td), tf=v128(tf), tmk=v128(tmk), tmv=v128(tmv), tmr=v128(tmr),
        wkT=np.ascontiguousarray(Wk.T), wvT=np.ascontiguousarray(Wv.T),
        wrT=np.ascontiguousarray(Wr.T), woT=np.ascontiguousarray(Wo.T))

    in_maps = []
    for core in range(NCORES):
        b, half = divmod(core, 2)
        s0 = half * TOK
        xm = np.ascontiguousarray(x[b, s0:s0 + TOK])
        if s0 == 0:
            xh = np.zeros((1, C), np.float32)
        else:
            xh = np.ascontiguousarray(x[b, s0 - 1:s0])
        selv = np.zeros((1, NCORES), np.float32)
        if half == 1:
            selv[0, core - 1] = 1.0
        m = dict(common)
        m.update(x=xm, xhalo=xh, sel=selv,
                 wupp=np.ascontiguousarray(wu[core * 3 * P:(core + 1) * 3 * P]),
                 wgatep=np.ascontiguousarray(
                     wg[core * 3 * P:(core + 1) * 3 * P]),
                 wdownp=np.ascontiguousarray(wd[core * P:(core + 1) * P]))
        in_maps.append(m)

    nc = build_nc()
    res = run_bass_kernel_spmd(nc, in_maps, core_ids=list(range(NCORES)))
    out = np.empty((4, 4096, C), np.float32)
    for core in range(NCORES):
        b, half = divmod(core, 2)
        s0 = half * TOK
        out[b, s0:s0 + TOK] = np.asarray(res.results[core]["out"])
    return out
